# version 02
# baseline (speedup 1.0000x reference)
import numpy as np
import ml_dtypes
from contextlib import ExitStack

import concourse.bass as bass
import concourse.mybir as mybir
from concourse.bass_utils import run_bass_kernel_spmd

F32 = mybir.dt.float32
BF16 = mybir.dt.bfloat16
I32 = mybir.dt.int32
U8 = mybir.dt.uint8
AF = mybir.ActivationFunctionType
ALU = mybir.AluOpType
AX = mybir.AxisListType

MAGIC = 12582912.0
NCORES = 8
D = 1024
I_FFN = 2816
TOK = 2048
NTT = TOK // 128
SEQ = 4096
E3 = 3072

ENGS = ("sync", "gpsimd", "vector", "scalar", "tensor")


class K:

    def __init__(self, nc, ctx):
        self.nc = nc
        self.ctx = ctx
        self.q = {e: [] for e in ENGS}
        self.csem = {e: ctx.enter_context(nc.semaphore(f"c_{e}")) for e in ENGS}
        self.dsem = {}
        self.ccnt = {e: 0 for e in ENGS}
        self.dcnt = {}

    def _dma_sem(self, eng, key):
        kk = (eng, key)
        if kk not in self.dsem:
            self.dsem[kk] = self.ctx.enter_context(
                self.nc.semaphore(f"d_{eng}_{key}"))
            self.dcnt[kk] = 0
        return kk

    def emit(self, eng, fn):
        sem = self.csem[eng]
        prev = self.ccnt[eng]
        self.ccnt[eng] += 1

        def wrapped(E, fn=fn, sem=sem, prev=prev):
            if prev > 0:
                E.wait_ge(sem, prev)
            fn(E).then_inc(sem, 1)
        self.q[eng].append(wrapped)
        return (sem, self.ccnt[eng])

    def dma(self, eng, fn, key="dflt"):
        kk = self._dma_sem(eng, key)
        self.dcnt[kk] += 16
        sem, val = self.dsem[kk], self.dcnt[kk]

        def wrapped(E, fn=fn, sem=sem):
            fn(E).then_inc(sem, 16)
        self.q[eng].append(wrapped)
        return (sem, val)

    def token(self, eng):
        return (self.csem[eng], self.ccnt[eng])

    def wait(self, eng, tok):
        if tok is None:
            return
        sem, val = tok
        if val <= 0:
            return
        self.q[eng].append(lambda E, s=sem, v=val: E.wait_ge(s, v))

    def barrier(self):
        toks = []
        for e in ENGS:
            toks.append(self.emit(e, lambda E: E.nop()))
        for kk, sem in self.dsem.items():
            if self.dcnt[kk] > 0:
                toks.append((sem, self.dcnt[kk]))
        for e in ENGS:
            for tok in toks:
                if tok[0] is not self.csem[e]:
                    self.wait(e, tok)

    def finalize(self):
        with self.nc.Block() as block:
            @block.sync
            def _(E):
                for f in self.q["sync"]:
                    f(E)

            @block.gpsimd
            def _(E):
                for f in self.q["gpsimd"]:
                    f(E)

            @block.vector
            def _(E):
                for f in self.q["vector"]:
                    f(E)

            @block.scalar
            def _(E):
                for f in self.q["scalar"]:
                    f(E)

            @block.tensor
            def _(E):
                for f in self.q["tensor"]:
                    f(E)


def emit_fp4_qd(k, x_ap, out_ap, scr, D_cols):
    nb = D_cols // 16
    v = "vector"
    bm, s, r = scr["bm"][:, 0:nb], scr["s"][:, 0:nb], scr["r"][:, 0:nb]
    t = scr["t"][:, 0:D_cols]
    u = scr["u"][:, 0:D_cols]
    w1 = scr["w1"][:, 0:D_cols]
    w2 = scr["w2"][:, 0:D_cols]
    dd = scr["dd"][:, 0:D_cols]
    ee = scr["ee"][:, 0:D_cols]
    x3 = x_ap.rearrange("p (b k) -> p b k", k=16)
    k.emit(v, lambda E: E.tensor_reduce(bm, x3, AX.X, ALU.max,
                                        apply_absolute_value=True))
    k.emit(v, lambda E: E.tensor_scalar(s, bm, 1.0 / 6.0, 1e-12,
                                        ALU.mult, ALU.max))
    k.emit(v, lambda E: E.reciprocal(r, s))
    rb = r.unsqueeze(2).broadcast_to([128, nb, 16])
    sb = s.unsqueeze(2).broadcast_to([128, nb, 16])
    t3 = t.rearrange("p (b k) -> p b k", k=16)
    k.emit(v, lambda E: E.tensor_tensor(t3, x3, rb, ALU.mult))
    k.emit(v, lambda E: E.tensor_mul(u, t, t))
    k.emit(v, lambda E: E.tensor_single_scalar(w1, u, 4.0, ALU.is_gt))
    k.emit(v, lambda E: E.tensor_single_scalar(w2, u, 16.0, ALU.is_gt))
    k.emit(v, lambda E: E.scalar_tensor_tensor(dd, w2, 0.5, w1, ALU.mult, ALU.add))
    k.emit(v, lambda E: E.scalar_tensor_tensor(ee, w1, 0.5, w2, ALU.mult, ALU.add))
    k.emit(v, lambda E: E.tensor_scalar(dd, dd, -1.0, 2.0, ALU.mult, ALU.add))
    k.emit(v, lambda E: E.tensor_scalar(ee, ee, 0.5, None, ALU.add))
    k.emit(v, lambda E: E.tensor_tensor(u, t, dd, ALU.mult))
    k.emit(v, lambda E: E.tensor_scalar(u, u, MAGIC, -MAGIC, ALU.add, ALU.add))
    k.emit(v, lambda E: E.tensor_tensor(u, u, ee, ALU.mult))
    u3 = u.rearrange("p (b k) -> p b k", k=16)
    o3 = out_ap.rearrange("p (b k) -> p b k", k=16)
    return k.emit(v, lambda E: E.tensor_tensor(o3, u3, sb, ALU.mult))


def emit_ln(k, x_ap, xn_ap, scr, eps=1e-5):
    DD = x_ap.shape[1]
    NP = x_ap.shape[0]
    st = scr["st"][0:NP, :]
    nch = (DD + 511) // 512
    for c in range(nch):
        cw = min(512, DD - c * 512)
        k.emit("vector", lambda E, c=c, cw=cw: E.bn_stats(
            st[:, 6 * c:6 * c + 6], x_ap[:, c * 512:c * 512 + cw]))
    k.emit("vector", lambda E: E.bn_aggr(st[:, 12:14], st[:, 0:6 * nch]))
    k.emit("vector", lambda E: E.tensor_scalar(st[:, 14:15], st[:, 13:14], eps, None, ALU.add))
    vr = k.emit("vector", lambda E: E.reciprocal(st[:, 15:16], st[:, 14:15]))
    k.wait("scalar", vr)
    srs = k.emit("scalar", lambda E: E.activation(st[:, 16:17], st[:, 15:16], AF.Sqrt))
    k.wait("vector", srs)
    vfin = k.emit("vector", lambda E: E.tensor_scalar(xn_ap, x_ap, st[:, 12:13], st[:, 16:17],
                                                      ALU.subtract, ALU.mult))
    return vfin



def build_prep(parts=("gu", "d", "b")):
    nc = bass.Bass("TRN2", target_bir_lowering=False)
    wg = nc.dram_tensor("wg", [I_FFN, 128], F32, kind="ExternalInput").ap()
    wu = nc.dram_tensor("wu", [I_FFN, 128], F32, kind="ExternalInput").ap()
    wd = nc.dram_tensor("wd", [128, I_FFN], F32, kind="ExternalInput").ap()
    cenT = nc.dram_tensor("cenT", [2, 1024], F32, kind="ExternalInput").ap()
    identb = nc.dram_tensor("identb", [128, 128], F32, kind="ExternalInput").ap()

    o_wgqT = nc.dram_tensor("o_wgqT", [128, I_FFN], BF16, kind="ExternalOutput").ap()
    o_wuqT = nc.dram_tensor("o_wuqT", [128, I_FFN], BF16, kind="ExternalOutput").ap()
    o_wdqT = nc.dram_tensor("o_wdqT", [22, 128, 128], BF16, kind="ExternalOutput").ap()
    o_B = nc.dram_tensor("o_B", [8, 128, 1024], F32, kind="ExternalOutput").ap()

    NTI = 22

    with ExitStack() as ctx:
        e = ctx.enter_context
        k = K(nc, ctx)
        sw = e(nc.sbuf_tensor("sw", [128, I_FFN], F32))
        swq = e(nc.sbuf_tensor("swq", [128, I_FFN], BF16))
        sib = e(nc.sbuf_tensor("sib", [128, 128], BF16))
        souT = e(nc.sbuf_tensor("souT", [128, I_FFN], BF16))
        scr = {nm: e(nc.sbuf_tensor(f"scr_{nm}", [128, I_FFN], F32))
               for nm in ("t", "u", "w1", "w2", "dd", "ee")}
        for nm in ("bm", "s", "r"):
            scr[nm] = e(nc.sbuf_tensor(f"scr_{nm}", [128, I_FFN // 16], F32))
        scr["sq"] = scr["t"]
        scen3 = e(nc.sbuf_tensor("scen3", [3, 1024], F32))
        scen23 = e(nc.sbuf_tensor("scen23", [3, 1024], F32))
        ssq2 = e(nc.sbuf_tensor("ssq2", [2, 1024], F32))
        sn2r = e(nc.sbuf_tensor("sn2r", [1, 1024], F32))
        sones2 = e(nc.sbuf_tensor("sones2", [2, 1], F32))
        sones1 = e(nc.sbuf_tensor("sones1", [1, 128], F32))
        sn2c = e(nc.sbuf_tensor("sn2c", [128, 8], F32))
        sdist = e(nc.sbuf_tensor("sdist", [128, 8 * 1024], F32))
        smax = e(nc.sbuf_tensor("smax", [128, 10], F32))
        sfac = e(nc.sbuf_tensor("sfac", [1, 4], F32))
        sfac128 = e(nc.sbuf_tensor("sfac128", [128, 1], F32))
        srow = e(nc.sbuf_tensor("srow", [1, 128], F32))
        sone1 = e(nc.sbuf_tensor("sone1", [1, 1], F32))
        ps_a = e(nc.psum_tensor("ps_a", [128, 512], F32))
        ps_b = e(nc.psum_tensor("ps_b", [128, 512], F32))
        ps_t0 = e(nc.psum_tensor("ps_t0", [128, 1024], BF16))
        ps_t1 = e(nc.psum_tensor("ps_t1", [128, 1024], BF16))
        dscr = nc.dram_tensor("dscr", [128], F32, kind="Internal").ap()

        c_id = k.dma("gpsimd", lambda E: E.dma_start(out=sib[:], in_=identb), key="id")

        if "gu" not in parts:
            wparts = []
        else:
            wparts = None
        wg_t = wg.rearrange("(n p) m -> n p m", p=128)
        wu_t = wu.rearrange("(n p) m -> n p m", p=128)
        for part, (src, outT) in enumerate([(wg_t, o_wgqT), (wu_t, o_wuqT)] if wparts is None else []):
            ld = [None] * NTI
            vq = [None] * NTI
            for i in range(NTI):
                if i >= 4:
                    k.wait("sync", vq[i - 4])
                ld[i] = k.dma("sync", lambda E, i=i, src=src: E.dma_start(
                    out=sw[:, i * 128:(i + 1) * 128], in_=src[i]), key=f"w{i % 4}")
                k.wait("vector", ld[i])
                sl = slice(i * 128, (i + 1) * 128)
                sub = {nm: scr[nm][:, 0:128] for nm in ("t", "u", "w1", "w2", "dd", "ee")}
                sub.update({nm: scr[nm][:, 0:8] for nm in ("bm", "s", "r")})
                vq[i] = emit_fp4_qd(k, sw[:, sl], swq[:, sl], sub, 128)
            for i in range(NTI):
                k.wait("tensor", vq[i])
                k.wait("tensor", c_id)
                pst = ps_t0 if i % 2 == 0 else ps_t1
                if i >= 2:
                    k.wait("tensor", k.token("vector"))
                tprev = k.emit("tensor", lambda E, i=i, pst=pst: E.transpose(
                    pst[:, 0:128], swq[:, i * 128:(i + 1) * 128], sib[:]))
                k.wait("vector", tprev)
                k.emit("vector", lambda E, i=i, pst=pst: E.tensor_copy(
                    souT[:, i * 128:(i + 1) * 128], pst[:, 0:128]))
            k.barrier()
            st = k.dma("sync", lambda E, outT=outT: E.dma_start(out=outT, in_=souT[:]), key="store")
            k.wait("sync", st)
            k.barrier()

        if "d" not in parts:
            k.barrier(); k.finalize(); return nc
        ldd = k.dma("sync", lambda E: E.dma_start(out=sw[:], in_=wd), key="wd")
        k.wait("vector", ldd)
        vq = emit_fp4_qd(k, sw[:], swq[:], scr, I_FFN)
        k.wait("tensor", vq)
        for i in range(NTI):
            pst = ps_t0 if i % 2 == 0 else ps_t1
            if i >= 2:
                k.wait("tensor", k.token("vector"))
            t = k.emit("tensor", lambda E, i=i, pst=pst: E.transpose(
                pst[:, 0:128], swq[:, i * 128:(i + 1) * 128], sib[:]))
            k.wait("vector", t)
            k.emit("vector", lambda E, i=i, pst=pst: E.tensor_copy(
                souT[:, i * 128:(i + 1) * 128], pst[:, 0:128]))
        k.barrier()
        souT_v = souT[:].rearrange("p (n m) -> p n m", m=128)
        st = k.dma("sync", lambda E: E.dma_start(out=o_wdqT.rearrange("n p m -> p n m"), in_=souT_v), key="store")
        k.wait("sync", st)
        k.barrier()

        if "b" not in parts:
            k.finalize(); return nc
        lc = k.dma("sync", lambda E: E.dma_start(out=scen3[0:2, :], in_=cenT), key="cen")
        k.wait("vector", lc)
        k.emit("vector", lambda E: E.memset(sones2[:], 1.0))
        k.emit("vector", lambda E: E.memset(sn2r[:], 1.0))
        vms = k.emit("vector", lambda E: E.memset(sones1[:], 1.0))
        k.wait("gpsimd", vms)
        gons = k.dma("gpsimd", lambda E: E.dma_start(out=scen23[2:3, :], in_=sn2r[:]), key="ones")
        k.emit("vector", lambda E: E.tensor_scalar(scen23[0:2, :], scen3[0:2, :], -2.0, None, ALU.mult))
        vsq = k.emit("vector", lambda E: E.tensor_mul(ssq2[:], scen3[0:2, :], scen3[0:2, :]))
        k.wait("tensor", vsq)
        k.emit("tensor", lambda E: E.matmul(ps_a[0:1, 0:512], sones2[:], ssq2[:, 0:512], start=True, stop=True))
        tn2 = k.emit("tensor", lambda E: E.matmul(ps_b[0:1, 0:512], sones2[:], ssq2[:, 512:1024], start=True, stop=True))
        k.wait("vector", tn2)
        k.wait("vector", gons)
        k.emit("vector", lambda E: E.tensor_copy(sn2r[0:1, 0:512], ps_a[0:1, 0:512]))
        vn2 = k.emit("vector", lambda E: E.tensor_copy(sn2r[0:1, 512:1024], ps_b[0:1, 0:512]))
        k.wait("gpsimd", vn2)
        gn2 = k.dma("gpsimd", lambda E: E.dma_start(out=scen3[2:3, :], in_=sn2r[:]), key="n2")
        k.wait("tensor", gn2)
        k.wait("tensor", vn2)
        for ti in range(8):
            k.emit("tensor", lambda E, ti=ti: E.matmul(
                ps_b[:, ti:ti + 1], sn2r[0:1, ti * 128:(ti + 1) * 128], sones1[0:1, 0:1],
                start=True, stop=True))
        k.wait("vector", k.token("tensor"))
        vn2c = k.emit("vector", lambda E: E.tensor_copy(sn2c[:], ps_b[:, 0:8]))
        vtile = vn2c
        for ti in range(8):
            sl = slice(ti * 128, (ti + 1) * 128)
            k.wait("tensor", vtile)
            k.emit("tensor", lambda E, sl=sl: E.matmul(ps_a[:, 0:512], scen23[:, sl], scen3[:, 0:512], start=True, stop=True))
            t2 = k.emit("tensor", lambda E, sl=sl: E.matmul(ps_b[:, 0:512], scen23[:, sl], scen3[:, 512:1024], start=True, stop=True))
            k.wait("vector", t2)
            d0 = ti * 1024
            k.emit("vector", lambda E, d0=d0, ti=ti: E.tensor_scalar(
                sdist[:, d0:d0 + 512], ps_a[:, 0:512], sn2c[:, ti:ti + 1], 0.0, ALU.add, ALU.max))
            vtile = k.emit("vector", lambda E, d0=d0, ti=ti: E.tensor_scalar(
                sdist[:, d0 + 512:d0 + 1024], ps_b[:, 0:512], sn2c[:, ti:ti + 1], 0.0, ALU.add, ALU.max))
            k.wait("scalar", vtile)
            k.emit("scalar", lambda E, d0=d0: E.activation(
                sdist[:, d0:d0 + 1024], sdist[:, d0:d0 + 1024], AF.Sqrt))
        k.barrier()
        k.emit("vector", lambda E: E.reduce_max(smax[:, 0:1], sdist[:], AX.X))
        k.barrier()
        st = k.dma("sync", lambda E: E.dma_start(out=dscr, in_=smax[:, 0:1]), key="dsw")
        k.wait("sync", st)
        lr = k.dma("sync", lambda E: E.dma_start(out=srow[:], in_=dscr.rearrange("(a b) -> a b", a=1)), key="dsr")
        k.wait("vector", lr)
        k.emit("vector", lambda E: E.reduce_max(sfac[:, 0:1], srow[:], AX.X))
        k.emit("vector", lambda E: E.reciprocal(sfac[:, 1:2], sfac[:, 0:1]))
        k.emit("vector", lambda E: E.tensor_scalar(sfac[:, 2:3], sfac[:, 0:1], 1e-6, None, ALU.is_gt))
        vf = k.emit("vector", lambda E: E.tensor_scalar(
            sfac[:, 1:2], sfac[:, 1:2], sfac[0:1, 2:3], -0.1, ALU.mult, ALU.mult))
        k.wait("tensor", vf)
        tf = k.emit("tensor", lambda E: E.matmul(ps_a[:, 0:1], sones1[0:1, :], sfac[0:1, 1:2], start=True, stop=True))
        k.wait("vector", tf)
        k.emit("vector", lambda E: E.tensor_copy(sfac128[:], ps_a[:, 0:1]))
        for ti in range(8):
            d0 = ti * 1024
            k.emit("vector", lambda E, d0=d0: E.tensor_scalar(
                sdist[:, d0:d0 + 1024], sdist[:, d0:d0 + 1024], sfac128[:, 0:1], None, ALU.mult))
        k.barrier()
        sdist_v = sdist[:].rearrange("p (n m) -> p n m", m=1024)
        st = k.dma("sync", lambda E: E.dma_start(out=o_B.rearrange("n p m -> p n m"), in_=sdist_v), key="store")
        k.wait("sync", st)
        k.barrier()
        k.finalize()
    return nc



FFN_CHUNKS = [(0, 512), (512, 512), (1024, 512), (1536, 512), (2048, 512), (2560, 256)]


def build_l0ffn(stage="full"):
    nc = bass.Bass("TRN2", target_bir_lowering=False)
    x_in = nc.dram_tensor("x", [TOK, D], F32, kind="ExternalInput").ap()
    wgqT = nc.dram_tensor("wgqT", [D, I_FFN], BF16, kind="ExternalInput").ap()
    wuqT = nc.dram_tensor("wuqT", [D, I_FFN], BF16, kind="ExternalInput").ap()
    wdqT = nc.dram_tensor("wdqT", [I_FFN, D], BF16, kind="ExternalInput").ap()
    identb = nc.dram_tensor("identb", [128, 128], F32, kind="ExternalInput").ap()
    y_out = nc.dram_tensor("y", [TOK, D], F32, kind="ExternalOutput").ap()

    x_t = x_in.rearrange("(n p) m -> n p m", p=128)
    y_t = y_out.rearrange("(n p) m -> n p m", p=128)

    with ExitStack() as ctx:
        e = ctx.enter_context
        k = K(nc, ctx)
        swg = e(nc.sbuf_tensor("swg", [128, 8 * I_FFN], BF16))
        swu = e(nc.sbuf_tensor("swu", [128, 8 * I_FFN], BF16))
        swd = e(nc.sbuf_tensor("swd", [128, 22 * D], BF16))
        sib = e(nc.sbuf_tensor("sib", [128, 128], BF16))
        sxqT = e(nc.sbuf_tensor("sxqT", [128, 8 * TOK], BF16))
        sx = [e(nc.sbuf_tensor(f"sx{i}", [128, D], F32)) for i in range(2)]
        sx2 = sx
        sxn = e(nc.sbuf_tensor("sxn", [128, D], F32))
        sq_bf = e(nc.sbuf_tensor("sq_bf", [128, D], BF16))
        scr = {nm: e(nc.sbuf_tensor(f"scr_{nm}", [128, D], F32))
               for nm in ("u", "w1", "w2", "dd", "ee")}
        scr["t"] = e(nc.sbuf_tensor("scr_t", [128, 1536], F32))
        scr["sq"] = scr["dd"]
        for nm in ("bm", "s", "r"):
            scr[nm] = e(nc.sbuf_tensor(f"scr_{nm}", [128, 64], F32))
        sst = e(nc.sbuf_tensor("sst", [128, 18], F32))
        scr["st"] = sst
        ssg = scr["w1"]
        sh = scr["ee"]
        shq = e(nc.sbuf_tensor("shq", [128, 512], BF16))
        shqT = e(nc.sbuf_tensor("shqT", [128, 512], BF16))
        sy = [sxn, scr["t"][:, 512:1536]]
        ps_g = [e(nc.psum_tensor(f"ps_g{i}", [128, 512], F32)) for i in range(2)]
        ps_u = [e(nc.psum_tensor(f"ps_u{i}", [128, 512], F32)) for i in range(2)]
        ps_o = [e(nc.psum_tensor(f"ps_o{i}", [128, 512], F32)) for i in range(2)]
        ps_t = [e(nc.psum_tensor(f"ps_t{i}", [128, 1024], BF16)) for i in range(2)]

        lwg = k.dma("sync", lambda E: E.dma_start(out=swg[:].rearrange("p (n m) -> p n m", m=I_FFN),
                                                  in_=wgqT.rearrange("(n p) m -> p n m", p=128)), key="wg")
        lwu = k.dma("sync", lambda E: E.dma_start(out=swu[:].rearrange("p (n m) -> p n m", m=I_FFN),
                                                  in_=wuqT.rearrange("(n p) m -> p n m", p=128)), key="wu")
        lwd = k.dma("sync", lambda E: E.dma_start(out=swd[:].rearrange("p (n m) -> p n m", m=D),
                                                  in_=wdqT.rearrange("(n p) m -> p n m", p=128)), key="wd")
        lid = k.dma("gpsimd", lambda E: E.dma_start(out=sib[:], in_=identb), key="id")

        vq = [None] * NTT
        vcp = [None] * NTT
        ld = [None] * NTT
        ttr = [None] * NTT
        for t in range(NTT):
            if t >= 2:
                k.wait("sync", vq[t - 2])
            ld[t] = k.dma("sync", lambda E, t=t: E.dma_start(out=sx[t % 2][:], in_=x_t[t]),
                          key=f"x{t % 2}")
            k.wait("vector", ld[t])
            k.wait("scalar", ld[t])
            if t >= 1:
                k.wait("scalar", vq[t - 1])
            emit_ln(k, sx[t % 2][:], sxn[:], scr)
            vq[t] = emit_fp4_qd(k, sxn[:], sq_bf[:], scr, D)
            k.wait("tensor", vq[t])
            if t == 0:
                k.wait("tensor", lid)
            if t >= 2:
                k.wait("tensor", vcp[t - 2])
            for dt in range(8):
                ttr[t] = k.emit("tensor", lambda E, t=t, dt=dt: E.transpose(
                    ps_t[t % 2][:, dt * 128:(dt + 1) * 128],
                    sq_bf[:, dt * 128:(dt + 1) * 128], sib[:]))
            k.wait("vector", ttr[t])
            dst = sxqT[:].rearrange("p (n m) -> p n m", m=TOK)[:, :, t * 128:(t + 1) * 128]
            src = ps_t[t % 2][:].rearrange("p (n m) -> p n m", m=128)
            vcp[t] = k.emit("vector", lambda E, dst=dst, src=src: E.tensor_copy(dst, src))

        if stage == "A":
            k.barrier()
            sA = k.dma("sync", lambda E: E.dma_start(out=y_t[0], in_=sx[0][:]), key="ya")
            k.wait("sync", sA)
            k.barrier()
            k.finalize()
            return nc
        k.wait("tensor", lwg)
        k.wait("tensor", lwu)
        k.wait("tensor", lwd)

        tg = [None, None]
        tu = [None, None]
        tdn = None
        vhq_prev = None
        vh_last = [None, None]
        ssil_last = [None, None]
        vhT_last = [None, None]
        vy = [None, None]
        st_y = [None, None]
        for t in range(NTT):
            if t < 2:
                k.wait("sync", vq[NTT - 2 + t])
            if t >= 2:
                k.wait("sync", vy[t % 2])
            ld2 = k.dma("sync", lambda E, t=t: E.dma_start(out=sx2[t % 2][:], in_=x_t[t]),
                        key=f"x2{t % 2}")
            for ci, (c0, cw) in enumerate(FFN_CHUNKS):
                pg, pu, pt = ps_g[ci % 2], ps_u[ci % 2], ps_t[ci % 2]
                if t == 0 and ci < 2:
                    k.wait("tensor", vcp[NTT - 1])
                k.wait("tensor", ssil_last[ci % 2])
                for dt in range(8):
                    lhs = sxqT[:, dt * TOK + t * 128: dt * TOK + (t + 1) * 128]
                    tg[ci % 2] = k.emit("tensor", lambda E, dt=dt, c0=c0, cw=cw, lhs=lhs, pg=pg: E.matmul(
                        pg[:, 0:cw], lhs, swg[:, dt * I_FFN + c0: dt * I_FFN + c0 + cw],
                        start=(dt == 0), stop=(dt == 7)))
                k.wait("tensor", vh_last[ci % 2])
                for dt in range(8):
                    lhs = sxqT[:, dt * TOK + t * 128: dt * TOK + (t + 1) * 128]
                    tu[ci % 2] = k.emit("tensor", lambda E, dt=dt, c0=c0, cw=cw, lhs=lhs, pu=pu: E.matmul(
                        pu[:, 0:cw], lhs, swu[:, dt * I_FFN + c0: dt * I_FFN + c0 + cw],
                        start=(dt == 0), stop=(dt == 7)))
                k.wait("scalar", tg[ci % 2])
                k.wait("scalar", vhq_prev)
                ssil = k.emit("scalar", lambda E, cw=cw, pg=pg: E.activation(
                    ssg[:, 0:cw], pg[:, 0:cw], AF.Silu))
                ssil_last[ci % 2] = ssil
                k.wait("vector", ssil)
                k.wait("vector", tu[ci % 2])
                vh = k.emit("vector", lambda E, cw=cw, pu=pu: E.tensor_tensor(
                    sh[:, 0:cw], ssg[:, 0:cw], pu[:, 0:cw], ALU.mult))
                vh_last[ci % 2] = vh
                sub = {nm: scr[nm][:, 0:cw] for nm in ("t", "u", "w1", "w2", "dd", "ee")}
                sub.update({nm: scr[nm][:, 0:cw // 16] for nm in ("bm", "s", "r")})
                vhq = emit_fp4_qd(k, sh[:, 0:cw], shq[:, 0:cw], sub, cw)
                vhq_prev = vhq
                if stage == "B1":
                    continue
                nt = cw // 128
                if stage == "B2x":
                    nt = cw // 128
                k.wait("tensor", vhq)
                k.wait("tensor", vhT_last[ci % 2])
                tt = None
                for it in range(nt):
                    tt = k.emit("tensor", lambda E, it=it, pt=pt: E.transpose(
                        pt[:, it * 128:(it + 1) * 128],
                        shq[:, it * 128:(it + 1) * 128], sib[:]))
                k.wait("vector", tt)
                vhT = k.emit("vector", lambda E, nt=nt, pt=pt: E.tensor_copy(
                    shqT[:, 0:nt * 128], pt[:, 0:nt * 128]))
                vhT_last[ci % 2] = vhT
                if stage == "B2":
                    continue
                k.wait("tensor", vhT)
                if ci == 0:
                    k.wait("tensor", vy[0])
                    k.wait("tensor", vy[1])
                last_ci = len(FFN_CHUNKS) - 1
                for it in range(nt):
                    iti = c0 // 128 + it
                    for half in range(2):
                        tdn = k.emit("tensor", lambda E, it=it, iti=iti, half=half, ci=ci, nt=nt, last_ci=last_ci: E.matmul(
                            ps_o[half][:, 0:512],
                            shqT[:, it * 128:(it + 1) * 128],
                            swd[:, iti * D + half * 512: iti * D + half * 512 + 512],
                            start=(ci == 0 and it == 0), stop=(ci == last_ci and it == nt - 1)))
            if stage in ("B1", "B2", "B3"):
                break
            k.wait("vector", tdn)
            k.wait("vector", ld2)
            if t >= 2:
                k.wait("vector", st_y[t % 2])
            for half in range(2):
                k.emit("vector", lambda E, half=half, t=t: E.tensor_scalar(
                    sy[t % 2][:, half * 512:(half + 1) * 512], ps_o[half][:, 0:512],
                    0.5, None, ALU.mult))
            for half in range(2):
                vy[t % 2] = k.emit("vector", lambda E, half=half, t=t: E.tensor_add(
                    sy[t % 2][:, half * 512:(half + 1) * 512],
                    sy[t % 2][:, half * 512:(half + 1) * 512],
                    sx2[t % 2][:, half * 512:(half + 1) * 512]))
            k.wait("sync", vy[t % 2])
            st_y[t % 2] = k.dma("sync", lambda E, t=t: E.dma_start(out=y_t[t], in_=sy[t % 2][:]),
                                key=f"y{t % 2}")
        if stage in ("B1", "B2", "B3"):
            k.barrier()
            sA = k.dma("sync", lambda E: E.dma_start(out=y_t[0], in_=sx[0][:]), key="ya")
            k.wait("sync", sA)
            k.barrier()
            k.finalize()
            return nc
        k.wait("sync", st_y[0])
        k.wait("sync", st_y[1])
        k.barrier()
        k.finalize()
    return nc



def build_l0attn(tok=TOK, seq=SEQ):
    H = 16
    HD = 64
    nmt = seq // 128
    nqc = tok // 512
    nmg = seq // 512
    nbias = min(nmt, 9)
    VW = H * 65

    nc = bass.Bass("TRN2", target_bir_lowering=False, dynamic_dma_scratch_size=8192)
    x_own = nc.dram_tensor("x_own", [tok, D], F32, kind="ExternalInput").ap()
    x_full = nc.dram_tensor("x_full", [seq, D], F32, kind="ExternalInput").ap()
    wqkvT = nc.dram_tensor("wqkvT", [D, E3], F32, kind="ExternalInput").ap()
    wprojT = nc.dram_tensor("wprojT", [D, D], F32, kind="ExternalInput").ap()
    gb = nc.dram_tensor("gb", [128, D], F32, kind="ExternalInput").ap()
    bbb = nc.dram_tensor("bbb", [128, D], F32, kind="ExternalInput").ap()
    bpb = nc.dram_tensor("bpb", [128, D], F32, kind="ExternalInput").ap()
    Bq = nc.dram_tensor("Bq", [nqc, nbias, 128, 512], F32, kind="ExternalInput").ap()
    identb = nc.dram_tensor("identb", [128, 128], F32, kind="ExternalInput").ap()
    y_out = nc.dram_tensor("y", [tok, D], F32, kind="ExternalOutput").ap()

    xf_t = x_full.rearrange("(n p) m -> n p m", p=128)
    xo_t = x_own.rearrange("(n p) m -> n p m", p=128)
    y_t = y_out.rearrange("(n p) m -> n p m", p=128)
    SCALE = 1.0 / 8.0

    with ExitStack() as ctx:
        e = ctx.enter_context
        k = K(nc, ctx)
        sKT = e(nc.sbuf_tensor("sKT", [128, 8 * seq], BF16))
        sV = e(nc.sbuf_tensor("sV", [128, nmt * VW], BF16))
        sQT = e(nc.sbuf_tensor("sQT", [128, 8 * tok], BF16))
        sAT = e(nc.sbuf_tensor("sAT", [128, 8 * tok], BF16))
        sxnTg = e(nc.sbuf_tensor("sxnTg", [128, max(8 * 512, nbias * 512)], BF16))
        sx = [e(nc.sbuf_tensor(f"sx{i}", [128, D], F32)) for i in range(2)]
        sxnb = e(nc.sbuf_tensor("sxnb", [128, D], BF16))
        sst = e(nc.sbuf_tensor("sst", [128, 18], F32))
        sib = e(nc.sbuf_tensor("sib", [128, 128], BF16))
        sone = e(nc.sbuf_tensor("sone", [128, 1], F32))
        sorow = e(nc.sbuf_tensor("sorow", [1, 128], F32))
        sonesb = e(nc.sbuf_tensor("sonesb", [128, 128], BF16))
        skn2 = e(nc.sbuf_tensor("skn2", [128, 9], F32))
        cscr = nc.dram_tensor("cscr", [16, tok], F32, kind="Internal").ap()
        if seq == 4096:
            sw = sAT[:, 0:16384]
            ssq_big = sAT[:, 0:8192].bitcast(F32)
            sw4 = sKT[:, 0:8 * 1024]
            sgb = sKT[:, 8192:10240].bitcast(F32)
            sbbb = sKT[:, 10240:12288].bitcast(F32)
            sy = sKT[:, 12288:14336].bitcast(F32)
            sxn = sKT[:, 14336:16384].bitcast(F32)
            sxn2 = sKT[:, 16384:18432].bitcast(F32)
        else:
            sw = e(nc.sbuf_tensor("sw", [128, 16384], BF16))[:, 0:16384]
            ssq_big = e(nc.sbuf_tensor("ssqb", [128, 4096], F32))[:, 0:4096]
            sw4 = sw[:, 0:8 * 1024]
            sgb = e(nc.sbuf_tensor("sgbt", [128, D], F32))[:, :]
            sbbb = e(nc.sbuf_tensor("sbbbt", [128, D], F32))[:, :]
            sy = e(nc.sbuf_tensor("syt", [128, D], F32))[:, :]
            sxn = e(nc.sbuf_tensor("sxnt", [128, D], F32))[:, :]
            sxn2 = e(nc.sbuf_tensor("sxn2t", [128, D], F32))[:, :]
        sbias = sxnTg[:, 0:nbias * 512]
        szrow = sx[0][0:1, 512:1024]
        snrm = sx[1][0:1, 500:1020]
        ssc = sx[0][:, 0:512]
        sPt = [sx[1][:, 0:256].bitcast(BF16), sx[1][:, 256:512].bitcast(BF16)]
        szl = sx[1][0:64, 512:1024]
        szi = sx[1][64:128, 512:1024]
        ps_s = [e(nc.psum_tensor(f"ps_s{i}", [128, 512], F32)) for i in range(2)]
        ps_a = [e(nc.psum_tensor(f"ps_a{i}", [128, 512], F32)) for i in range(2)]
        ps_z = e(nc.psum_tensor("ps_z", [128, 512], F32))
        ps_t = [e(nc.psum_tensor(f"ps_t{i}", [128, 1024], BF16)) for i in range(2)]

        lid = k.dma("gpsimd", lambda E: E.dma_start(out=sib[:], in_=identb), key="id")

        xfree = [None, None]
        xldlast = [None, None]

        def ln_transpose_group(src_t, g, ld_prev_tok):
            vtok = None
            for j in range(4):
                t = 4 * g + j
                k.wait("sync", xfree[j % 2])
                ldx = k.dma("sync", lambda E, t=t, j=j, src_t=src_t: E.dma_start(
                    out=sx[j % 2][:], in_=xf_t[t] if src_t is xf_t else xo_t[t]),
                    key=f"x{j % 2}")
                xldlast[j % 2] = ldx
                k.wait("vector", ldx)
                k.wait("scalar", ldx)
                emit_ln(k, sx[j % 2][:], sxnb[:], {"st": sst})
                xfree[j % 2] = k.token("vector")
                vln = k.token("vector")
                k.wait("tensor", vln)
                if g == 0 and j == 0:
                    k.wait("tensor", lid)
                ttr = None
                for dt in range(8):
                    ttr = k.emit("tensor", lambda E, dt=dt, j=j: E.transpose(
                        ps_t[j % 2][:, dt * 128:(dt + 1) * 128],
                        sxnb[:, dt * 128:(dt + 1) * 128], sib[:]))
                k.wait("vector", ttr)
                dst = sxnTg[:, 0:4096].rearrange("p (n m) -> p n m", m=512)[:, :, j * 128:(j + 1) * 128]
                srcp = ps_t[j % 2][:].rearrange("p (n m) -> p n m", m=128)
                vtok = k.emit("vector", lambda E, dst=dst, srcp=srcp: E.tensor_copy(dst, srcp))
                if j < 3:
                    k.wait("tensor", vtok)
            return vtok

        lwk = k.dma("gpsimd", lambda E: E.dma_start(
            out=sw[:].rearrange("p (n m) -> p n m", m=2048),
            in_=wqkvT[:, 1024:3072].rearrange("(n p) m -> p n m", p=128)), key="wk")
        vsetones = k.emit("vector", lambda E: E.memset(
            sV[:].rearrange("p (n m) -> p n m", m=65)[:, :, 64:65], 1.0))
        prev_group_done = [None]
        for g in range(nmg):
            vg = ln_transpose_group(xf_t, g, None)
            k.wait("tensor", vg)
            k.wait("tensor", lwk)
            for et in range(8):
                tm = None
                for dt in range(8):
                    tm = k.emit("tensor", lambda E, et=et, dt=dt: E.matmul(
                        ps_s[et % 2][:, 0:512],
                        sw[:, dt * 2048 + et * 128: dt * 2048 + et * 128 + 128],
                        sxnTg[:, dt * 512: dt * 512 + 512],
                        start=(dt == 0), stop=(dt == 7)))
                k.wait("vector", tm)
                vev = k.emit("vector", lambda E, et=et, g=g: E.tensor_copy(
                    sKT[:, et * seq + g * 512: et * seq + g * 512 + 512], ps_s[et % 2][:, 0:512]))
                if et >= 1:
                    pass
                k.wait("tensor", vev)
            for j in range(4):
                t = 4 * g + j
                for ch in range(2):
                    tm = None
                    for dt in range(8):
                        tm = k.emit("tensor", lambda E, j=j, ch=ch, dt=dt: E.matmul(
                            ps_a[ch][:, 0:512],
                            sxnTg[:, dt * 512 + j * 128: dt * 512 + j * 128 + 128],
                            sw[:, dt * 2048 + 1024 + ch * 512: dt * 2048 + 1024 + ch * 512 + 512],
                            start=(dt == 0), stop=(dt == 7)))
                    k.wait("vector", tm)
                    dstv = sV[:].rearrange("p (n m) -> p n m", m=65)[
                        :, t * 16 + ch * 8:t * 16 + (ch + 1) * 8, 0:64]
                    srcv = ps_a[ch][:, 0:512].rearrange("p (n m) -> p n m", m=64)
                    vev = k.emit("vector", lambda E, dstv=dstv, srcv=srcv: E.tensor_copy(dstv, srcv))
                    k.wait("tensor", vev)
            prev_group_done[0] = k.token("vector")

        k.wait("gpsimd", prev_group_done[0])
        k.wait("gpsimd", lwk)
        lwq = k.dma("gpsimd", lambda E: E.dma_start(
            out=sw[:, 0:8 * 1024].rearrange("p (n m) -> p n m", m=1024),
            in_=wqkvT[:, 0:1024].rearrange("(n p) m -> p n m", p=128)), key="wk")
        for g in range(tok // 512):
            vg = ln_transpose_group(xo_t, g, None)
            k.wait("tensor", vg)
            k.wait("tensor", lwq)
            for et in range(8):
                tm = None
                for dt in range(8):
                    tm = k.emit("tensor", lambda E, et=et, dt=dt: E.matmul(
                        ps_s[et % 2][:, 0:512],
                        sw[:, dt * 1024 + et * 128: dt * 1024 + et * 128 + 128],
                        sxnTg[:, dt * 512: dt * 512 + 512],
                        start=(dt == 0), stop=(dt == 7)))
                k.wait("vector", tm)
                vev = k.emit("vector", lambda E, et=et, g=g: E.tensor_copy(
                    sQT[:, et * tok + g * 512: et * tok + g * 512 + 512], ps_s[et % 2][:, 0:512]))
                k.wait("tensor", vev)

        k.wait("scalar", k.token("vector"))
        k.emit("vector", lambda E: E.memset(sone[:], 1.0))
        k.emit("vector", lambda E: E.memset(sorow[:], 1.0))
        k.emit("vector", lambda E: E.memset(sonesb[:], 1.0))
        k.emit("vector", lambda E: E.memset(skn2[:], 0.0))
        k.wait("tensor", k.token("vector"))
        ssq_k = ssq_big[:, 0:seq]
        last_tm = [None]
        for et in range(8):
            k.wait("scalar", last_tm[0])
            sq_t = k.emit("scalar", lambda E, et=et: E.activation(
                ssq_k, sKT[:, et * seq:(et + 1) * seq], AF.Square))
            k.wait("tensor", sq_t)
            for ch in range(seq // 512):
                for hb, base in ((0, 0), (1, 64)):
                    tm = k.emit("tensor", lambda E, ch=ch, base=base: E.matmul(
                        ps_z[base:base + 1, 0:512], sone[base:base + 64, 0:1],
                        ssq_big[base:base + 64, ch * 512:(ch + 1) * 512],
                        start=True, stop=True))
                    last_tm[0] = tm
                    k.wait("vector", tm)
                    k.emit("vector", lambda E, base=base: E.reduce_max(
                        snrm[0:1, 0:1] if base == 0 else None, None, None)) if False else None
                    k.emit("vector", lambda E, base=base, et=et: E.reduce_max(
                        skn2[base:base + 1, 8:9], ps_z[base:base + 1, 0:512], AX.X))
                    vm = k.emit("vector", lambda E, base=base, et=et: E.tensor_max(
                        skn2[base:base + 1, et:et + 1], skn2[base:base + 1, et:et + 1],
                        skn2[base:base + 1, 8:9]))
                    k.wait("tensor", vm)
        ssq_q = ssq_big[:, 0:tok]
        crow = sAT[:, 2 * tok: 2 * tok + 2 * tok].bitcast(F32)
        st_c = [None]
        for et in range(8):
            k.wait("scalar", last_tm[0])
            k.wait("scalar", k.token("vector"))
            sq_t = k.emit("scalar", lambda E, et=et: E.activation(
                ssq_q, sQT[:, et * tok:(et + 1) * tok], AF.Square))
            k.wait("tensor", sq_t)
            k.wait("tensor", st_c[0])
            for qc in range(tok // 512):
                for hb, base in ((0, 0), (1, 64)):
                    k.wait("tensor", k.token("vector"))
                    tm = k.emit("tensor", lambda E, qc=qc, base=base: E.matmul(
                        ps_z[base:base + 1, 0:512], sone[base:base + 64, 0:1],
                        ssq_q[base:base + 64, qc * 512:(qc + 1) * 512],
                        start=True, stop=True))
                    last_tm[0] = tm
                    k.wait("vector", tm)
                    vv = k.emit("vector", lambda E, qc=qc, base=base, et=et: E.tensor_scalar(
                        crow[base:base + 1, qc * 512:(qc + 1) * 512],
                        ps_z[base:base + 1, 0:512], skn2[base:base + 1, et:et + 1],
                        None, ALU.mult))
                    k.wait("scalar", vv)
                    sv = k.emit("scalar", lambda E, qc=qc, base=base: E.activation(
                        crow[base:base + 1, qc * 512:(qc + 1) * 512],
                        crow[base:base + 1, qc * 512:(qc + 1) * 512], AF.Sqrt))
                    k.wait("vector", sv)
                    k.emit("vector", lambda E, qc=qc, base=base: E.tensor_scalar(
                        crow[base:base + 1, qc * 512:(qc + 1) * 512],
                        crow[base:base + 1, qc * 512:(qc + 1) * 512], -1.0, None, ALU.mult))
            vfin = k.token("vector")
            k.wait("sync", vfin)
            st_c[0] = k.dma("sync", lambda E, et=et: E.dma_start(
                out=cscr[2 * et:2 * et + 2, :],
                in_=crow[0:65:64, 0:tok]), key="cs")
        k.wait("gpsimd", st_c[0])
        p2v = k.token("vector")
        exp_done = [p2v, p2v]
        pv_free = [p2v, p2v]
        k.wait("gpsimd", k.token("tensor"))
        k.wait("vector", xldlast[0])
        k.wait("scalar", xldlast[1])
        k.wait("vector", xldlast[1])
        scnr = sxnb[:, 0:512]
        scnr_tok = None
        for qc in range(nqc):
            k.wait("gpsimd", k.token("vector"))
            lbq = k.dma("gpsimd", lambda E, qc=qc: E.dma_start(
                out=sbias[:].rearrange("p (n m) -> p n m", m=512),
                in_=Bq[qc]), key="bq")
            for hp in range(8):
                k.wait("gpsimd", k.token("tensor"))
                scnr_tok = k.dma("gpsimd", lambda E, qc=qc, hp=hp: E.dma_start(
                    out=scnr[0:65:64, 0:512],
                    in_=cscr[2 * hp:2 * hp + 2, qc * 512:(qc + 1) * 512]), key="cn")
                k.wait("tensor", scnr_tok)
                z_read = [None, None]
                for mt in range(nmt):
                    for hi in range(2):
                        h = 2 * hp + hi
                        base = 64 * (h % 2)
                        lhsK = sKT[base:base + 64,
                                   (h // 2) * seq + mt * 128:(h // 2) * seq + (mt + 1) * 128]
                        rhsQ = sQT[base:base + 64,
                                   (h // 2) * tok + qc * 512:(h // 2) * tok + (qc + 1) * 512]
                        k.wait("tensor", exp_done[hi])
                        k.emit("tensor", lambda E, lhsK=lhsK, rhsQ=rhsQ, hi=hi: E.matmul(
                            ps_s[hi][:, 0:512], lhsK, rhsQ, start=True, stop=False))
                        tsc = k.emit("tensor", lambda E, hi=hi, base=base: E.matmul(
                            ps_s[hi][:, 0:512], sonesb[base:base + 1, :],
                            scnr[base:base + 1, 0:512], start=False, stop=True))
                        if mt < nbias:
                            k.wait("vector", tsc)
                            k.wait("vector", lbq)
                            k.emit("vector", lambda E, hi=hi: E.tensor_scalar(
                                ssc[:], ps_s[hi][:, 0:512], SCALE, None, ALU.mult))
                            k.emit("vector", lambda E, mt=mt: E.tensor_add(
                                ssc[:], ssc[:], sbias[:, mt * 512:(mt + 1) * 512]))
                            vv = k.token("vector")
                            k.wait("scalar", vv)
                            sexp = k.emit("scalar", lambda E, hi=hi: E.activation(
                                sPt[hi][:], ssc[:], AF.Exp, bias=0.0, scale=1.0))
                            exp_done[hi] = sexp
                        else:
                            k.wait("scalar", tsc)
                            sexp = k.emit("scalar", lambda E, hi=hi: E.activation(
                                sPt[hi][:], ps_s[hi][:, 0:512], AF.Exp, bias=0.0, scale=SCALE))
                            exp_done[hi] = sexp
                        k.wait("tensor", sexp)
                        if mt == 0:
                            k.wait("tensor", pv_free[hi])
                        lhsV = sV[:].rearrange("p (n m) -> p n m", m=65)[
                            :, mt * 16 + h: mt * 16 + h + 1, 0:65].squeeze(1)
                        tpv = k.emit("tensor", lambda E, lhsV=lhsV, hi=hi, mt=mt, nmt=nmt: E.matmul(
                            ps_a[hi][0:65, 0:512], lhsV, sPt[hi][:],
                            start=(mt == 0), stop=(mt == nmt - 1)))
                        if mt == nmt - 1:
                            z_read[hi] = tpv
                for hi in range(2):
                    h = 2 * hp + hi
                    k.wait("vector", z_read[hi])
                    vz = k.emit("vector", lambda E, hi=hi: E.tensor_copy(szrow[:], ps_a[hi][64:65, 0:512]))
                    k.wait("tensor", vz)
                    tz = k.emit("tensor", lambda E: E.matmul(
                        ps_z[0:64, 0:512], sorow[0:1, 0:64],
                        szrow[:], start=True, stop=True))
                    k.wait("scalar", tz)
                    sl1 = k.emit("scalar", lambda E: E.activation(szl[:], ps_z[0:64, 0:512], AF.Ln))
                    k.wait("scalar", sl1)
                    sl2 = k.emit("scalar", lambda E: E.activation(szi[:], szl[:], AF.Exp, scale=-1.0))
                    k.wait("vector", sl2)
                    base = 64 * (h % 2)
                    dsta = sAT[base:base + 64,
                               (h // 2) * tok + qc * 512:(h // 2) * tok + (qc + 1) * 512]
                    pv_free[hi] = k.emit("vector", lambda E, dsta=dsta, hi=hi: E.tensor_tensor(
                        dsta, ps_a[hi][0:64, 0:512], szi[:], ALU.mult))
                    k.wait("tensor", sl1)

        p3v = k.token("vector")
        p3s = k.token("scalar")
        p3t = k.token("tensor")
        k.wait("gpsimd", lwq)
        k.wait("gpsimd", p3v)
        k.wait("gpsimd", p3t)
        k.wait("gpsimd", p3s)
        lwp = k.dma("gpsimd", lambda E: E.dma_start(
            out=sw4.rearrange("p (n m) -> p n m", m=1024),
            in_=wprojT.rearrange("(n p) m -> p n m", p=128)), key="wk")
        k.wait("sync", p3t)
        k.wait("sync", p3v)
        lbp = k.dma("sync", lambda E: E.dma_start(out=sy[:], in_=bpb), key="bp")
        lgb = k.dma("sync", lambda E: E.dma_start(out=sgb[:], in_=gb), key="gb")
        lbb = k.dma("sync", lambda E: E.dma_start(out=sbbb[:], in_=bbb), key="bb")
        k.wait("vector", lbp)
        k.wait("vector", lbb)
        vbb = k.emit("vector", lambda E: E.tensor_add(sbbb[:], sbbb[:], sy[:]))
        st_y = [None, None]
        pa_free = [None, None]
        for t in range(tok // 128):
            if t < 2:
                k.wait("sync", p3v)
                k.wait("sync", p3s)
                k.wait("sync", p3t)
            ldx = k.dma("sync", lambda E, t=t: E.dma_start(out=sx[t % 2][:], in_=xo_t[t]), key=f"x{t % 2}")
            k.wait("vector", ldx)
            k.wait("scalar", ldx)
            if t >= 1:
                k.wait("vector", st_y[(t - 1) % 2])
                k.wait("scalar", st_y[(t - 1) % 2])
            emit_ln(k, sx[t % 2][:], sxn, {"st": sst})
            k.wait("scalar", k.token("vector"))
            emit_ln(k, sxn, sxn2, {"st": sst})
            if t == 0:
                k.wait("vector", lgb)
            k.emit("vector", lambda E: E.tensor_mul(sxn2[:], sxn2[:], sgb[:]))
            vgl = k.emit("vector", lambda E: E.tensor_add(sxn2[:], sxn2[:], sbbb[:]))
            if t == 0:
                k.wait("tensor", lwp)
                k.wait("tensor", k.token("vector"))
            for half in range(2):
                k.wait("tensor", pa_free[half])
                tm = None
                for dt in range(8):
                    tm = k.emit("tensor", lambda E, dt=dt, half=half, t=t: E.matmul(
                        ps_a[half][:, 0:512],
                        sAT[:, dt * tok + t * 128: dt * tok + (t + 1) * 128],
                        sw4[:, dt * 1024 + half * 512: dt * 1024 + half * 512 + 512],
                        start=(dt == 0), stop=(dt == 7)))
                k.wait("vector", tm)
                pa_free[half] = k.emit("vector", lambda E, half=half, t=t: E.tensor_add(
                    sxn[:, half * 512:(half + 1) * 512],
                    ps_a[half][:, 0:512], sx[t % 2][:, half * 512:(half + 1) * 512]))
            vy = k.emit("vector", lambda E: E.tensor_add(sxn[:], sxn[:], sxn2[:]))
            k.wait("sync", vy)
            st_y[t % 2] = k.dma("sync", lambda E, t=t: E.dma_start(out=y_t[t], in_=sxn[:]), key=f"y{t % 2}")
            k.wait("vector", st_y[t % 2]) if t >= 14 else None
        k.wait("sync", st_y[0])
        k.wait("sync", st_y[1])
        k.barrier()
        k.finalize()
    return nc



def build_l1ffn(tok=TOK):
    nc = bass.Bass("TRN2", target_bir_lowering=False)
    x_in = nc.dram_tensor("x", [tok, D], F32, kind="ExternalInput").ap()
    xb_in = nc.dram_tensor("xb", [1, D], F32, kind="ExternalInput").ap()
    wrT = nc.dram_tensor("wrT", [D, D], F32, kind="ExternalInput").ap()
    wkT = nc.dram_tensor("wkT", [D, D], F32, kind="ExternalInput").ap()
    wvT = nc.dram_tensor("wvT", [D, D], F32, kind="ExternalInput").ap()
    mur = nc.dram_tensor("mur", [128, 8], F32, kind="ExternalInput").ap()
    muk = nc.dram_tensor("muk", [128, 8], F32, kind="ExternalInput").ap()
    identb = nc.dram_tensor("identb", [128, 128], F32, kind="ExternalInput").ap()
    y_out = nc.dram_tensor("y", [tok, D], F32, kind="ExternalOutput").ap()
    x_t = x_in.rearrange("(n p) m -> n p m", p=128)
    y_t = y_out.rearrange("(n p) m -> n p m", p=128)
    ntt = tok // 128
    dscr = nc.dram_tensor("dscr", [D], F32, kind="Internal").ap()

    with ExitStack() as ctx:
        e = ctx.enter_context
        k = K(nc, ctx)
        sxnT = e(nc.sbuf_tensor("sxnT", [128, 8 * tok], BF16))
        sxrT = e(nc.sbuf_tensor("sxrT", [128, 8 * tok], BF16))
        sxkT = e(nc.sbuf_tensor("sxkT", [128, 8 * tok], BF16))
        skaT = e(nc.sbuf_tensor("skaT", [128, 8 * tok], BF16))
        swr = e(nc.sbuf_tensor("swr", [128, 8 * D], BF16))
        swk = e(nc.sbuf_tensor("swk", [128, 8 * D], BF16))
        swv = e(nc.sbuf_tensor("swv", [128, 8 * D], BF16))
        sx = [e(nc.sbuf_tensor(f"sx{i}", [128, D], F32)) for i in range(2)]
        sxnb = e(nc.sbuf_tensor("sxnb", [128, D], BF16))
        sst = e(nc.sbuf_tensor("sst", [128, 18], F32))
        sib = e(nc.sbuf_tensor("sib", [128, 128], BF16))
        smu = e(nc.sbuf_tensor("smu", [128, 16], F32))
        sxbc = e(nc.sbuf_tensor("sxbc", [128, 16], BF16))
        sxbrow = e(nc.sbuf_tensor("sxbrow", [1, D], F32))
        srl = e(nc.sbuf_tensor("srl", [128, 512], BF16))
        sr = e(nc.sbuf_tensor("sr", [128, D], F32))
        sv1 = e(nc.sbuf_tensor("sv1", [128, D], F32))
        sy = e(nc.sbuf_tensor("sy", [128, D], F32))
        ps_a = [e(nc.psum_tensor(f"ps_a{i}", [128, 512], F32)) for i in range(2)]
        ps_b = [e(nc.psum_tensor(f"ps_b{i}", [128, 512], F32)) for i in range(2)]
        ps_t = [e(nc.psum_tensor(f"ps_t{i}", [128, 1024], BF16)) for i in range(2)]

        lid = k.dma("gpsimd", lambda E: E.dma_start(out=sib[:], in_=identb), key="id")
        lmur = k.dma("sync", lambda E: E.dma_start(out=smu[:, 0:8], in_=mur), key="mu")
        lmuk = k.dma("sync", lambda E: E.dma_start(out=smu[:, 8:16], in_=muk), key="mu2")
        lwr = k.dma("gpsimd", lambda E: E.dma_start(
            out=swr[:].rearrange("p (n m) -> p n m", m=D),
            in_=wrT.rearrange("(n p) m -> p n m", p=128)), key="wr")
        lwk = k.dma("gpsimd", lambda E: E.dma_start(
            out=swk[:].rearrange("p (n m) -> p n m", m=D),
            in_=wkT.rearrange("(n p) m -> p n m", p=128)), key="wk2")
        lwv = k.dma("gpsimd", lambda E: E.dma_start(
            out=swv[:].rearrange("p (n m) -> p n m", m=D),
            in_=wvT.rearrange("(n p) m -> p n m", p=128)), key="wv")
        lxb = k.dma("sync", lambda E: E.dma_start(out=sxbrow[:], in_=xb_in), key="xb")

        vcp = [None] * ntt
        ld = [None] * ntt
        for t in range(ntt):
            if t >= 2:
                k.wait("sync", vcp[t - 2])
            ld[t] = k.dma("sync", lambda E, t=t: E.dma_start(out=sx[t % 2][:], in_=x_t[t]), key=f"x{t % 2}")
            k.wait("vector", ld[t])
            emit_ln(k, sx[t % 2][:], sxnb[:], {"st": sst})
            vln = k.token("vector")
            k.wait("tensor", vln)
            if t == 0:
                k.wait("tensor", lid)
            if t >= 2:
                k.wait("tensor", vcp[t - 2])
            ttr = None
            for dt in range(8):
                ttr = k.emit("tensor", lambda E, dt=dt, t=t: E.transpose(
                    ps_t[t % 2][:, dt * 128:(dt + 1) * 128],
                    sxnb[:, dt * 128:(dt + 1) * 128], sib[:]))
            k.wait("vector", ttr)
            dst = sxnT[:].rearrange("p (n m) -> p n m", m=tok)[:, :, t * 128:(t + 1) * 128]
            srcp = ps_t[t % 2][:].rearrange("p (n m) -> p n m", m=128)
            vcp[t] = k.emit("vector", lambda E, dst=dst, srcp=srcp: E.tensor_copy(dst, srcp))
        k.wait("vector", lxb)
        emit_ln(k, sxbrow[:], sxbrow[:], {"st": sst})
        vbr = k.token("vector")
        k.wait("sync", vbr)
        stb = k.dma("sync", lambda E: E.dma_start(out=dscr, in_=sxbrow[:]), key="xb2")
        k.wait("sync", stb)
        k.wait("gpsimd", stb)
        lxc = k.dma("gpsimd", lambda E: E.dma_start(
            out=sxbc[:, 0:8], in_=dscr.rearrange("(n p) -> p n", p=128),
            allow_slow_non_contiguous=True), key="xb3")

        k.wait("vector", lxc)
        k.wait("vector", lmur)
        k.wait("vector", lmuk)
        for which, dstT, mu0 in (("r", sxrT, 0), ("k", sxkT, 8)):
            for dt in range(8):
                xo = dt * tok
                k.emit("vector", lambda E, xo=xo, dstT=dstT: E.tensor_sub(
                    dstT[:, xo + 1: xo + tok], sxnT[:, xo + 1: xo + tok], sxnT[:, xo: xo + tok - 1]))
                k.emit("vector", lambda E, xo=xo, dstT=dstT, mu0=mu0, dt=dt: E.scalar_tensor_tensor(
                    dstT[:, xo + 1: xo + tok], dstT[:, xo + 1: xo + tok],
                    smu[:, mu0 + dt: mu0 + dt + 1], sxnT[:, xo: xo + tok - 1], ALU.mult, ALU.add))
                k.emit("vector", lambda E, xo=xo, dstT=dstT, dt=dt: E.tensor_sub(
                    dstT[:, xo: xo + 1], sxnT[:, xo: xo + 1], sxbc[:, dt: dt + 1]))
                k.emit("vector", lambda E, xo=xo, dstT=dstT, mu0=mu0, dt=dt: E.scalar_tensor_tensor(
                    dstT[:, xo: xo + 1], dstT[:, xo: xo + 1],
                    smu[:, mu0 + dt: mu0 + dt + 1], sxbc[:, dt: dt + 1], ALU.mult, ALU.add))
        vmix = k.token("vector")

        k.wait("tensor", vmix)
        k.wait("tensor", lwk)
        srl_free = [None, None]
        for kt in range(8):
            for ch in range(tok // 512):
                pb = ps_b[ch % 2]
                k.wait("tensor", srl_free[ch % 2])
                tm = None
                for ct in range(8):
                    tm = k.emit("tensor", lambda E, kt=kt, ch=ch, ct=ct, pb=pb: E.matmul(
                        pb[:, 0:512],
                        swk[:, ct * D + kt * 128: ct * D + kt * 128 + 128],
                        sxkT[:, ct * tok + ch * 512: ct * tok + (ch + 1) * 512],
                        start=(ct == 0), stop=(ct == 7)))
                k.wait("vector", tm)
                vrl = k.emit("vector", lambda E, pb=pb: E.tensor_scalar(
                    srl[:], pb[:, 0:512], 0.0, None, ALU.max))
                srl_free[ch % 2] = vrl
                k.wait("scalar", vrl)
                ssqk = k.emit("scalar", lambda E, kt=kt, ch=ch: E.activation(
                    skaT[:, kt * tok + ch * 512: kt * tok + (ch + 1) * 512], srl[:], AF.Square))
                k.wait("vector", ssqk)
                srl_free[ch % 2] = ssqk

        k.wait("tensor", lwr)
        k.wait("tensor", lwv)
        k.wait("tensor", k.token("scalar"))
        st_y = [None, None]
        pr_free = [None, None]
        pv_free = [None, None]
        for t in range(ntt):
            if t < 2:
                k.wait("sync", vcp[ntt - 2 + t])
            else:
                k.wait("sync", st_y[t % 2])
            ld2 = k.dma("sync", lambda E, t=t: E.dma_start(out=sx[t % 2][:], in_=x_t[t]), key=f"x{t % 2}")
            for half in range(2):
                k.wait("tensor", pr_free[half])
                tm = None
                for ct in range(8):
                    tm = k.emit("tensor", lambda E, ct=ct, half=half, t=t: E.matmul(
                        ps_a[half][:, 0:512],
                        sxrT[:, ct * tok + t * 128: ct * tok + (t + 1) * 128],
                        swr[:, ct * D + half * 512: ct * D + half * 512 + 512],
                        start=(ct == 0), stop=(ct == 7)))
                k.wait("scalar", tm)
                k.wait("scalar", pv_free[half])
                pr_free[half] = k.emit("scalar", lambda E, half=half: E.activation(
                    sr[:, half * 512:(half + 1) * 512], ps_a[half][:, 0:512], AF.Sigmoid))
            for half in range(2):
                k.wait("tensor", pv_free[half])
                tm = None
                for ct in range(8):
                    tm = k.emit("tensor", lambda E, ct=ct, half=half, t=t: E.matmul(
                        ps_b[half][:, 0:512],
                        skaT[:, ct * tok + t * 128: ct * tok + (t + 1) * 128],
                        swv[:, ct * D + half * 512: ct * D + half * 512 + 512],
                        start=(ct == 0), stop=(ct == 7)))
                k.wait("vector", tm)
                k.wait("vector", pr_free[half])
                pv_free[half] = k.emit("vector", lambda E, half=half: E.tensor_tensor(
                    sv1[:, half * 512:(half + 1) * 512], sr[:, half * 512:(half + 1) * 512],
                    ps_b[half][:, 0:512], ALU.mult))
            k.wait("vector", ld2)
            if t >= 1:
                k.wait("vector", st_y[(t - 1) % 2])
            vy = k.emit("vector", lambda E, t=t: E.scalar_tensor_tensor(
                sy[:], sv1[:], 0.5, sx[t % 2][:], ALU.mult, ALU.add))
            k.wait("sync", vy)
            st_y[t % 2] = k.dma("sync", lambda E, t=t: E.dma_start(out=y_t[t], in_=sy[:]), key=f"y{t % 2}")
            k.wait("vector", st_y[t % 2]) if False else None
        k.wait("sync", st_y[0])
        k.wait("sync", st_y[1])
        k.barrier()
        k.finalize()
    return nc



def build_l1attn(tok=TOK, seq=SEQ):
    H, HD, FEAT = 16, 64, 256
    nmt = seq // 128
    ntt = tok // 128
    VW = H * 65

    nc = bass.Bass("TRN2", target_bir_lowering=False, dynamic_dma_scratch_size=8192)
    x_own = nc.dram_tensor("x_own", [tok, D], F32, kind="ExternalInput").ap()
    x_full = nc.dram_tensor("x_full", [seq, D], F32, kind="ExternalInput").ap()
    wqkv = nc.dram_tensor("wqkv", [D, E3], F32, kind="ExternalInput").ap()
    projp = nc.dram_tensor("projp", [8, 128, FEAT], F32, kind="ExternalInput").ap()
    wout = nc.dram_tensor("wout", [D, D], F32, kind="ExternalInput").ap()
    bob = nc.dram_tensor("bob", [128, D], F32, kind="ExternalInput").ap()
    identb = nc.dram_tensor("identb", [128, 128], F32, kind="ExternalInput").ap()
    y_out = nc.dram_tensor("y", [tok, D], F32, kind="ExternalOutput").ap()

    xf_t = x_full.rearrange("(n p) m -> n p m", p=128)
    xo_t = x_own.rearrange("(n p) m -> n p m", p=128)
    y_t = y_out.rearrange("(n p) m -> n p m", p=128)

    with ExitStack() as ctx:
        e = ctx.enter_context
        k = K(nc, ctx)
        sKT = e(nc.sbuf_tensor("sKT", [128, 8 * seq], BF16))
        sV = e(nc.sbuf_tensor("sV", [128, nmt * VW], BF16))
        sQT = e(nc.sbuf_tensor("sQT", [128, 8 * tok], BF16))
        sxnTg = e(nc.sbuf_tensor("sxnTg", [128, 8 * 512], BF16))
        sx = [e(nc.sbuf_tensor(f"sx{i}", [128, D], F32)) for i in range(2)]
        sxnb = e(nc.sbuf_tensor("sxnb", [128, D], BF16))
        sst = e(nc.sbuf_tensor("sst", [128, 18], F32))
        sib = e(nc.sbuf_tensor("sib", [128, 128], BF16))
        sproj = e(nc.sbuf_tensor("sproj", [128, 8 * FEAT], BF16))
        skv = e(nc.sbuf_tensor("skv", [128, H * 130], BF16))
        skp = [e(nc.sbuf_tensor(f"skp{i}", [128, FEAT], BF16)) for i in range(2)]
        selu = sxnTg[:, 0:1024].bitcast(F32)
        selu2 = sxnTg[:, 1024:2048].bitcast(F32)
        szi = e(nc.sbuf_tensor("szi", [128, 1], F32))
        sy = e(nc.sbuf_tensor("sy", [128, D], F32))
        sbo = e(nc.sbuf_tensor("sbo", [128, D], F32))
        swl = e(nc.sbuf_tensor("swl", [128, 8 * 1024], BF16))
        sattn = None
        sqp = None
        ps_a = [e(nc.psum_tensor(f"ps_a{i}", [128, 512], F32)) for i in range(2)]
        ps_kv = [e(nc.psum_tensor(f"ps_kv{i}", [128, 512], F32)) for i in range(4)]
        ps_t = [e(nc.psum_tensor(f"ps_t{i}", [128, 1024], BF16)) for i in range(2)]

        lid = k.dma("gpsimd", lambda E: E.dma_start(out=sib[:], in_=identb), key="id")
        lpj = k.dma("gpsimd", lambda E: E.dma_start(
            out=sproj[:].rearrange("p (n m) -> p n m", m=FEAT), in_=projp), key="pj")

        xfree = [None, None]
        xldlast = [None, None]

        def ln_group(src_t, g):
            vtok = None
            for j in range(4):
                t = 4 * g + j
                k.wait("sync", xfree[j % 2])
                ldx = k.dma("sync", lambda E, t=t, j=j, src_t=src_t: E.dma_start(
                    out=sx[j % 2][:], in_=xf_t[t] if src_t is xf_t else xo_t[t]),
                    key=f"x{j % 2}")
                xldlast[j % 2] = ldx
                k.wait("vector", ldx)
                emit_ln(k, sx[j % 2][:], sxnb[:], {"st": sst})
                xfree[j % 2] = k.token("vector")
                vln = k.token("vector")
                k.wait("tensor", vln)
                if g == 0 and j == 0:
                    k.wait("tensor", lid)
                ttr = None
                for dt in range(8):
                    ttr = k.emit("tensor", lambda E, dt=dt, j=j: E.transpose(
                        ps_t[j % 2][:, dt * 128:(dt + 1) * 128],
                        sxnb[:, dt * 128:(dt + 1) * 128], sib[:]))
                k.wait("vector", ttr)
                dst = sxnTg[:, 0:4096].rearrange("p (n m) -> p n m", m=512)[:, :, j * 128:(j + 1) * 128]
                srcp = ps_t[j % 2][:].rearrange("p (n m) -> p n m", m=128)
                vtok = k.emit("vector", lambda E, dst=dst, srcp=srcp: E.tensor_copy(dst, srcp))
                if j < 3:
                    k.wait("tensor", vtok)
            return vtok

        lwk = k.dma("gpsimd", lambda E: E.dma_start(
            out=swl[:].rearrange("p (n m) -> p n m", m=1024),
            in_=wqkv[:, 1024:2048].rearrange("(n p) m -> p n m", p=128)), key="wk")
        vone = k.emit("vector", lambda E: E.memset(
            sV[:].rearrange("p (n m) -> p n m", m=65)[:, :, 64:65], 1.0))
        lwv_tok = [None]
        for g in range(seq // 512):
            vg = ln_group(xf_t, g)
            k.wait("tensor", vg)
            k.wait("tensor", lwk)
            for et in range(8):
                tm = None
                for dt in range(8):
                    tm = k.emit("tensor", lambda E, et=et, dt=dt: E.matmul(
                        ps_a[et % 2][:, 0:512],
                        swl[:, dt * 1024 + et * 128: dt * 1024 + et * 128 + 128],
                        sxnTg[:, dt * 512: dt * 512 + 512],
                        start=(dt == 0), stop=(dt == 7)))
                k.wait("vector", tm)
                vev = k.emit("vector", lambda E, et=et, g=g: E.tensor_copy(
                    sKT[:, et * seq + g * 512: et * seq + g * 512 + 512], ps_a[et % 2][:, 0:512]))
                k.wait("tensor", vev)
            prev = k.token("vector")
        k.wait("gpsimd", lwk)
        k.wait("gpsimd", prev)
        lwv = k.dma("gpsimd", lambda E: E.dma_start(
            out=swl[:].rearrange("p (n m) -> p n m", m=1024),
            in_=wqkv[:, 2048:3072].rearrange("(n p) m -> p n m", p=128)), key="wk")
        for g in range(seq // 512):
            vg = ln_group(xf_t, g)
            k.wait("tensor", vg)
            k.wait("tensor", lwv)
            k.wait("tensor", vone)
            for j in range(4):
                t = 4 * g + j
                for ch in range(2):
                    tm = None
                    for dt in range(8):
                        tm = k.emit("tensor", lambda E, j=j, ch=ch, dt=dt: E.matmul(
                            ps_a[ch][:, 0:512],
                            sxnTg[:, dt * 512 + j * 128: dt * 512 + j * 128 + 128],
                            swl[:, dt * 1024 + ch * 512: dt * 1024 + ch * 512 + 512],
                            start=(dt == 0), stop=(dt == 7)))
                    k.wait("vector", tm)
                    dstv = sV[:].rearrange("p (n m) -> p n m", m=65)[
                        :, t * 16 + ch * 8:t * 16 + (ch + 1) * 8, 0:64]
                    srcv = ps_a[ch][:, 0:512].rearrange("p (n m) -> p n m", m=64)
                    vev = k.emit("vector", lambda E, dstv=dstv, srcv=srcv: E.tensor_copy(dstv, srcv))
                    k.wait("tensor", vev)
            prev = k.token("vector")
        k.wait("gpsimd", lwv)
        k.wait("gpsimd", prev)
        lwq = k.dma("gpsimd", lambda E: E.dma_start(
            out=swl[:].rearrange("p (n m) -> p n m", m=1024),
            in_=wqkv[:, 0:1024].rearrange("(n p) m -> p n m", p=128)), key="wk")
        for g in range(tok // 512):
            vg = ln_group(xo_t, g)
            k.wait("tensor", vg)
            k.wait("tensor", lwq)
            for et in range(8):
                tm = None
                for dt in range(8):
                    tm = k.emit("tensor", lambda E, et=et, dt=dt: E.matmul(
                        ps_a[et % 2][:, 0:512],
                        swl[:, dt * 1024 + et * 128: dt * 1024 + et * 128 + 128],
                        sxnTg[:, dt * 512: dt * 512 + 512],
                        start=(dt == 0), stop=(dt == 7)))
                k.wait("vector", tm)
                vev = k.emit("vector", lambda E, et=et, g=g: E.tensor_copy(
                    sQT[:, et * tok + g * 512: et * tok + g * 512 + 512], ps_a[et % 2][:, 0:512]))
                k.wait("tensor", vev)

        k.wait("tensor", lpj)

        def emit_elu1(pin, pout_bf, width):
            v1 = k.emit("vector", lambda E: E.tensor_scalar(selu[:, 0:width], pin[:, 0:width], 0.0, None, ALU.min))
            k.wait("scalar", v1)
            s1 = k.emit("scalar", lambda E: E.activation(selu2[:, 0:width], selu[:, 0:width], AF.Exp))
            k.wait("vector", s1)
            v2 = k.emit("vector", lambda E: E.tensor_scalar(selu[:, 0:width], pin[:, 0:width], 0.0, None, ALU.max))
            v3 = k.emit("vector", lambda E: E.tensor_add(pout_bf, selu[:, 0:width], selu2[:, 0:width]))
            return v3

        kp_free = [None, None]
        kv_done = []
        for hp in range(8):
            kvp = [ps_kv[0], ps_kv[1], ps_kv[2], ps_kv[3]]
            for nt in range(nmt):
                for hi in range(2):
                    h = 2 * hp + hi
                    base = 64 * (h % 2)
                    lhsK = sKT[base:base + 64, (h // 2) * seq + nt * 128:(h // 2) * seq + (nt + 1) * 128]
                    rhsP = sproj[base:base + 64, hp * FEAT:(hp + 1) * FEAT]
                    k.wait("tensor", kp_free[hi])
                    tkp = k.emit("tensor", lambda E, lhsK=lhsK, rhsP=rhsP, hi=hi: E.matmul(
                        ps_a[hi][:, 0:FEAT], lhsK, rhsP, start=True, stop=True))
                    k.wait("vector", tkp)
                    velu = emit_elu1(ps_a[hi], skp[hi][:, 0:FEAT], FEAT)
                    kp_free[hi] = velu
                    k.wait("tensor", velu)
                    for ft in range(2):
                        tkv = k.emit("tensor", lambda E, hi=hi, ft=ft, nt=nt, h=h: E.matmul(
                            kvp[2 * hi + ft][:, 0:65],
                            skp[hi][:, ft * 128:(ft + 1) * 128],
                            sV[:, (nt * 16 + h) * 65:(nt * 16 + h) * 65 + 65],
                            start=(nt == 0), stop=(nt == nmt - 1)))
                        if nt == nmt - 1:
                            k.wait("vector", tkv)
            for hi in range(2):
                h = 2 * hp + hi
                for ft in range(2):
                    k.emit("vector", lambda E, hi=hi, ft=ft, h=h: E.tensor_copy(
                        skv[:, h * 130 + ft * 65: h * 130 + ft * 65 + 65],
                        kvp[2 * hi + ft][:, 0:65]))
            kv_done.append(k.token("vector"))
        kvd = k.token("vector")

        sattn = sV[:, 0:8192]
        sattn = sV[:, 0:ntt * 1024]
        sqp = sV[:, ntt * 1024: ntt * 1024 + 2 * tok]
        qp_free = None
        at_free = [None, None]
        at_done = []
        for hp in range(8):
            for hi in range(2):
                h = 2 * hp + hi
                base = 64 * (h % 2)
                for fc in range(2):
                    for qc in range(tok // 512):
                        rhsQ = sQT[base:base + 64, (h // 2) * tok + qc * 512:(h // 2) * tok + (qc + 1) * 512]
                        lhsP = sproj[base:base + 64, hp * FEAT + fc * 128: hp * FEAT + (fc + 1) * 128]
                        k.wait("tensor", qp_free)
                        tqp = k.emit("tensor", lambda E, lhsP=lhsP, rhsQ=rhsQ, fc=fc: E.matmul(
                            ps_a[fc][:, 0:512], lhsP, rhsQ, start=True, stop=True))
                        k.wait("vector", tqp)
                        dq = sqp.rearrange("p (n m) -> p n m", m=tok)[:, fc:fc + 1, qc * 512:(qc + 1) * 512].squeeze(1)
                        qp_free = emit_elu1(ps_a[fc], dq, 512)
                k.wait("tensor", qp_free)
                if hp == 0 and hi == 0:
                    k.wait("tensor", kvd)
                for tt in range(ntt):
                    k.wait("tensor", at_free[tt % 2])
                    tm = None
                    for ft in range(2):
                        tm = k.emit("tensor", lambda E, ft=ft, tt=tt, h=h: E.matmul(
                            ps_kv[2 + (tt % 2)][:, 0:65],
                            sqp[:, ft * tok + tt * 128: ft * tok + (tt + 1) * 128],
                            skv[:, h * 130 + ft * 65: h * 130 + ft * 65 + 65],
                            start=(ft == 0), stop=(ft == 1)))
                    k.wait("vector", tm)
                    pk = ps_kv[2 + (tt % 2)]
                    k.emit("vector", lambda E, pk=pk: E.tensor_scalar(szi[:], pk[:, 64:65], 1e-8, None, ALU.add))
                    k.emit("vector", lambda E: E.reciprocal(szi[:], szi[:]))
                    va = k.emit("vector", lambda E, pk=pk, tt=tt, h=h: E.tensor_scalar(
                        sattn[:, tt * 1024 + h * 64: tt * 1024 + (h + 1) * 64],
                        pk[:, 0:64], szi[:, 0:1], None, ALU.mult))
                    at_free[tt % 2] = va
                    at_done.append(va)
        atd = k.token("vector")

        sAT = sKT[:, 0:8 * tok]
        if 8 * tok + 8192 <= 8 * seq:
            sw4 = sKT[:, 8 * tok: 8 * tok + 8192]
        else:
            sw4 = swl[:, 0:8192]
        p3t = k.token("tensor")
        p3v = k.token("vector")
        p3s = k.token("scalar")
        k.wait("gpsimd", p3t)
        k.wait("gpsimd", p3v)
        lw4 = k.dma("gpsimd", lambda E: E.dma_start(
            out=sw4.rearrange("p (n m) -> p n m", m=1024),
            in_=wout.rearrange("(n p) m -> p n m", p=128)), key="wk")
        k.wait("sync", p3t)
        k.wait("sync", p3v)
        lbo = k.dma("sync", lambda E: E.dma_start(out=sbo[:], in_=bob), key="bo")
        k.wait("tensor", atd)
        vT = [None] * ntt
        for tt in range(ntt):
            ttr = None
            for dt in range(8):
                ttr = k.emit("tensor", lambda E, tt=tt, dt=dt: E.transpose(
                    ps_t[tt % 2][:, dt * 128:(dt + 1) * 128],
                    sattn[:, tt * 1024 + dt * 128: tt * 1024 + (dt + 1) * 128], sib[:]))
            k.wait("vector", ttr)
            dst = sAT.rearrange("p (n m) -> p n m", m=tok)[:, :, tt * 128:(tt + 1) * 128]
            srcp = ps_t[tt % 2][:].rearrange("p (n m) -> p n m", m=128)
            vT[tt] = k.emit("vector", lambda E, dst=dst, srcp=srcp: E.tensor_copy(dst, srcp))
            if tt + 1 < ntt:
                k.wait("tensor", vT[tt])
        st_y = [None, None]
        pa_free = [None, None]
        for t in range(ntt):
            if t < 2:
                k.wait("sync", xldlast[t % 2])
                k.wait("sync", k.token("vector"))
            else:
                k.wait("sync", st_y[t % 2])
            ldx = k.dma("sync", lambda E, t=t: E.dma_start(out=sx[t % 2][:], in_=xo_t[t]), key=f"x{t % 2}")
            if t == 0:
                k.wait("tensor", lw4)
                k.wait("tensor", vT[ntt - 1])
            for half in range(2):
                k.wait("tensor", pa_free[half])
                tm = None
                for dt in range(8):
                    tm = k.emit("tensor", lambda E, dt=dt, half=half, t=t: E.matmul(
                        ps_a[half][:, 0:512],
                        sAT[:, dt * tok + t * 128: dt * tok + (t + 1) * 128],
                        sw4[:, dt * 1024 + half * 512: dt * 1024 + half * 512 + 512],
                        start=(dt == 0), stop=(dt == 7)))
                k.wait("vector", tm)
                k.wait("vector", ldx)
                if half == 0:
                    k.wait("vector", lbo)
                if t >= 1:
                    k.wait("vector", st_y[(t - 1) % 2])
                k.emit("vector", lambda E, half=half, t=t: E.tensor_add(
                    sy[:, half * 512:(half + 1) * 512], ps_a[half][:, 0:512],
                    sx[t % 2][:, half * 512:(half + 1) * 512]))
                pa_free[half] = k.emit("vector", lambda E, half=half: E.tensor_add(
                    sy[:, half * 512:(half + 1) * 512], sy[:, half * 512:(half + 1) * 512],
                    sbo[:, half * 512:(half + 1) * 512]))
            vy = k.token("vector")
            k.wait("sync", vy)
            st_y[t % 2] = k.dma("sync", lambda E, t=t: E.dma_start(out=y_t[t], in_=sy[:]), key=f"y{t % 2}")
        k.wait("sync", st_y[0])
        k.wait("sync", st_y[1])
        k.barrier()
        k.finalize()
    return nc



_DBG = None


def kernel(**inp):
    X = np.ascontiguousarray(inp["L0_attn_x"], dtype=np.float32).reshape(16384, D)
    ident = np.eye(128, dtype=np.float32)

    def tile_row(v):
        return np.ascontiguousarray(np.broadcast_to(np.asarray(v, np.float32), (128, D)))

    nc0 = build_prep()
    wg = np.ascontiguousarray(inp["L0_ffn_w_gate"], dtype=np.float32)
    wu = np.ascontiguousarray(inp["L0_ffn_w_up"], dtype=np.float32)
    wd = np.ascontiguousarray(inp["L0_ffn_w_down"], dtype=np.float32)
    cenT = np.ascontiguousarray(np.asarray(inp["L0_attn_spatial_centers"], np.float32).T)
    in_maps = []
    for c in range(NCORES):
        in_maps.append({
            "wg": np.ascontiguousarray(wg[:, c * 128:(c + 1) * 128]),
            "wu": np.ascontiguousarray(wu[:, c * 128:(c + 1) * 128]),
            "wd": np.ascontiguousarray(wd[c * 128:(c + 1) * 128, :]),
            "cenT": cenT,
            "identb": ident,
        })
    res = run_bass_kernel_spmd(nc0, in_maps, core_ids=list(range(NCORES)))
    wgqT = np.concatenate([res.results[c]["o_wgqT"] for c in range(NCORES)], axis=0)
    wuqT = np.concatenate([res.results[c]["o_wuqT"] for c in range(NCORES)], axis=0)
    wdqT = np.concatenate([res.results[c]["o_wdqT"].reshape(I_FFN, 128) for c in range(NCORES)], axis=1)
    Bmat = res.results[0]["o_B"].reshape(1024, 1024)

    Bpad = np.zeros((1152, SEQ), np.float32)
    Bpad[1:1025, 1:1025] = Bmat[:, 0:1024]

    nc_ffn0 = build_l0ffn()
    nc_attn0 = build_l0attn()
    nc_ffn1 = build_l1ffn()
    nc_attn1 = build_l1attn()

    def run_l0ffn(Xc):
        im = [{"x": np.ascontiguousarray(Xc[c * TOK:(c + 1) * TOK]),
               "wgqT": wgqT, "wuqT": wuqT, "wdqT": np.ascontiguousarray(wdqT),
               "identb": ident} for c in range(NCORES)]
        r = run_bass_kernel_spmd(nc_ffn0, im, core_ids=list(range(NCORES)))
        return np.concatenate([r.results[c]["y"] for c in range(NCORES)], axis=0)

    wqkv0T = np.ascontiguousarray(np.asarray(inp["L0_attn_W_qkv"], np.float32).T)
    wproj0T = np.ascontiguousarray(np.asarray(inp["L0_attn_W_proj"], np.float32).T)
    gb = tile_row(inp["L0_attn_ln_gamma"])
    bbb = tile_row(inp["L0_attn_ln_beta"])
    bpb = tile_row(inp["L0_attn_b_proj"])

    def run_l0attn(Xc):
        im = []
        for c in range(NCORES):
            b, hq = c // 2, c % 2
            Bq = np.zeros((4, 9, 128, 512), np.float32)
            for qc in range(4):
                for mt in range(9):
                    Bq[qc, mt] = Bpad[mt * 128:(mt + 1) * 128,
                                      hq * 2048 + qc * 512: hq * 2048 + (qc + 1) * 512]
            im.append({
                "x_own": np.ascontiguousarray(Xc[c * TOK:(c + 1) * TOK]),
                "x_full": np.ascontiguousarray(Xc[b * SEQ:(b + 1) * SEQ]),
                "wqkvT": wqkv0T, "wprojT": wproj0T,
                "gb": gb, "bbb": bbb, "bpb": bpb, "Bq": Bq, "identb": ident,
            })
        r = run_bass_kernel_spmd(nc_attn0, im, core_ids=list(range(NCORES)))
        return np.concatenate([r.results[c]["y"] for c in range(NCORES)], axis=0)

    wrT = np.ascontiguousarray(np.asarray(inp["L1_ffn_W_r"], np.float32).T)
    wkT = np.ascontiguousarray(np.asarray(inp["L1_ffn_W_k"], np.float32).T)
    wvT = np.ascontiguousarray(np.asarray(inp["L1_ffn_W_v"], np.float32).T)
    murT = np.ascontiguousarray(np.asarray(inp["L1_ffn_mu_r"], np.float32).reshape(8, 128).T)
    mukT = np.ascontiguousarray(np.asarray(inp["L1_ffn_mu_k"], np.float32).reshape(8, 128).T)

    def run_l1ffn(Xc):
        im = []
        for c in range(NCORES):
            xb = np.zeros((1, D), np.float32) if c % 2 == 0 else \
                np.ascontiguousarray(Xc[c * TOK - 1: c * TOK])
            im.append({"x": np.ascontiguousarray(Xc[c * TOK:(c + 1) * TOK]), "xb": xb,
                       "wrT": wrT, "wkT": wkT, "wvT": wvT,
                       "mur": murT, "muk": mukT, "identb": ident})
        r = run_bass_kernel_spmd(nc_ffn1, im, core_ids=list(range(NCORES)))
        return np.concatenate([r.results[c]["y"] for c in range(NCORES)], axis=0)

    wqkv1 = np.ascontiguousarray(inp["L1_attn_w_qkv"], dtype=np.float32)
    proj1 = np.asarray(inp["L1_attn_proj_matrix"], np.float32)
    projp = np.zeros((8, 128, 256), np.float32)
    for h in range(16):
        projp[h // 2, 64 * (h % 2):64 * (h % 2) + 64, :] = proj1[h]
    wout1 = np.ascontiguousarray(inp["L1_attn_w_out"], dtype=np.float32)
    bob = tile_row(inp["L1_attn_b_out"])

    def run_l1attn(Xc):
        im = []
        for c in range(NCORES):
            b = c // 2
            im.append({"x_own": np.ascontiguousarray(Xc[c * TOK:(c + 1) * TOK]),
                       "x_full": np.ascontiguousarray(Xc[b * SEQ:(b + 1) * SEQ]),
                       "wqkv": wqkv1, "projp": projp, "wout": wout1,
                       "bob": bob, "identb": ident})
        r = run_bass_kernel_spmd(nc_attn1, im, core_ids=list(range(NCORES)))
        return np.concatenate([r.results[c]["y"] for c in range(NCORES)], axis=0)

    X1 = run_l0ffn(X)
    X2 = run_l0attn(X1)
    X3 = run_l0ffn(X2)
    X4 = run_l1ffn(X3)
    X5 = run_l1attn(X4)
    X6 = run_l1ffn(X5)
    if _DBG is not None:
        _DBG.extend([X1, X2, X3, X4, X5, X6])
    return X6.reshape(4, SEQ, D)


if __name__ == "__main__":
    pass


# version 04
# speedup vs baseline: 1.1393x; 1.1393x over previous
import numpy as np
import ml_dtypes
from contextlib import ExitStack

import concourse.bass as bass
import concourse.mybir as mybir
from concourse.bass_utils import run_bass_kernel_spmd

F32 = mybir.dt.float32
BF16 = mybir.dt.bfloat16
I32 = mybir.dt.int32
U8 = mybir.dt.uint8
AF = mybir.ActivationFunctionType
ALU = mybir.AluOpType
AX = mybir.AxisListType

MAGIC = 12582912.0
NCORES = 8
D = 1024
I_FFN = 2816
TOK = 2048
NTT = TOK // 128
SEQ = 4096
E3 = 3072

ENGS = ("sync", "gpsimd", "vector", "scalar", "tensor")


class K:

    def __init__(self, nc, ctx):
        self.nc = nc
        self.ctx = ctx
        self.q = {e: [] for e in ENGS}
        self.csem = {e: ctx.enter_context(nc.semaphore(f"c_{e}")) for e in ENGS}
        self.dsem = {}
        self.ccnt = {e: 0 for e in ENGS}
        self.dcnt = {}

    def _dma_sem(self, eng, key):
        kk = (eng, key)
        if kk not in self.dsem:
            self.dsem[kk] = self.ctx.enter_context(
                self.nc.semaphore(f"d_{eng}_{key}"))
            self.dcnt[kk] = 0
        return kk

    def emit(self, eng, fn):
        sem = self.csem[eng]
        prev = self.ccnt[eng]
        self.ccnt[eng] += 1

        def wrapped(E, fn=fn, sem=sem, prev=prev):
            if prev > 0:
                E.wait_ge(sem, prev)
            fn(E).then_inc(sem, 1)
        self.q[eng].append(wrapped)
        return (sem, self.ccnt[eng])

    def dma(self, eng, fn, key="dflt"):
        kk = self._dma_sem(eng, key)
        self.dcnt[kk] += 16
        sem, val = self.dsem[kk], self.dcnt[kk]

        def wrapped(E, fn=fn, sem=sem):
            fn(E).then_inc(sem, 16)
        self.q[eng].append(wrapped)
        return (sem, val)

    def token(self, eng):
        return (self.csem[eng], self.ccnt[eng])

    def wait(self, eng, tok):
        if tok is None:
            return
        sem, val = tok
        if val <= 0:
            return
        self.q[eng].append(lambda E, s=sem, v=val: E.wait_ge(s, v))

    def barrier(self):
        toks = []
        for e in ENGS:
            toks.append(self.emit(e, lambda E: E.nop()))
        for kk, sem in self.dsem.items():
            if self.dcnt[kk] > 0:
                toks.append((sem, self.dcnt[kk]))
        for e in ENGS:
            for tok in toks:
                if tok[0] is not self.csem[e]:
                    self.wait(e, tok)

    def finalize(self):
        with self.nc.Block() as block:
            @block.sync
            def _(E):
                for f in self.q["sync"]:
                    f(E)

            @block.gpsimd
            def _(E):
                for f in self.q["gpsimd"]:
                    f(E)

            @block.vector
            def _(E):
                for f in self.q["vector"]:
                    f(E)

            @block.scalar
            def _(E):
                for f in self.q["scalar"]:
                    f(E)

            @block.tensor
            def _(E):
                for f in self.q["tensor"]:
                    f(E)


def emit_fp4_qd(k, x_ap, out_ap, scr, D_cols):
    nb = D_cols // 16
    v = "vector"
    bm, s, r = scr["bm"][:, 0:nb], scr["s"][:, 0:nb], scr["r"][:, 0:nb]
    t = scr["t"][:, 0:D_cols]
    u = scr["u"][:, 0:D_cols]
    w1 = scr["w1"][:, 0:D_cols]
    w2 = scr["w2"][:, 0:D_cols]
    dd = scr["dd"][:, 0:D_cols]
    ee = scr["ee"][:, 0:D_cols]
    x3 = x_ap.rearrange("p (b k) -> p b k", k=16)
    k.emit(v, lambda E: E.tensor_reduce(bm, x3, AX.X, ALU.max,
                                        apply_absolute_value=True))
    k.emit(v, lambda E: E.tensor_scalar(s, bm, 1.0 / 6.0, 1e-12,
                                        ALU.mult, ALU.max))
    k.emit(v, lambda E: E.reciprocal(r, s))
    rb = r.unsqueeze(2).broadcast_to([128, nb, 16])
    sb = s.unsqueeze(2).broadcast_to([128, nb, 16])
    t3 = t.rearrange("p (b k) -> p b k", k=16)
    k.emit(v, lambda E: E.tensor_tensor(t3, x3, rb, ALU.mult))
    k.emit(v, lambda E: E.tensor_mul(u, t, t))
    k.emit(v, lambda E: E.tensor_single_scalar(w1, u, 4.0, ALU.is_gt))
    k.emit(v, lambda E: E.tensor_single_scalar(w2, u, 16.0, ALU.is_gt))
    k.emit(v, lambda E: E.scalar_tensor_tensor(dd, w2, 0.5, w1, ALU.mult, ALU.add))
    k.emit(v, lambda E: E.scalar_tensor_tensor(ee, w1, 0.5, w2, ALU.mult, ALU.add))
    k.emit(v, lambda E: E.tensor_scalar(dd, dd, -1.0, 2.0, ALU.mult, ALU.add))
    k.emit(v, lambda E: E.tensor_scalar(ee, ee, 0.5, None, ALU.add))
    k.emit(v, lambda E: E.tensor_tensor(u, t, dd, ALU.mult))
    k.emit(v, lambda E: E.tensor_scalar(u, u, MAGIC, -MAGIC, ALU.add, ALU.add))
    k.emit(v, lambda E: E.tensor_tensor(u, u, ee, ALU.mult))
    u3 = u.rearrange("p (b k) -> p b k", k=16)
    o3 = out_ap.rearrange("p (b k) -> p b k", k=16)
    return k.emit(v, lambda E: E.tensor_tensor(o3, u3, sb, ALU.mult))


def emit_ln(k, x_ap, xn_ap, scr, eps=1e-5):
    DD = x_ap.shape[1]
    NP = x_ap.shape[0]
    st = scr["st"][0:NP, :]
    nch = (DD + 511) // 512
    for c in range(nch):
        cw = min(512, DD - c * 512)
        k.emit("vector", lambda E, c=c, cw=cw: E.bn_stats(
            st[:, 6 * c:6 * c + 6], x_ap[:, c * 512:c * 512 + cw]))
    k.emit("vector", lambda E: E.bn_aggr(st[:, 12:14], st[:, 0:6 * nch]))
    k.emit("vector", lambda E: E.tensor_scalar(st[:, 14:15], st[:, 13:14], eps, None, ALU.add))
    vr = k.emit("vector", lambda E: E.reciprocal(st[:, 15:16], st[:, 14:15]))
    k.wait("scalar", vr)
    srs = k.emit("scalar", lambda E: E.activation(st[:, 16:17], st[:, 15:16], AF.Sqrt))
    k.wait("vector", srs)
    vfin = k.emit("vector", lambda E: E.tensor_scalar(xn_ap, x_ap, st[:, 12:13], st[:, 16:17],
                                                      ALU.subtract, ALU.mult))
    return vfin



def build_prep(parts=("gu", "d", "b")):
    nc = bass.Bass("TRN2", target_bir_lowering=False)
    wg = nc.dram_tensor("wg", [I_FFN, 128], F32, kind="ExternalInput").ap()
    wu = nc.dram_tensor("wu", [I_FFN, 128], F32, kind="ExternalInput").ap()
    wd = nc.dram_tensor("wd", [128, I_FFN], F32, kind="ExternalInput").ap()
    cenT = nc.dram_tensor("cenT", [2, 1024], F32, kind="ExternalInput").ap()
    identb = nc.dram_tensor("identb", [128, 128], F32, kind="ExternalInput").ap()

    o_wgqT = nc.dram_tensor("o_wgqT", [128, I_FFN], BF16, kind="ExternalOutput").ap()
    o_wuqT = nc.dram_tensor("o_wuqT", [128, I_FFN], BF16, kind="ExternalOutput").ap()
    o_wdqT = nc.dram_tensor("o_wdqT", [22, 128, 128], BF16, kind="ExternalOutput").ap()
    o_B = nc.dram_tensor("o_B", [8, 128, 1024], F32, kind="ExternalOutput").ap()

    NTI = 22

    with ExitStack() as ctx:
        e = ctx.enter_context
        k = K(nc, ctx)
        sw = e(nc.sbuf_tensor("sw", [128, I_FFN], F32))
        swq = e(nc.sbuf_tensor("swq", [128, I_FFN], BF16))
        sib = e(nc.sbuf_tensor("sib", [128, 128], BF16))
        souT = e(nc.sbuf_tensor("souT", [128, I_FFN], BF16))
        scr = {nm: e(nc.sbuf_tensor(f"scr_{nm}", [128, I_FFN], F32))
               for nm in ("t", "u", "w1", "w2", "dd", "ee")}
        for nm in ("bm", "s", "r"):
            scr[nm] = e(nc.sbuf_tensor(f"scr_{nm}", [128, I_FFN // 16], F32))
        scr["sq"] = scr["t"]
        scen3 = e(nc.sbuf_tensor("scen3", [3, 1024], F32))
        scen23 = e(nc.sbuf_tensor("scen23", [3, 1024], F32))
        ssq2 = e(nc.sbuf_tensor("ssq2", [2, 1024], F32))
        sn2r = e(nc.sbuf_tensor("sn2r", [1, 1024], F32))
        sones2 = e(nc.sbuf_tensor("sones2", [2, 1], F32))
        sones1 = e(nc.sbuf_tensor("sones1", [1, 128], F32))
        sn2c = e(nc.sbuf_tensor("sn2c", [128, 8], F32))
        sdist = e(nc.sbuf_tensor("sdist", [128, 8 * 1024], F32))
        smax = e(nc.sbuf_tensor("smax", [128, 10], F32))
        sfac = e(nc.sbuf_tensor("sfac", [1, 4], F32))
        sfac128 = e(nc.sbuf_tensor("sfac128", [128, 1], F32))
        srow = e(nc.sbuf_tensor("srow", [1, 128], F32))
        sone1 = e(nc.sbuf_tensor("sone1", [1, 1], F32))
        ps_a = e(nc.psum_tensor("ps_a", [128, 512], F32))
        ps_b = e(nc.psum_tensor("ps_b", [128, 512], F32))
        ps_t0 = e(nc.psum_tensor("ps_t0", [128, 1024], BF16))
        ps_t1 = e(nc.psum_tensor("ps_t1", [128, 1024], BF16))
        dscr = nc.dram_tensor("dscr", [128], F32, kind="Internal").ap()

        c_id = k.dma("gpsimd", lambda E: E.dma_start(out=sib[:], in_=identb), key="id")

        if "gu" not in parts:
            wparts = []
        else:
            wparts = None
        wg_t = wg.rearrange("(n p) m -> n p m", p=128)
        wu_t = wu.rearrange("(n p) m -> n p m", p=128)
        for part, (src, outT) in enumerate([(wg_t, o_wgqT), (wu_t, o_wuqT)] if wparts is None else []):
            ld = [None] * NTI
            vq = [None] * NTI
            for i in range(NTI):
                if i >= 4:
                    k.wait("sync", vq[i - 4])
                ld[i] = k.dma("sync", lambda E, i=i, src=src: E.dma_start(
                    out=sw[:, i * 128:(i + 1) * 128], in_=src[i]), key=f"w{i % 4}")
                k.wait("vector", ld[i])
                sl = slice(i * 128, (i + 1) * 128)
                sub = {nm: scr[nm][:, 0:128] for nm in ("t", "u", "w1", "w2", "dd", "ee")}
                sub.update({nm: scr[nm][:, 0:8] for nm in ("bm", "s", "r")})
                vq[i] = emit_fp4_qd(k, sw[:, sl], swq[:, sl], sub, 128)
            for i in range(NTI):
                k.wait("tensor", vq[i])
                k.wait("tensor", c_id)
                pst = ps_t0 if i % 2 == 0 else ps_t1
                if i >= 2:
                    k.wait("tensor", k.token("vector"))
                tprev = k.emit("tensor", lambda E, i=i, pst=pst: E.transpose(
                    pst[:, 0:128], swq[:, i * 128:(i + 1) * 128], sib[:]))
                k.wait("vector", tprev)
                k.emit("vector", lambda E, i=i, pst=pst: E.tensor_copy(
                    souT[:, i * 128:(i + 1) * 128], pst[:, 0:128]))
            k.barrier()
            st = k.dma("sync", lambda E, outT=outT: E.dma_start(out=outT, in_=souT[:]), key="store")
            k.wait("sync", st)
            k.barrier()

        if "d" not in parts:
            k.barrier(); k.finalize(); return nc
        ldd = k.dma("sync", lambda E: E.dma_start(out=sw[:], in_=wd), key="wd")
        k.wait("vector", ldd)
        vq = emit_fp4_qd(k, sw[:], swq[:], scr, I_FFN)
        k.wait("tensor", vq)
        for i in range(NTI):
            pst = ps_t0 if i % 2 == 0 else ps_t1
            if i >= 2:
                k.wait("tensor", k.token("vector"))
            t = k.emit("tensor", lambda E, i=i, pst=pst: E.transpose(
                pst[:, 0:128], swq[:, i * 128:(i + 1) * 128], sib[:]))
            k.wait("vector", t)
            k.emit("vector", lambda E, i=i, pst=pst: E.tensor_copy(
                souT[:, i * 128:(i + 1) * 128], pst[:, 0:128]))
        k.barrier()
        souT_v = souT[:].rearrange("p (n m) -> p n m", m=128)
        st = k.dma("sync", lambda E: E.dma_start(out=o_wdqT.rearrange("n p m -> p n m"), in_=souT_v), key="store")
        k.wait("sync", st)
        k.barrier()

        if "b" not in parts:
            k.finalize(); return nc
        lc = k.dma("sync", lambda E: E.dma_start(out=scen3[0:2, :], in_=cenT), key="cen")
        k.wait("vector", lc)
        k.emit("vector", lambda E: E.memset(sones2[:], 1.0))
        k.emit("vector", lambda E: E.memset(sn2r[:], 1.0))
        vms = k.emit("vector", lambda E: E.memset(sones1[:], 1.0))
        k.wait("gpsimd", vms)
        gons = k.dma("gpsimd", lambda E: E.dma_start(out=scen23[2:3, :], in_=sn2r[:]), key="ones")
        k.emit("vector", lambda E: E.tensor_scalar(scen23[0:2, :], scen3[0:2, :], -2.0, None, ALU.mult))
        vsq = k.emit("vector", lambda E: E.tensor_mul(ssq2[:], scen3[0:2, :], scen3[0:2, :]))
        k.wait("tensor", vsq)
        k.emit("tensor", lambda E: E.matmul(ps_a[0:1, 0:512], sones2[:], ssq2[:, 0:512], start=True, stop=True))
        tn2 = k.emit("tensor", lambda E: E.matmul(ps_b[0:1, 0:512], sones2[:], ssq2[:, 512:1024], start=True, stop=True))
        k.wait("vector", tn2)
        k.wait("vector", gons)
        k.emit("vector", lambda E: E.tensor_copy(sn2r[0:1, 0:512], ps_a[0:1, 0:512]))
        vn2 = k.emit("vector", lambda E: E.tensor_copy(sn2r[0:1, 512:1024], ps_b[0:1, 0:512]))
        k.wait("gpsimd", vn2)
        gn2 = k.dma("gpsimd", lambda E: E.dma_start(out=scen3[2:3, :], in_=sn2r[:]), key="n2")
        k.wait("tensor", gn2)
        k.wait("tensor", vn2)
        for ti in range(8):
            k.emit("tensor", lambda E, ti=ti: E.matmul(
                ps_b[:, ti:ti + 1], sn2r[0:1, ti * 128:(ti + 1) * 128], sones1[0:1, 0:1],
                start=True, stop=True))
        k.wait("vector", k.token("tensor"))
        vn2c = k.emit("vector", lambda E: E.tensor_copy(sn2c[:], ps_b[:, 0:8]))
        vtile = vn2c
        for ti in range(8):
            sl = slice(ti * 128, (ti + 1) * 128)
            k.wait("tensor", vtile)
            k.emit("tensor", lambda E, sl=sl: E.matmul(ps_a[:, 0:512], scen23[:, sl], scen3[:, 0:512], start=True, stop=True))
            t2 = k.emit("tensor", lambda E, sl=sl: E.matmul(ps_b[:, 0:512], scen23[:, sl], scen3[:, 512:1024], start=True, stop=True))
            k.wait("vector", t2)
            d0 = ti * 1024
            k.emit("vector", lambda E, d0=d0, ti=ti: E.tensor_scalar(
                sdist[:, d0:d0 + 512], ps_a[:, 0:512], sn2c[:, ti:ti + 1], 0.0, ALU.add, ALU.max))
            vtile = k.emit("vector", lambda E, d0=d0, ti=ti: E.tensor_scalar(
                sdist[:, d0 + 512:d0 + 1024], ps_b[:, 0:512], sn2c[:, ti:ti + 1], 0.0, ALU.add, ALU.max))
            k.wait("scalar", vtile)
            k.emit("scalar", lambda E, d0=d0: E.activation(
                sdist[:, d0:d0 + 1024], sdist[:, d0:d0 + 1024], AF.Sqrt))
        k.barrier()
        k.emit("vector", lambda E: E.reduce_max(smax[:, 0:1], sdist[:], AX.X))
        k.barrier()
        st = k.dma("sync", lambda E: E.dma_start(out=dscr, in_=smax[:, 0:1]), key="dsw")
        k.wait("sync", st)
        lr = k.dma("sync", lambda E: E.dma_start(out=srow[:], in_=dscr.rearrange("(a b) -> a b", a=1)), key="dsr")
        k.wait("vector", lr)
        k.emit("vector", lambda E: E.reduce_max(sfac[:, 0:1], srow[:], AX.X))
        k.emit("vector", lambda E: E.reciprocal(sfac[:, 1:2], sfac[:, 0:1]))
        k.emit("vector", lambda E: E.tensor_scalar(sfac[:, 2:3], sfac[:, 0:1], 1e-6, None, ALU.is_gt))
        vf = k.emit("vector", lambda E: E.tensor_scalar(
            sfac[:, 1:2], sfac[:, 1:2], sfac[0:1, 2:3], -0.1, ALU.mult, ALU.mult))
        k.wait("tensor", vf)
        tf = k.emit("tensor", lambda E: E.matmul(ps_a[:, 0:1], sones1[0:1, :], sfac[0:1, 1:2], start=True, stop=True))
        k.wait("vector", tf)
        k.emit("vector", lambda E: E.tensor_copy(sfac128[:], ps_a[:, 0:1]))
        for ti in range(8):
            d0 = ti * 1024
            k.emit("vector", lambda E, d0=d0: E.tensor_scalar(
                sdist[:, d0:d0 + 1024], sdist[:, d0:d0 + 1024], sfac128[:, 0:1], None, ALU.mult))
        k.barrier()
        sdist_v = sdist[:].rearrange("p (n m) -> p n m", m=1024)
        st = k.dma("sync", lambda E: E.dma_start(out=o_B.rearrange("n p m -> p n m"), in_=sdist_v), key="store")
        k.wait("sync", st)
        k.barrier()
        k.finalize()
    return nc



FFN_CHUNKS = [(0, 512), (512, 512), (1024, 512), (1536, 512), (2048, 512), (2560, 256)]


def build_l0ffn(stage="full"):
    nc = bass.Bass("TRN2", target_bir_lowering=False)
    x_in = nc.dram_tensor("x", [TOK, D], F32, kind="ExternalInput").ap()
    wgqT = nc.dram_tensor("wgqT", [D, I_FFN], BF16, kind="ExternalInput").ap()
    wuqT = nc.dram_tensor("wuqT", [D, I_FFN], BF16, kind="ExternalInput").ap()
    wdqT = nc.dram_tensor("wdqT", [I_FFN, D], BF16, kind="ExternalInput").ap()
    identb = nc.dram_tensor("identb", [128, 128], F32, kind="ExternalInput").ap()
    y_out = nc.dram_tensor("y", [TOK, D], F32, kind="ExternalOutput").ap()

    x_t = x_in.rearrange("(n p) m -> n p m", p=128)
    y_t = y_out.rearrange("(n p) m -> n p m", p=128)

    with ExitStack() as ctx:
        e = ctx.enter_context
        k = K(nc, ctx)
        swg = e(nc.sbuf_tensor("swg", [128, 8 * I_FFN], BF16))
        swu = e(nc.sbuf_tensor("swu", [128, 8 * I_FFN], BF16))
        swd = e(nc.sbuf_tensor("swd", [128, 22 * D], BF16))
        sib = e(nc.sbuf_tensor("sib", [128, 128], BF16))
        sxqT = e(nc.sbuf_tensor("sxqT", [128, 8 * TOK], BF16))
        sx = [e(nc.sbuf_tensor(f"sx{i}", [128, D], F32)) for i in range(2)]
        sx2 = sx
        sxn = e(nc.sbuf_tensor("sxn", [128, D], F32))
        sq_bf = e(nc.sbuf_tensor("sq_bf", [128, D], BF16))
        scr = {nm: e(nc.sbuf_tensor(f"scr_{nm}", [128, D], F32))
               for nm in ("u", "w1", "w2", "dd", "ee")}
        scr["t"] = e(nc.sbuf_tensor("scr_t", [128, 1536], F32))
        scr["sq"] = scr["dd"]
        for nm in ("bm", "s", "r"):
            scr[nm] = e(nc.sbuf_tensor(f"scr_{nm}", [128, 64], F32))
        sst = e(nc.sbuf_tensor("sst", [128, 18], F32))
        scr["st"] = sst
        ssg = scr["w1"]
        sh = scr["ee"]
        shq = e(nc.sbuf_tensor("shq", [128, 512], BF16))
        shqT = e(nc.sbuf_tensor("shqT", [128, 512], BF16))
        sy = [sxn, scr["t"][:, 512:1536]]
        ps_g = [e(nc.psum_tensor(f"ps_g{i}", [128, 512], F32)) for i in range(2)]
        ps_u = [e(nc.psum_tensor(f"ps_u{i}", [128, 512], F32)) for i in range(2)]
        ps_o = [e(nc.psum_tensor(f"ps_o{i}", [128, 512], F32)) for i in range(2)]
        ps_t = [e(nc.psum_tensor(f"ps_t{i}", [128, 1024], BF16)) for i in range(2)]

        lwg = k.dma("sync", lambda E: E.dma_start(out=swg[:].rearrange("p (n m) -> p n m", m=I_FFN),
                                                  in_=wgqT.rearrange("(n p) m -> p n m", p=128)), key="wg")
        lwu = k.dma("sync", lambda E: E.dma_start(out=swu[:].rearrange("p (n m) -> p n m", m=I_FFN),
                                                  in_=wuqT.rearrange("(n p) m -> p n m", p=128)), key="wu")
        lwd = k.dma("sync", lambda E: E.dma_start(out=swd[:].rearrange("p (n m) -> p n m", m=D),
                                                  in_=wdqT.rearrange("(n p) m -> p n m", p=128)), key="wd")
        lid = k.dma("gpsimd", lambda E: E.dma_start(out=sib[:], in_=identb), key="id")

        vq = [None] * NTT
        vcp = [None] * NTT
        ld = [None] * NTT
        ttr = [None] * NTT
        for t in range(NTT):
            if t >= 2:
                k.wait("sync", vq[t - 2])
            ld[t] = k.dma("sync", lambda E, t=t: E.dma_start(out=sx[t % 2][:], in_=x_t[t]),
                          key=f"x{t % 2}")
            k.wait("vector", ld[t])
            k.wait("scalar", ld[t])
            if t >= 1:
                k.wait("scalar", vq[t - 1])
            emit_ln(k, sx[t % 2][:], sxn[:], scr)
            vq[t] = emit_fp4_qd(k, sxn[:], sq_bf[:], scr, D)
            k.wait("tensor", vq[t])
            if t == 0:
                k.wait("tensor", lid)
            if t >= 2:
                k.wait("tensor", vcp[t - 2])
            for dt in range(8):
                ttr[t] = k.emit("tensor", lambda E, t=t, dt=dt: E.transpose(
                    ps_t[t % 2][:, dt * 128:(dt + 1) * 128],
                    sq_bf[:, dt * 128:(dt + 1) * 128], sib[:]))
            k.wait("vector", ttr[t])
            dst = sxqT[:].rearrange("p (n m) -> p n m", m=TOK)[:, :, t * 128:(t + 1) * 128]
            src = ps_t[t % 2][:].rearrange("p (n m) -> p n m", m=128)
            vcp[t] = k.emit("vector", lambda E, dst=dst, src=src: E.tensor_copy(dst, src))

        if stage == "A":
            k.barrier()
            sA = k.dma("sync", lambda E: E.dma_start(out=y_t[0], in_=sx[0][:]), key="ya")
            k.wait("sync", sA)
            k.barrier()
            k.finalize()
            return nc
        k.wait("tensor", lwg)
        k.wait("tensor", lwu)
        k.wait("tensor", lwd)

        tg = [None, None]
        tu = [None, None]
        tdn = None
        vhq_prev = None
        vh_last = [None, None]
        ssil_last = [None, None]
        vhT_last = [None, None]
        vy = [None, None]
        st_y = [None, None]
        for t in range(NTT):
            if t < 2:
                k.wait("sync", vq[NTT - 2 + t])
            if t >= 2:
                k.wait("sync", vy[t % 2])
            ld2 = k.dma("sync", lambda E, t=t: E.dma_start(out=sx2[t % 2][:], in_=x_t[t]),
                        key=f"x2{t % 2}")
            for ci, (c0, cw) in enumerate(FFN_CHUNKS):
                pg, pu, pt = ps_g[ci % 2], ps_u[ci % 2], ps_t[ci % 2]
                if t == 0 and ci < 2:
                    k.wait("tensor", vcp[NTT - 1])
                k.wait("tensor", ssil_last[ci % 2])
                for dt in range(8):
                    lhs = sxqT[:, dt * TOK + t * 128: dt * TOK + (t + 1) * 128]
                    tg[ci % 2] = k.emit("tensor", lambda E, dt=dt, c0=c0, cw=cw, lhs=lhs, pg=pg: E.matmul(
                        pg[:, 0:cw], lhs, swg[:, dt * I_FFN + c0: dt * I_FFN + c0 + cw],
                        start=(dt == 0), stop=(dt == 7)))
                k.wait("tensor", vh_last[ci % 2])
                for dt in range(8):
                    lhs = sxqT[:, dt * TOK + t * 128: dt * TOK + (t + 1) * 128]
                    tu[ci % 2] = k.emit("tensor", lambda E, dt=dt, c0=c0, cw=cw, lhs=lhs, pu=pu: E.matmul(
                        pu[:, 0:cw], lhs, swu[:, dt * I_FFN + c0: dt * I_FFN + c0 + cw],
                        start=(dt == 0), stop=(dt == 7)))
                k.wait("scalar", tg[ci % 2])
                k.wait("scalar", vhq_prev)
                ssil = k.emit("scalar", lambda E, cw=cw, pg=pg: E.activation(
                    ssg[:, 0:cw], pg[:, 0:cw], AF.Silu))
                ssil_last[ci % 2] = ssil
                k.wait("vector", ssil)
                k.wait("vector", tu[ci % 2])
                vh = k.emit("vector", lambda E, cw=cw, pu=pu: E.tensor_tensor(
                    sh[:, 0:cw], ssg[:, 0:cw], pu[:, 0:cw], ALU.mult))
                vh_last[ci % 2] = vh
                sub = {nm: scr[nm][:, 0:cw] for nm in ("t", "u", "w1", "w2", "dd", "ee")}
                sub.update({nm: scr[nm][:, 0:cw // 16] for nm in ("bm", "s", "r")})
                vhq = emit_fp4_qd(k, sh[:, 0:cw], shq[:, 0:cw], sub, cw)
                vhq_prev = vhq
                if stage == "B1":
                    continue
                nt = cw // 128
                if stage == "B2x":
                    nt = cw // 128
                k.wait("tensor", vhq)
                k.wait("tensor", vhT_last[ci % 2])
                tt = None
                for it in range(nt):
                    tt = k.emit("tensor", lambda E, it=it, pt=pt: E.transpose(
                        pt[:, it * 128:(it + 1) * 128],
                        shq[:, it * 128:(it + 1) * 128], sib[:]))
                k.wait("vector", tt)
                vhT = k.emit("vector", lambda E, nt=nt, pt=pt: E.tensor_copy(
                    shqT[:, 0:nt * 128], pt[:, 0:nt * 128]))
                vhT_last[ci % 2] = vhT
                if stage == "B2":
                    continue
                k.wait("tensor", vhT)
                if ci == 0:
                    k.wait("tensor", vy[0])
                    k.wait("tensor", vy[1])
                last_ci = len(FFN_CHUNKS) - 1
                for it in range(nt):
                    iti = c0 // 128 + it
                    for half in range(2):
                        tdn = k.emit("tensor", lambda E, it=it, iti=iti, half=half, ci=ci, nt=nt, last_ci=last_ci: E.matmul(
                            ps_o[half][:, 0:512],
                            shqT[:, it * 128:(it + 1) * 128],
                            swd[:, iti * D + half * 512: iti * D + half * 512 + 512],
                            start=(ci == 0 and it == 0), stop=(ci == last_ci and it == nt - 1)))
            if stage in ("B1", "B2", "B3"):
                break
            k.wait("vector", tdn)
            k.wait("vector", ld2)
            if t >= 2:
                k.wait("vector", st_y[t % 2])
            for half in range(2):
                k.emit("vector", lambda E, half=half, t=t: E.tensor_scalar(
                    sy[t % 2][:, half * 512:(half + 1) * 512], ps_o[half][:, 0:512],
                    0.5, None, ALU.mult))
            for half in range(2):
                vy[t % 2] = k.emit("vector", lambda E, half=half, t=t: E.tensor_add(
                    sy[t % 2][:, half * 512:(half + 1) * 512],
                    sy[t % 2][:, half * 512:(half + 1) * 512],
                    sx2[t % 2][:, half * 512:(half + 1) * 512]))
            k.wait("sync", vy[t % 2])
            st_y[t % 2] = k.dma("sync", lambda E, t=t: E.dma_start(out=y_t[t], in_=sy[t % 2][:]),
                                key=f"y{t % 2}")
        if stage in ("B1", "B2", "B3"):
            k.barrier()
            sA = k.dma("sync", lambda E: E.dma_start(out=y_t[0], in_=sx[0][:]), key="ya")
            k.wait("sync", sA)
            k.barrier()
            k.finalize()
            return nc
        k.wait("sync", st_y[0])
        k.wait("sync", st_y[1])
        k.barrier()
        k.finalize()
    return nc



def build_l0attn(tok=TOK, seq=SEQ):
    H = 16
    HD = 64
    nmt = seq // 128
    nqc = tok // 512
    nmg = seq // 512
    nbias = min(nmt, 9)
    VW = H * 65

    nc = bass.Bass("TRN2", target_bir_lowering=False, dynamic_dma_scratch_size=8192)
    x_own = nc.dram_tensor("x_own", [tok, D], F32, kind="ExternalInput").ap()
    x_full = nc.dram_tensor("x_full", [seq, D], F32, kind="ExternalInput").ap()
    wqkvT = nc.dram_tensor("wqkvT", [D, E3], F32, kind="ExternalInput").ap()
    wprojT = nc.dram_tensor("wprojT", [D, D], F32, kind="ExternalInput").ap()
    gb = nc.dram_tensor("gb", [128, D], F32, kind="ExternalInput").ap()
    bbb = nc.dram_tensor("bbb", [128, D], F32, kind="ExternalInput").ap()
    bpb = nc.dram_tensor("bpb", [128, D], F32, kind="ExternalInput").ap()
    Bq = nc.dram_tensor("Bq", [nqc, nbias, 128, 512], F32, kind="ExternalInput").ap()
    identb = nc.dram_tensor("identb", [128, 128], F32, kind="ExternalInput").ap()
    y_out = nc.dram_tensor("y", [tok, D], F32, kind="ExternalOutput").ap()

    xf_t = x_full.rearrange("(n p) m -> n p m", p=128)
    xo_t = x_own.rearrange("(n p) m -> n p m", p=128)
    y_t = y_out.rearrange("(n p) m -> n p m", p=128)
    SCALE = 1.0 / 8.0

    with ExitStack() as ctx:
        e = ctx.enter_context
        k = K(nc, ctx)
        sKT = e(nc.sbuf_tensor("sKT", [128, 8 * seq], BF16))
        sV = e(nc.sbuf_tensor("sV", [128, nmt * VW], BF16))
        sQT = e(nc.sbuf_tensor("sQT", [128, 8 * tok], BF16))
        sAT = e(nc.sbuf_tensor("sAT", [128, 8 * tok], BF16))
        sxnTg = e(nc.sbuf_tensor("sxnTg", [128, max(8 * 512, nbias * 512)], BF16))
        sx = [e(nc.sbuf_tensor(f"sx{i}", [128, D], F32)) for i in range(2)]
        sxnb = e(nc.sbuf_tensor("sxnb", [128, D], BF16))
        sptx = e(nc.sbuf_tensor("sptx", [128, 1024], BF16))
        sst = e(nc.sbuf_tensor("sst", [128, 18], F32))
        sib = e(nc.sbuf_tensor("sib", [128, 128], BF16))
        sone = e(nc.sbuf_tensor("sone", [128, 1], F32))
        sorow = e(nc.sbuf_tensor("sorow", [1, 128], F32))
        sonesb = e(nc.sbuf_tensor("sonesb", [128, 128], BF16))
        skn2 = e(nc.sbuf_tensor("skn2", [128, 9], F32))
        cscr = nc.dram_tensor("cscr", [16, tok], F32, kind="Internal").ap()
        if seq == 4096:
            sw = sAT[:, 0:16384]
            ssq_big = sAT[:, 0:8192].bitcast(F32)
            sw4 = sKT[:, 0:8 * 1024]
            sgb = sKT[:, 8192:10240].bitcast(F32)
            sbbb = sKT[:, 10240:12288].bitcast(F32)
            sy = sKT[:, 12288:14336].bitcast(F32)
            sxn = sKT[:, 14336:16384].bitcast(F32)
            sxn2 = sKT[:, 16384:18432].bitcast(F32)
        else:
            sw = e(nc.sbuf_tensor("sw", [128, 16384], BF16))[:, 0:16384]
            ssq_big = e(nc.sbuf_tensor("ssqb", [128, 4096], F32))[:, 0:4096]
            sw4 = sw[:, 0:8 * 1024]
            sgb = e(nc.sbuf_tensor("sgbt", [128, D], F32))[:, :]
            sbbb = e(nc.sbuf_tensor("sbbbt", [128, D], F32))[:, :]
            sy = e(nc.sbuf_tensor("syt", [128, D], F32))[:, :]
            sxn = e(nc.sbuf_tensor("sxnt", [128, D], F32))[:, :]
            sxn2 = e(nc.sbuf_tensor("sxn2t", [128, D], F32))[:, :]
        sbias = sxnTg[:, 0:nbias * 512]
        szrow = sx[0][0:1, 512:1024]
        snrm = sx[1][0:1, 500:1020]
        ssc = sx[0][:, 0:512]
        sPt = [sx[1][:, 0:256].bitcast(BF16), sx[1][:, 256:512].bitcast(BF16)]
        szl = sx[1][0:64, 512:1024]
        szi = sx[1][64:128, 512:1024]
        ps_s = [e(nc.psum_tensor(f"ps_s{i}", [128, 512], F32)) for i in range(2)]
        ps_a = [e(nc.psum_tensor(f"ps_a{i}", [128, 512], F32)) for i in range(2)]
        ps_z = e(nc.psum_tensor("ps_z", [128, 512], F32))
        ps_t = [e(nc.psum_tensor(f"ps_t{i}", [128, 1024], BF16)) for i in range(2)]

        lid = k.dma("gpsimd", lambda E: E.dma_start(out=sib[:], in_=identb), key="id")

        xfree = [None, None]
        xldlast = [None, None]

        def ln_transpose_group(src_t, g, ld_prev_tok):
            vtok = None
            for j in range(4):
                t = 4 * g + j
                k.wait("sync", xfree[j % 2])
                ldx = k.dma("sync", lambda E, t=t, j=j, src_t=src_t: E.dma_start(
                    out=sx[j % 2][:], in_=xf_t[t] if src_t is xf_t else xo_t[t]),
                    key=f"x{j % 2}")
                xldlast[j % 2] = ldx
                k.wait("vector", ldx)
                k.wait("scalar", ldx)
                emit_ln(k, sx[j % 2][:], sxnb[:], {"st": sst})
                xfree[j % 2] = k.token("vector")
                vln = k.token("vector")
                k.wait("tensor", vln)
                if g == 0 and j == 0:
                    k.wait("tensor", lid)
                ttr = None
                for dt in range(8):
                    ttr = k.emit("tensor", lambda E, dt=dt, j=j: E.transpose(
                        ps_t[j % 2][:, dt * 128:(dt + 1) * 128],
                        sxnb[:, dt * 128:(dt + 1) * 128], sib[:]))
                k.wait("vector", ttr)
                dst = sxnTg[:, 0:4096].rearrange("p (n m) -> p n m", m=512)[:, :, j * 128:(j + 1) * 128]
                srcp = ps_t[j % 2][:].rearrange("p (n m) -> p n m", m=128)
                vtok = k.emit("vector", lambda E, dst=dst, srcp=srcp: E.tensor_copy(dst, srcp))
                if j < 3:
                    k.wait("tensor", vtok)
            return vtok

        lwk = k.dma("gpsimd", lambda E: E.dma_start(
            out=sw[:].rearrange("p (n m) -> p n m", m=2048),
            in_=wqkvT[:, 1024:3072].rearrange("(n p) m -> p n m", p=128)), key="wk")
        vsetones = k.emit("vector", lambda E: E.memset(
            sV[:].rearrange("p (n m) -> p n m", m=65)[:, :, 64:65], 1.0))
        prev_group_done = [None]
        cp_s = [None, None]
        cp_a = [None, None]
        for g in range(nmg):
            vg = ln_transpose_group(xf_t, g, None)
            k.wait("tensor", vg)
            k.wait("tensor", lwk)
            for et in range(8):
                k.wait("tensor", cp_s[et % 2])
                tm = None
                for dt in range(8):
                    tm = k.emit("tensor", lambda E, et=et, dt=dt: E.matmul(
                        ps_s[et % 2][:, 0:512],
                        sw[:, dt * 2048 + et * 128: dt * 2048 + et * 128 + 128],
                        sxnTg[:, dt * 512: dt * 512 + 512],
                        start=(dt == 0), stop=(dt == 7)))
                k.wait("vector", tm)
                vev = k.emit("vector", lambda E, et=et, g=g: E.tensor_copy(
                    sKT[:, et * seq + g * 512: et * seq + g * 512 + 512], ps_s[et % 2][:, 0:512]))
                cp_s[et % 2] = vev
            for j in range(4):
                t = 4 * g + j
                for ch in range(2):
                    k.wait("tensor", cp_a[ch])
                    tm = None
                    for dt in range(8):
                        tm = k.emit("tensor", lambda E, j=j, ch=ch, dt=dt: E.matmul(
                            ps_a[ch][:, 0:512],
                            sxnTg[:, dt * 512 + j * 128: dt * 512 + j * 128 + 128],
                            sw[:, dt * 2048 + 1024 + ch * 512: dt * 2048 + 1024 + ch * 512 + 512],
                            start=(dt == 0), stop=(dt == 7)))
                    k.wait("vector", tm)
                    dstv = sV[:].rearrange("p (n m) -> p n m", m=65)[
                        :, t * 16 + ch * 8:t * 16 + (ch + 1) * 8, 0:64]
                    srcv = ps_a[ch][:, 0:512].rearrange("p (n m) -> p n m", m=64)
                    vev = k.emit("vector", lambda E, dstv=dstv, srcv=srcv: E.tensor_copy(dstv, srcv))
                    cp_a[ch] = vev
            prev_group_done[0] = k.token("vector")

        k.wait("gpsimd", prev_group_done[0])
        k.wait("gpsimd", lwk)
        lwq = k.dma("gpsimd", lambda E: E.dma_start(
            out=sw[:, 0:8 * 1024].rearrange("p (n m) -> p n m", m=1024),
            in_=wqkvT[:, 0:1024].rearrange("(n p) m -> p n m", p=128)), key="wk")
        for g in range(tok // 512):
            vg = ln_transpose_group(xo_t, g, None)
            k.wait("tensor", vg)
            k.wait("tensor", lwq)
            for et in range(8):
                k.wait("tensor", cp_s[et % 2])
                tm = None
                for dt in range(8):
                    tm = k.emit("tensor", lambda E, et=et, dt=dt: E.matmul(
                        ps_s[et % 2][:, 0:512],
                        sw[:, dt * 1024 + et * 128: dt * 1024 + et * 128 + 128],
                        sxnTg[:, dt * 512: dt * 512 + 512],
                        start=(dt == 0), stop=(dt == 7)))
                k.wait("vector", tm)
                vev = k.emit("vector", lambda E, et=et, g=g: E.tensor_copy(
                    sQT[:, et * tok + g * 512: et * tok + g * 512 + 512], ps_s[et % 2][:, 0:512]))
                cp_s[et % 2] = vev

        k.wait("scalar", k.token("vector"))
        k.emit("vector", lambda E: E.memset(sone[:], 1.0))
        k.emit("vector", lambda E: E.memset(sorow[:], 1.0))
        k.emit("vector", lambda E: E.memset(sonesb[:], 1.0))
        k.emit("vector", lambda E: E.memset(skn2[:], 0.0))
        k.wait("tensor", k.token("vector"))
        ssq_k = ssq_big[:, 0:seq]
        last_tm = [None]
        for et in range(8):
            k.wait("scalar", last_tm[0])
            sq_t = k.emit("scalar", lambda E, et=et: E.activation(
                ssq_k, sKT[:, et * seq:(et + 1) * seq], AF.Square))
            k.wait("tensor", sq_t)
            for ch in range(seq // 512):
                for hb, base in ((0, 0), (1, 64)):
                    tm = k.emit("tensor", lambda E, ch=ch, base=base: E.matmul(
                        ps_z[base:base + 1, 0:512], sone[base:base + 64, 0:1],
                        ssq_big[base:base + 64, ch * 512:(ch + 1) * 512],
                        start=True, stop=True))
                    last_tm[0] = tm
                    k.wait("vector", tm)
                    k.emit("vector", lambda E, base=base: E.reduce_max(
                        snrm[0:1, 0:1] if base == 0 else None, None, None)) if False else None
                    k.emit("vector", lambda E, base=base, et=et: E.reduce_max(
                        skn2[base:base + 1, 8:9], ps_z[base:base + 1, 0:512], AX.X))
                    vm = k.emit("vector", lambda E, base=base, et=et: E.tensor_max(
                        skn2[base:base + 1, et:et + 1], skn2[base:base + 1, et:et + 1],
                        skn2[base:base + 1, 8:9]))
                    k.wait("tensor", vm)
        ssq_q = ssq_big[:, 0:tok]
        crow = sAT[:, 2 * tok: 2 * tok + 2 * tok].bitcast(F32)
        st_c = [None]
        for et in range(8):
            k.wait("scalar", last_tm[0])
            k.wait("scalar", k.token("vector"))
            sq_t = k.emit("scalar", lambda E, et=et: E.activation(
                ssq_q, sQT[:, et * tok:(et + 1) * tok], AF.Square))
            k.wait("tensor", sq_t)
            k.wait("tensor", st_c[0])
            for qc in range(tok // 512):
                for hb, base in ((0, 0), (1, 64)):
                    k.wait("tensor", k.token("vector"))
                    tm = k.emit("tensor", lambda E, qc=qc, base=base: E.matmul(
                        ps_z[base:base + 1, 0:512], sone[base:base + 64, 0:1],
                        ssq_q[base:base + 64, qc * 512:(qc + 1) * 512],
                        start=True, stop=True))
                    last_tm[0] = tm
                    k.wait("vector", tm)
                    vv = k.emit("vector", lambda E, qc=qc, base=base, et=et: E.tensor_scalar(
                        crow[base:base + 1, qc * 512:(qc + 1) * 512],
                        ps_z[base:base + 1, 0:512], skn2[base:base + 1, et:et + 1],
                        None, ALU.mult))
                    k.wait("scalar", vv)
                    sv = k.emit("scalar", lambda E, qc=qc, base=base: E.activation(
                        crow[base:base + 1, qc * 512:(qc + 1) * 512],
                        crow[base:base + 1, qc * 512:(qc + 1) * 512], AF.Sqrt))
                    k.wait("vector", sv)
                    k.emit("vector", lambda E, qc=qc, base=base: E.tensor_scalar(
                        crow[base:base + 1, qc * 512:(qc + 1) * 512],
                        crow[base:base + 1, qc * 512:(qc + 1) * 512], -1.0, None, ALU.mult))
            vfin = k.token("vector")
            k.wait("sync", vfin)
            st_c[0] = k.dma("sync", lambda E, et=et: E.dma_start(
                out=cscr[2 * et:2 * et + 2, :],
                in_=crow[0:65:64, 0:tok]), key="cs")
        k.wait("gpsimd", st_c[0])
        p2v = k.token("vector")
        pv_free = [p2v, p2v]
        k.wait("gpsimd", k.token("tensor"))
        k.wait("vector", xldlast[0])
        k.wait("scalar", xldlast[1])
        k.wait("vector", xldlast[1])
        ps_sx = [ps_s[0][:, 0:512], ps_s[1][:, 0:512],
                 ps_t[0][:].bitcast(F32)[:, 0:512], ps_t[1][:].bitcast(F32)[:, 0:512]]
        sPt4 = [sPt[0], sPt[1], sptx[:, 0:512], sptx[:, 512:1024]]
        scnrbufs = [sxnb[:, 0:512], sxnb[:, 512:1024]]
        exp_ring = [p2v] * 4
        pv_ring = [p2v] * 4
        cn_tok = [None, None]
        cn_last_use = [p2v, p2v]
        NIT = 2 * nmt
        LOOK = 2

        def load_cn(qc, hp):
            b = hp % 2
            k.wait("gpsimd", cn_last_use[b])
            cn_tok[b] = k.dma("gpsimd", lambda E, qc=qc, hp=hp, b=b: E.dma_start(
                out=scnrbufs[b][0:65:64, 0:512],
                in_=cscr[2 * hp:2 * hp + 2, qc * 512:(qc + 1) * 512]), key=f"cn{b}")

        for qc in range(nqc):
            k.wait("gpsimd", k.token("vector"))
            lbq = k.dma("gpsimd", lambda E, qc=qc: E.dma_start(
                out=sbias[:].rearrange("p (n m) -> p n m", m=512),
                in_=Bq[qc]), key="bq")
            load_cn(qc, 0)
            for hp in range(8):
                b = hp % 2
                if hp < 7:
                    load_cn(qc, hp + 1)
                scnr = scnrbufs[b]
                k.wait("tensor", cn_tok[b])
                z_read = [None, None]
                sexp_list = [None] * NIT
                pv_list = [None] * NIT

                def emit_score(idx, qc=qc, hp=hp, b=b, scnr=scnr,
                               sexp_list=sexp_list, lbq=lbq):
                    mt, hi = idx // 2, idx % 2
                    st = idx % 4
                    h = 2 * hp + hi
                    base = 64 * (h % 2)
                    lhsK = sKT[base:base + 64,
                               (h // 2) * seq + mt * 128:(h // 2) * seq + (mt + 1) * 128]
                    rhsQ = sQT[base:base + 64,
                               (h // 2) * tok + qc * 512:(h // 2) * tok + (qc + 1) * 512]
                    k.wait("tensor", exp_ring[st])
                    k.emit("tensor", lambda E, lhsK=lhsK, rhsQ=rhsQ, st=st: E.matmul(
                        ps_sx[st], lhsK, rhsQ, start=True, stop=False))
                    tsc = k.emit("tensor", lambda E, st=st, base=base, scnr=scnr: E.matmul(
                        ps_sx[st], sonesb[base:base + 1, :],
                        scnr[base:base + 1, 0:512], start=False, stop=True))
                    cn_last_use[b] = tsc
                    if mt < nbias:
                        sb_ = ssc
                        k.wait("vector", tsc)
                        k.wait("vector", k.token("scalar"))
                        if idx == 0:
                            k.wait("vector", lbq)
                        k.emit("vector", lambda E, st=st, sb_=sb_: E.tensor_scalar(
                            sb_, ps_sx[st], SCALE, None, ALU.mult))
                        exp_ring[st] = k.token("vector")
                        k.emit("vector", lambda E, mt=mt, sb_=sb_: E.tensor_add(
                            sb_, sb_, sbias[:, mt * 512:(mt + 1) * 512]))
                        vv = k.token("vector")
                        k.wait("scalar", vv)
                        k.wait("scalar", pv_ring[st])
                        sexp = k.emit("scalar", lambda E, st=st, sb_=sb_: E.activation(
                            sPt4[st], sb_, AF.Exp, bias=0.0, scale=1.0))
                    else:
                        k.wait("scalar", tsc)
                        k.wait("scalar", pv_ring[st])
                        sexp = k.emit("scalar", lambda E, st=st: E.activation(
                            sPt4[st], ps_sx[st], AF.Exp, bias=0.0, scale=SCALE))
                        exp_ring[st] = sexp
                    sexp_list[idx] = sexp

                def emit_pv(idx, hp=hp, z_read=z_read,
                            sexp_list=sexp_list, pv_list=pv_list):
                    mt, hi = idx // 2, idx % 2
                    st = idx % 4
                    h = 2 * hp + hi
                    k.wait("tensor", sexp_list[idx])
                    if mt == 0:
                        k.wait("tensor", pv_free[hi])
                    lhsV = sV[:].rearrange("p (n m) -> p n m", m=65)[
                        :, mt * 16 + h: mt * 16 + h + 1, 0:65].squeeze(1)
                    tpv = k.emit("tensor", lambda E, lhsV=lhsV, st=st, hi=hi, mt=mt: E.matmul(
                        ps_a[hi][0:65, 0:512], lhsV, sPt4[st],
                        start=(mt == 0), stop=(mt == nmt - 1)))
                    pv_list[idx] = tpv
                    pv_ring[st] = tpv
                    if mt == nmt - 1:
                        z_read[hi] = tpv

                for idx in range(NIT):
                    emit_score(idx)
                    if idx >= LOOK:
                        emit_pv(idx - LOOK)
                for idx in range(NIT - LOOK, NIT):
                    emit_pv(idx)
                for hi in range(2):
                    h = 2 * hp + hi
                    k.wait("vector", z_read[hi])
                    vz = k.emit("vector", lambda E, hi=hi: E.tensor_copy(szrow[:], ps_a[hi][64:65, 0:512]))
                    k.wait("tensor", vz)
                    tz = k.emit("tensor", lambda E: E.matmul(
                        ps_z[0:64, 0:512], sorow[0:1, 0:64],
                        szrow[:], start=True, stop=True))
                    k.wait("scalar", tz)
                    sl1 = k.emit("scalar", lambda E: E.activation(szl[:], ps_z[0:64, 0:512], AF.Ln))
                    k.wait("scalar", sl1)
                    sl2 = k.emit("scalar", lambda E: E.activation(szi[:], szl[:], AF.Exp, scale=-1.0))
                    k.wait("vector", sl2)
                    base = 64 * (h % 2)
                    dsta = sAT[base:base + 64,
                               (h // 2) * tok + qc * 512:(h // 2) * tok + (qc + 1) * 512]
                    pv_free[hi] = k.emit("vector", lambda E, dsta=dsta, hi=hi: E.tensor_tensor(
                        dsta, ps_a[hi][0:64, 0:512], szi[:], ALU.mult))
                    k.wait("tensor", sl1)

        p3v = k.token("vector")
        p3s = k.token("scalar")
        p3t = k.token("tensor")
        k.wait("gpsimd", lwq)
        k.wait("gpsimd", p3v)
        k.wait("gpsimd", p3t)
        k.wait("gpsimd", p3s)
        lwp = k.dma("gpsimd", lambda E: E.dma_start(
            out=sw4.rearrange("p (n m) -> p n m", m=1024),
            in_=wprojT.rearrange("(n p) m -> p n m", p=128)), key="wk")
        k.wait("sync", p3t)
        k.wait("sync", p3v)
        lbp = k.dma("sync", lambda E: E.dma_start(out=sy[:], in_=bpb), key="bp")
        lgb = k.dma("sync", lambda E: E.dma_start(out=sgb[:], in_=gb), key="gb")
        lbb = k.dma("sync", lambda E: E.dma_start(out=sbbb[:], in_=bbb), key="bb")
        k.wait("vector", lbp)
        k.wait("vector", lbb)
        vbb = k.emit("vector", lambda E: E.tensor_add(sbbb[:], sbbb[:], sy[:]))
        st_y = [None, None]
        pa_free = [None, None]
        for t in range(tok // 128):
            if t < 2:
                k.wait("sync", p3v)
                k.wait("sync", p3s)
                k.wait("sync", p3t)
            ldx = k.dma("sync", lambda E, t=t: E.dma_start(out=sx[t % 2][:], in_=xo_t[t]), key=f"x{t % 2}")
            k.wait("vector", ldx)
            k.wait("scalar", ldx)
            if t >= 1:
                k.wait("vector", st_y[(t - 1) % 2])
                k.wait("scalar", st_y[(t - 1) % 2])
            emit_ln(k, sx[t % 2][:], sxn, {"st": sst})
            k.wait("scalar", k.token("vector"))
            emit_ln(k, sxn, sxn2, {"st": sst})
            if t == 0:
                k.wait("vector", lgb)
            k.emit("vector", lambda E: E.tensor_mul(sxn2[:], sxn2[:], sgb[:]))
            vgl = k.emit("vector", lambda E: E.tensor_add(sxn2[:], sxn2[:], sbbb[:]))
            if t == 0:
                k.wait("tensor", lwp)
                k.wait("tensor", k.token("vector"))
            for half in range(2):
                k.wait("tensor", pa_free[half])
                tm = None
                for dt in range(8):
                    tm = k.emit("tensor", lambda E, dt=dt, half=half, t=t: E.matmul(
                        ps_a[half][:, 0:512],
                        sAT[:, dt * tok + t * 128: dt * tok + (t + 1) * 128],
                        sw4[:, dt * 1024 + half * 512: dt * 1024 + half * 512 + 512],
                        start=(dt == 0), stop=(dt == 7)))
                k.wait("vector", tm)
                pa_free[half] = k.emit("vector", lambda E, half=half, t=t: E.tensor_add(
                    sxn[:, half * 512:(half + 1) * 512],
                    ps_a[half][:, 0:512], sx[t % 2][:, half * 512:(half + 1) * 512]))
            vy = k.emit("vector", lambda E: E.tensor_add(sxn[:], sxn[:], sxn2[:]))
            k.wait("sync", vy)
            st_y[t % 2] = k.dma("sync", lambda E, t=t: E.dma_start(out=y_t[t], in_=sxn[:]), key=f"y{t % 2}")
            k.wait("vector", st_y[t % 2]) if t >= 14 else None
        k.wait("sync", st_y[0])
        k.wait("sync", st_y[1])
        k.barrier()
        k.finalize()
    return nc



def build_l1ffn(tok=TOK):
    nc = bass.Bass("TRN2", target_bir_lowering=False)
    x_in = nc.dram_tensor("x", [tok, D], F32, kind="ExternalInput").ap()
    xb_in = nc.dram_tensor("xb", [1, D], F32, kind="ExternalInput").ap()
    wrT = nc.dram_tensor("wrT", [D, D], F32, kind="ExternalInput").ap()
    wkT = nc.dram_tensor("wkT", [D, D], F32, kind="ExternalInput").ap()
    wvT = nc.dram_tensor("wvT", [D, D], F32, kind="ExternalInput").ap()
    mur = nc.dram_tensor("mur", [128, 8], F32, kind="ExternalInput").ap()
    muk = nc.dram_tensor("muk", [128, 8], F32, kind="ExternalInput").ap()
    identb = nc.dram_tensor("identb", [128, 128], F32, kind="ExternalInput").ap()
    y_out = nc.dram_tensor("y", [tok, D], F32, kind="ExternalOutput").ap()
    x_t = x_in.rearrange("(n p) m -> n p m", p=128)
    y_t = y_out.rearrange("(n p) m -> n p m", p=128)
    ntt = tok // 128
    dscr = nc.dram_tensor("dscr", [D], F32, kind="Internal").ap()

    with ExitStack() as ctx:
        e = ctx.enter_context
        k = K(nc, ctx)
        sxnT = e(nc.sbuf_tensor("sxnT", [128, 8 * tok], BF16))
        sxrT = e(nc.sbuf_tensor("sxrT", [128, 8 * tok], BF16))
        sxkT = e(nc.sbuf_tensor("sxkT", [128, 8 * tok], BF16))
        skaT = e(nc.sbuf_tensor("skaT", [128, 8 * tok], BF16))
        swr = e(nc.sbuf_tensor("swr", [128, 8 * D], BF16))
        swk = e(nc.sbuf_tensor("swk", [128, 8 * D], BF16))
        swv = e(nc.sbuf_tensor("swv", [128, 8 * D], BF16))
        sx = [e(nc.sbuf_tensor(f"sx{i}", [128, D], F32)) for i in range(2)]
        sxnb = e(nc.sbuf_tensor("sxnb", [128, D], BF16))
        sst = e(nc.sbuf_tensor("sst", [128, 18], F32))
        sib = e(nc.sbuf_tensor("sib", [128, 128], BF16))
        smu = e(nc.sbuf_tensor("smu", [128, 16], F32))
        sxbc = e(nc.sbuf_tensor("sxbc", [128, 16], BF16))
        sxbrow = e(nc.sbuf_tensor("sxbrow", [1, D], F32))
        srl = e(nc.sbuf_tensor("srl", [128, 512], BF16))
        sr = e(nc.sbuf_tensor("sr", [128, D], F32))
        sv1 = e(nc.sbuf_tensor("sv1", [128, D], F32))
        sy = e(nc.sbuf_tensor("sy", [128, D], F32))
        ps_a = [e(nc.psum_tensor(f"ps_a{i}", [128, 512], F32)) for i in range(2)]
        ps_b = [e(nc.psum_tensor(f"ps_b{i}", [128, 512], F32)) for i in range(2)]
        ps_t = [e(nc.psum_tensor(f"ps_t{i}", [128, 1024], BF16)) for i in range(2)]

        lid = k.dma("gpsimd", lambda E: E.dma_start(out=sib[:], in_=identb), key="id")
        lmur = k.dma("sync", lambda E: E.dma_start(out=smu[:, 0:8], in_=mur), key="mu")
        lmuk = k.dma("sync", lambda E: E.dma_start(out=smu[:, 8:16], in_=muk), key="mu2")
        lwr = k.dma("gpsimd", lambda E: E.dma_start(
            out=swr[:].rearrange("p (n m) -> p n m", m=D),
            in_=wrT.rearrange("(n p) m -> p n m", p=128)), key="wr")
        lwk = k.dma("gpsimd", lambda E: E.dma_start(
            out=swk[:].rearrange("p (n m) -> p n m", m=D),
            in_=wkT.rearrange("(n p) m -> p n m", p=128)), key="wk2")
        lwv = k.dma("gpsimd", lambda E: E.dma_start(
            out=swv[:].rearrange("p (n m) -> p n m", m=D),
            in_=wvT.rearrange("(n p) m -> p n m", p=128)), key="wv")
        lxb = k.dma("sync", lambda E: E.dma_start(out=sxbrow[:], in_=xb_in), key="xb")

        vcp = [None] * ntt
        ld = [None] * ntt
        for t in range(ntt):
            if t >= 2:
                k.wait("sync", vcp[t - 2])
            ld[t] = k.dma("sync", lambda E, t=t: E.dma_start(out=sx[t % 2][:], in_=x_t[t]), key=f"x{t % 2}")
            k.wait("vector", ld[t])
            emit_ln(k, sx[t % 2][:], sxnb[:], {"st": sst})
            vln = k.token("vector")
            k.wait("tensor", vln)
            if t == 0:
                k.wait("tensor", lid)
            if t >= 2:
                k.wait("tensor", vcp[t - 2])
            ttr = None
            for dt in range(8):
                ttr = k.emit("tensor", lambda E, dt=dt, t=t: E.transpose(
                    ps_t[t % 2][:, dt * 128:(dt + 1) * 128],
                    sxnb[:, dt * 128:(dt + 1) * 128], sib[:]))
            k.wait("vector", ttr)
            dst = sxnT[:].rearrange("p (n m) -> p n m", m=tok)[:, :, t * 128:(t + 1) * 128]
            srcp = ps_t[t % 2][:].rearrange("p (n m) -> p n m", m=128)
            vcp[t] = k.emit("vector", lambda E, dst=dst, srcp=srcp: E.tensor_copy(dst, srcp))
        k.wait("vector", lxb)
        emit_ln(k, sxbrow[:], sxbrow[:], {"st": sst})
        vbr = k.token("vector")
        k.wait("sync", vbr)
        stb = k.dma("sync", lambda E: E.dma_start(out=dscr, in_=sxbrow[:]), key="xb2")
        k.wait("sync", stb)
        k.wait("gpsimd", stb)
        lxc = k.dma("gpsimd", lambda E: E.dma_start(
            out=sxbc[:, 0:8], in_=dscr.rearrange("(n p) -> p n", p=128),
            allow_slow_non_contiguous=True), key="xb3")

        k.wait("vector", lxc)
        k.wait("vector", lmur)
        k.wait("vector", lmuk)
        for which, dstT, mu0 in (("r", sxrT, 0), ("k", sxkT, 8)):
            for dt in range(8):
                xo = dt * tok
                k.emit("vector", lambda E, xo=xo, dstT=dstT: E.tensor_sub(
                    dstT[:, xo + 1: xo + tok], sxnT[:, xo + 1: xo + tok], sxnT[:, xo: xo + tok - 1]))
                k.emit("vector", lambda E, xo=xo, dstT=dstT, mu0=mu0, dt=dt: E.scalar_tensor_tensor(
                    dstT[:, xo + 1: xo + tok], dstT[:, xo + 1: xo + tok],
                    smu[:, mu0 + dt: mu0 + dt + 1], sxnT[:, xo: xo + tok - 1], ALU.mult, ALU.add))
                k.emit("vector", lambda E, xo=xo, dstT=dstT, dt=dt: E.tensor_sub(
                    dstT[:, xo: xo + 1], sxnT[:, xo: xo + 1], sxbc[:, dt: dt + 1]))
                k.emit("vector", lambda E, xo=xo, dstT=dstT, mu0=mu0, dt=dt: E.scalar_tensor_tensor(
                    dstT[:, xo: xo + 1], dstT[:, xo: xo + 1],
                    smu[:, mu0 + dt: mu0 + dt + 1], sxbc[:, dt: dt + 1], ALU.mult, ALU.add))
        vmix = k.token("vector")

        k.wait("tensor", vmix)
        k.wait("tensor", lwk)
        srl_free = [None, None]
        for kt in range(8):
            for ch in range(tok // 512):
                pb = ps_b[ch % 2]
                k.wait("tensor", srl_free[ch % 2])
                tm = None
                for ct in range(8):
                    tm = k.emit("tensor", lambda E, kt=kt, ch=ch, ct=ct, pb=pb: E.matmul(
                        pb[:, 0:512],
                        swk[:, ct * D + kt * 128: ct * D + kt * 128 + 128],
                        sxkT[:, ct * tok + ch * 512: ct * tok + (ch + 1) * 512],
                        start=(ct == 0), stop=(ct == 7)))
                k.wait("vector", tm)
                vrl = k.emit("vector", lambda E, pb=pb: E.tensor_scalar(
                    srl[:], pb[:, 0:512], 0.0, None, ALU.max))
                srl_free[ch % 2] = vrl
                k.wait("scalar", vrl)
                ssqk = k.emit("scalar", lambda E, kt=kt, ch=ch: E.activation(
                    skaT[:, kt * tok + ch * 512: kt * tok + (ch + 1) * 512], srl[:], AF.Square))
                k.wait("vector", ssqk)
                srl_free[ch % 2] = ssqk

        k.wait("tensor", lwr)
        k.wait("tensor", lwv)
        k.wait("tensor", k.token("scalar"))
        st_y = [None, None]
        pr_free = [None, None]
        pv_free = [None, None]
        for t in range(ntt):
            if t < 2:
                k.wait("sync", vcp[ntt - 2 + t])
            else:
                k.wait("sync", st_y[t % 2])
            ld2 = k.dma("sync", lambda E, t=t: E.dma_start(out=sx[t % 2][:], in_=x_t[t]), key=f"x{t % 2}")
            for half in range(2):
                k.wait("tensor", pr_free[half])
                tm = None
                for ct in range(8):
                    tm = k.emit("tensor", lambda E, ct=ct, half=half, t=t: E.matmul(
                        ps_a[half][:, 0:512],
                        sxrT[:, ct * tok + t * 128: ct * tok + (t + 1) * 128],
                        swr[:, ct * D + half * 512: ct * D + half * 512 + 512],
                        start=(ct == 0), stop=(ct == 7)))
                k.wait("scalar", tm)
                k.wait("scalar", pv_free[half])
                pr_free[half] = k.emit("scalar", lambda E, half=half: E.activation(
                    sr[:, half * 512:(half + 1) * 512], ps_a[half][:, 0:512], AF.Sigmoid))
            for half in range(2):
                k.wait("tensor", pv_free[half])
                tm = None
                for ct in range(8):
                    tm = k.emit("tensor", lambda E, ct=ct, half=half, t=t: E.matmul(
                        ps_b[half][:, 0:512],
                        skaT[:, ct * tok + t * 128: ct * tok + (t + 1) * 128],
                        swv[:, ct * D + half * 512: ct * D + half * 512 + 512],
                        start=(ct == 0), stop=(ct == 7)))
                k.wait("vector", tm)
                k.wait("vector", pr_free[half])
                pv_free[half] = k.emit("vector", lambda E, half=half: E.tensor_tensor(
                    sv1[:, half * 512:(half + 1) * 512], sr[:, half * 512:(half + 1) * 512],
                    ps_b[half][:, 0:512], ALU.mult))
            k.wait("vector", ld2)
            if t >= 1:
                k.wait("vector", st_y[(t - 1) % 2])
            vy = k.emit("vector", lambda E, t=t: E.scalar_tensor_tensor(
                sy[:], sv1[:], 0.5, sx[t % 2][:], ALU.mult, ALU.add))
            k.wait("sync", vy)
            st_y[t % 2] = k.dma("sync", lambda E, t=t: E.dma_start(out=y_t[t], in_=sy[:]), key=f"y{t % 2}")
            k.wait("vector", st_y[t % 2]) if False else None
        k.wait("sync", st_y[0])
        k.wait("sync", st_y[1])
        k.barrier()
        k.finalize()
    return nc



def build_l1attn(tok=TOK, seq=SEQ):
    H, HD, FEAT = 16, 64, 256
    nmt = seq // 128
    ntt = tok // 128
    VW = H * 65

    nc = bass.Bass("TRN2", target_bir_lowering=False, dynamic_dma_scratch_size=8192)
    x_own = nc.dram_tensor("x_own", [tok, D], F32, kind="ExternalInput").ap()
    x_full = nc.dram_tensor("x_full", [seq, D], F32, kind="ExternalInput").ap()
    wqkv = nc.dram_tensor("wqkv", [D, E3], F32, kind="ExternalInput").ap()
    projp = nc.dram_tensor("projp", [8, 128, FEAT], F32, kind="ExternalInput").ap()
    wout = nc.dram_tensor("wout", [D, D], F32, kind="ExternalInput").ap()
    bob = nc.dram_tensor("bob", [128, D], F32, kind="ExternalInput").ap()
    identb = nc.dram_tensor("identb", [128, 128], F32, kind="ExternalInput").ap()
    y_out = nc.dram_tensor("y", [tok, D], F32, kind="ExternalOutput").ap()

    xf_t = x_full.rearrange("(n p) m -> n p m", p=128)
    xo_t = x_own.rearrange("(n p) m -> n p m", p=128)
    y_t = y_out.rearrange("(n p) m -> n p m", p=128)

    with ExitStack() as ctx:
        e = ctx.enter_context
        k = K(nc, ctx)
        sKT = e(nc.sbuf_tensor("sKT", [128, 8 * seq], BF16))
        sV = e(nc.sbuf_tensor("sV", [128, nmt * VW], BF16))
        sQT = e(nc.sbuf_tensor("sQT", [128, 8 * tok], BF16))
        sxnTg = e(nc.sbuf_tensor("sxnTg", [128, 8 * 512], BF16))
        sx = [e(nc.sbuf_tensor(f"sx{i}", [128, D], F32)) for i in range(2)]
        sxnb = e(nc.sbuf_tensor("sxnb", [128, D], BF16))
        sst = e(nc.sbuf_tensor("sst", [128, 18], F32))
        sib = e(nc.sbuf_tensor("sib", [128, 128], BF16))
        sproj = e(nc.sbuf_tensor("sproj", [128, 8 * FEAT], BF16))
        skv = e(nc.sbuf_tensor("skv", [128, H * 130], BF16))
        skp = [e(nc.sbuf_tensor(f"skp{i}", [128, FEAT], BF16)) for i in range(2)]
        selu = sxnTg[:, 0:1024].bitcast(F32)
        selu2 = sxnTg[:, 1024:2048].bitcast(F32)
        szi = e(nc.sbuf_tensor("szi", [128, 1], F32))
        sy = e(nc.sbuf_tensor("sy", [128, D], F32))
        sbo = e(nc.sbuf_tensor("sbo", [128, D], F32))
        swl = e(nc.sbuf_tensor("swl", [128, 8 * 1024], BF16))
        sattn = None
        sqp = None
        ps_a = [e(nc.psum_tensor(f"ps_a{i}", [128, 512], F32)) for i in range(2)]
        ps_kv = [e(nc.psum_tensor(f"ps_kv{i}", [128, 512], F32)) for i in range(4)]
        ps_t = [e(nc.psum_tensor(f"ps_t{i}", [128, 1024], BF16)) for i in range(2)]

        lid = k.dma("gpsimd", lambda E: E.dma_start(out=sib[:], in_=identb), key="id")
        lpj = k.dma("gpsimd", lambda E: E.dma_start(
            out=sproj[:].rearrange("p (n m) -> p n m", m=FEAT), in_=projp), key="pj")

        xfree = [None, None]
        xldlast = [None, None]

        def ln_group(src_t, g):
            vtok = None
            for j in range(4):
                t = 4 * g + j
                k.wait("sync", xfree[j % 2])
                ldx = k.dma("sync", lambda E, t=t, j=j, src_t=src_t: E.dma_start(
                    out=sx[j % 2][:], in_=xf_t[t] if src_t is xf_t else xo_t[t]),
                    key=f"x{j % 2}")
                xldlast[j % 2] = ldx
                k.wait("vector", ldx)
                emit_ln(k, sx[j % 2][:], sxnb[:], {"st": sst})
                xfree[j % 2] = k.token("vector")
                vln = k.token("vector")
                k.wait("tensor", vln)
                if g == 0 and j == 0:
                    k.wait("tensor", lid)
                ttr = None
                for dt in range(8):
                    ttr = k.emit("tensor", lambda E, dt=dt, j=j: E.transpose(
                        ps_t[j % 2][:, dt * 128:(dt + 1) * 128],
                        sxnb[:, dt * 128:(dt + 1) * 128], sib[:]))
                k.wait("vector", ttr)
                dst = sxnTg[:, 0:4096].rearrange("p (n m) -> p n m", m=512)[:, :, j * 128:(j + 1) * 128]
                srcp = ps_t[j % 2][:].rearrange("p (n m) -> p n m", m=128)
                vtok = k.emit("vector", lambda E, dst=dst, srcp=srcp: E.tensor_copy(dst, srcp))
                if j < 3:
                    k.wait("tensor", vtok)
            return vtok

        lwk = k.dma("gpsimd", lambda E: E.dma_start(
            out=swl[:].rearrange("p (n m) -> p n m", m=1024),
            in_=wqkv[:, 1024:2048].rearrange("(n p) m -> p n m", p=128)), key="wk")
        vone = k.emit("vector", lambda E: E.memset(
            sV[:].rearrange("p (n m) -> p n m", m=65)[:, :, 64:65], 1.0))
        lwv_tok = [None]
        for g in range(seq // 512):
            vg = ln_group(xf_t, g)
            k.wait("tensor", vg)
            k.wait("tensor", lwk)
            for et in range(8):
                tm = None
                for dt in range(8):
                    tm = k.emit("tensor", lambda E, et=et, dt=dt: E.matmul(
                        ps_a[et % 2][:, 0:512],
                        swl[:, dt * 1024 + et * 128: dt * 1024 + et * 128 + 128],
                        sxnTg[:, dt * 512: dt * 512 + 512],
                        start=(dt == 0), stop=(dt == 7)))
                k.wait("vector", tm)
                vev = k.emit("vector", lambda E, et=et, g=g: E.tensor_copy(
                    sKT[:, et * seq + g * 512: et * seq + g * 512 + 512], ps_a[et % 2][:, 0:512]))
                k.wait("tensor", vev)
            prev = k.token("vector")
        k.wait("gpsimd", lwk)
        k.wait("gpsimd", prev)
        lwv = k.dma("gpsimd", lambda E: E.dma_start(
            out=swl[:].rearrange("p (n m) -> p n m", m=1024),
            in_=wqkv[:, 2048:3072].rearrange("(n p) m -> p n m", p=128)), key="wk")
        for g in range(seq // 512):
            vg = ln_group(xf_t, g)
            k.wait("tensor", vg)
            k.wait("tensor", lwv)
            k.wait("tensor", vone)
            for j in range(4):
                t = 4 * g + j
                for ch in range(2):
                    tm = None
                    for dt in range(8):
                        tm = k.emit("tensor", lambda E, j=j, ch=ch, dt=dt: E.matmul(
                            ps_a[ch][:, 0:512],
                            sxnTg[:, dt * 512 + j * 128: dt * 512 + j * 128 + 128],
                            swl[:, dt * 1024 + ch * 512: dt * 1024 + ch * 512 + 512],
                            start=(dt == 0), stop=(dt == 7)))
                    k.wait("vector", tm)
                    dstv = sV[:].rearrange("p (n m) -> p n m", m=65)[
                        :, t * 16 + ch * 8:t * 16 + (ch + 1) * 8, 0:64]
                    srcv = ps_a[ch][:, 0:512].rearrange("p (n m) -> p n m", m=64)
                    vev = k.emit("vector", lambda E, dstv=dstv, srcv=srcv: E.tensor_copy(dstv, srcv))
                    k.wait("tensor", vev)
            prev = k.token("vector")
        k.wait("gpsimd", lwv)
        k.wait("gpsimd", prev)
        lwq = k.dma("gpsimd", lambda E: E.dma_start(
            out=swl[:].rearrange("p (n m) -> p n m", m=1024),
            in_=wqkv[:, 0:1024].rearrange("(n p) m -> p n m", p=128)), key="wk")
        for g in range(tok // 512):
            vg = ln_group(xo_t, g)
            k.wait("tensor", vg)
            k.wait("tensor", lwq)
            for et in range(8):
                tm = None
                for dt in range(8):
                    tm = k.emit("tensor", lambda E, et=et, dt=dt: E.matmul(
                        ps_a[et % 2][:, 0:512],
                        swl[:, dt * 1024 + et * 128: dt * 1024 + et * 128 + 128],
                        sxnTg[:, dt * 512: dt * 512 + 512],
                        start=(dt == 0), stop=(dt == 7)))
                k.wait("vector", tm)
                vev = k.emit("vector", lambda E, et=et, g=g: E.tensor_copy(
                    sQT[:, et * tok + g * 512: et * tok + g * 512 + 512], ps_a[et % 2][:, 0:512]))
                k.wait("tensor", vev)

        k.wait("tensor", lpj)

        def emit_elu1(pin, pout_bf, width):
            v1 = k.emit("vector", lambda E: E.tensor_scalar(selu[:, 0:width], pin[:, 0:width], 0.0, None, ALU.min))
            k.wait("scalar", v1)
            s1 = k.emit("scalar", lambda E: E.activation(selu2[:, 0:width], selu[:, 0:width], AF.Exp))
            k.wait("vector", s1)
            v2 = k.emit("vector", lambda E: E.tensor_scalar(selu[:, 0:width], pin[:, 0:width], 0.0, None, ALU.max))
            v3 = k.emit("vector", lambda E: E.tensor_add(pout_bf, selu[:, 0:width], selu2[:, 0:width]))
            return v3

        kp_free = [None, None]
        kv_done = []
        for hp in range(8):
            kvp = [ps_kv[0], ps_kv[1], ps_kv[2], ps_kv[3]]
            for nt in range(nmt):
                for hi in range(2):
                    h = 2 * hp + hi
                    base = 64 * (h % 2)
                    lhsK = sKT[base:base + 64, (h // 2) * seq + nt * 128:(h // 2) * seq + (nt + 1) * 128]
                    rhsP = sproj[base:base + 64, hp * FEAT:(hp + 1) * FEAT]
                    k.wait("tensor", kp_free[hi])
                    tkp = k.emit("tensor", lambda E, lhsK=lhsK, rhsP=rhsP, hi=hi: E.matmul(
                        ps_a[hi][:, 0:FEAT], lhsK, rhsP, start=True, stop=True))
                    k.wait("vector", tkp)
                    velu = emit_elu1(ps_a[hi], skp[hi][:, 0:FEAT], FEAT)
                    kp_free[hi] = velu
                    k.wait("tensor", velu)
                    for ft in range(2):
                        tkv = k.emit("tensor", lambda E, hi=hi, ft=ft, nt=nt, h=h: E.matmul(
                            kvp[2 * hi + ft][:, 0:65],
                            skp[hi][:, ft * 128:(ft + 1) * 128],
                            sV[:, (nt * 16 + h) * 65:(nt * 16 + h) * 65 + 65],
                            start=(nt == 0), stop=(nt == nmt - 1)))
                        if nt == nmt - 1:
                            k.wait("vector", tkv)
            for hi in range(2):
                h = 2 * hp + hi
                for ft in range(2):
                    k.emit("vector", lambda E, hi=hi, ft=ft, h=h: E.tensor_copy(
                        skv[:, h * 130 + ft * 65: h * 130 + ft * 65 + 65],
                        kvp[2 * hi + ft][:, 0:65]))
            kv_done.append(k.token("vector"))
        kvd = k.token("vector")

        sattn = sV[:, 0:8192]
        sattn = sV[:, 0:ntt * 1024]
        sqp = sV[:, ntt * 1024: ntt * 1024 + 2 * tok]
        qp_free = None
        at_free = [None, None]
        at_done = []
        for hp in range(8):
            for hi in range(2):
                h = 2 * hp + hi
                base = 64 * (h % 2)
                for fc in range(2):
                    for qc in range(tok // 512):
                        rhsQ = sQT[base:base + 64, (h // 2) * tok + qc * 512:(h // 2) * tok + (qc + 1) * 512]
                        lhsP = sproj[base:base + 64, hp * FEAT + fc * 128: hp * FEAT + (fc + 1) * 128]
                        k.wait("tensor", qp_free)
                        tqp = k.emit("tensor", lambda E, lhsP=lhsP, rhsQ=rhsQ, fc=fc: E.matmul(
                            ps_a[fc][:, 0:512], lhsP, rhsQ, start=True, stop=True))
                        k.wait("vector", tqp)
                        dq = sqp.rearrange("p (n m) -> p n m", m=tok)[:, fc:fc + 1, qc * 512:(qc + 1) * 512].squeeze(1)
                        qp_free = emit_elu1(ps_a[fc], dq, 512)
                k.wait("tensor", qp_free)
                if hp == 0 and hi == 0:
                    k.wait("tensor", kvd)
                for tt in range(ntt):
                    k.wait("tensor", at_free[tt % 2])
                    tm = None
                    for ft in range(2):
                        tm = k.emit("tensor", lambda E, ft=ft, tt=tt, h=h: E.matmul(
                            ps_kv[2 + (tt % 2)][:, 0:65],
                            sqp[:, ft * tok + tt * 128: ft * tok + (tt + 1) * 128],
                            skv[:, h * 130 + ft * 65: h * 130 + ft * 65 + 65],
                            start=(ft == 0), stop=(ft == 1)))
                    k.wait("vector", tm)
                    pk = ps_kv[2 + (tt % 2)]
                    k.emit("vector", lambda E, pk=pk: E.tensor_scalar(szi[:], pk[:, 64:65], 1e-8, None, ALU.add))
                    k.emit("vector", lambda E: E.reciprocal(szi[:], szi[:]))
                    va = k.emit("vector", lambda E, pk=pk, tt=tt, h=h: E.tensor_scalar(
                        sattn[:, tt * 1024 + h * 64: tt * 1024 + (h + 1) * 64],
                        pk[:, 0:64], szi[:, 0:1], None, ALU.mult))
                    at_free[tt % 2] = va
                    at_done.append(va)
        atd = k.token("vector")

        sAT = sKT[:, 0:8 * tok]
        if 8 * tok + 8192 <= 8 * seq:
            sw4 = sKT[:, 8 * tok: 8 * tok + 8192]
        else:
            sw4 = swl[:, 0:8192]
        p3t = k.token("tensor")
        p3v = k.token("vector")
        p3s = k.token("scalar")
        k.wait("gpsimd", p3t)
        k.wait("gpsimd", p3v)
        lw4 = k.dma("gpsimd", lambda E: E.dma_start(
            out=sw4.rearrange("p (n m) -> p n m", m=1024),
            in_=wout.rearrange("(n p) m -> p n m", p=128)), key="wk")
        k.wait("sync", p3t)
        k.wait("sync", p3v)
        lbo = k.dma("sync", lambda E: E.dma_start(out=sbo[:], in_=bob), key="bo")
        k.wait("tensor", atd)
        vT = [None] * ntt
        for tt in range(ntt):
            ttr = None
            for dt in range(8):
                ttr = k.emit("tensor", lambda E, tt=tt, dt=dt: E.transpose(
                    ps_t[tt % 2][:, dt * 128:(dt + 1) * 128],
                    sattn[:, tt * 1024 + dt * 128: tt * 1024 + (dt + 1) * 128], sib[:]))
            k.wait("vector", ttr)
            dst = sAT.rearrange("p (n m) -> p n m", m=tok)[:, :, tt * 128:(tt + 1) * 128]
            srcp = ps_t[tt % 2][:].rearrange("p (n m) -> p n m", m=128)
            vT[tt] = k.emit("vector", lambda E, dst=dst, srcp=srcp: E.tensor_copy(dst, srcp))
            if tt + 1 < ntt:
                k.wait("tensor", vT[tt])
        st_y = [None, None]
        pa_free = [None, None]
        for t in range(ntt):
            if t < 2:
                k.wait("sync", xldlast[t % 2])
                k.wait("sync", k.token("vector"))
            else:
                k.wait("sync", st_y[t % 2])
            ldx = k.dma("sync", lambda E, t=t: E.dma_start(out=sx[t % 2][:], in_=xo_t[t]), key=f"x{t % 2}")
            if t == 0:
                k.wait("tensor", lw4)
                k.wait("tensor", vT[ntt - 1])
            for half in range(2):
                k.wait("tensor", pa_free[half])
                tm = None
                for dt in range(8):
                    tm = k.emit("tensor", lambda E, dt=dt, half=half, t=t: E.matmul(
                        ps_a[half][:, 0:512],
                        sAT[:, dt * tok + t * 128: dt * tok + (t + 1) * 128],
                        sw4[:, dt * 1024 + half * 512: dt * 1024 + half * 512 + 512],
                        start=(dt == 0), stop=(dt == 7)))
                k.wait("vector", tm)
                k.wait("vector", ldx)
                if half == 0:
                    k.wait("vector", lbo)
                if t >= 1:
                    k.wait("vector", st_y[(t - 1) % 2])
                k.emit("vector", lambda E, half=half, t=t: E.tensor_add(
                    sy[:, half * 512:(half + 1) * 512], ps_a[half][:, 0:512],
                    sx[t % 2][:, half * 512:(half + 1) * 512]))
                pa_free[half] = k.emit("vector", lambda E, half=half: E.tensor_add(
                    sy[:, half * 512:(half + 1) * 512], sy[:, half * 512:(half + 1) * 512],
                    sbo[:, half * 512:(half + 1) * 512]))
            vy = k.token("vector")
            k.wait("sync", vy)
            st_y[t % 2] = k.dma("sync", lambda E, t=t: E.dma_start(out=y_t[t], in_=sy[:]), key=f"y{t % 2}")
        k.wait("sync", st_y[0])
        k.wait("sync", st_y[1])
        k.barrier()
        k.finalize()
    return nc



_DBG = None


def kernel(**inp):
    X = np.ascontiguousarray(inp["L0_attn_x"], dtype=np.float32).reshape(16384, D)
    ident = np.eye(128, dtype=np.float32)

    def tile_row(v):
        return np.ascontiguousarray(np.broadcast_to(np.asarray(v, np.float32), (128, D)))

    nc0 = build_prep()
    wg = np.ascontiguousarray(inp["L0_ffn_w_gate"], dtype=np.float32)
    wu = np.ascontiguousarray(inp["L0_ffn_w_up"], dtype=np.float32)
    wd = np.ascontiguousarray(inp["L0_ffn_w_down"], dtype=np.float32)
    cenT = np.ascontiguousarray(np.asarray(inp["L0_attn_spatial_centers"], np.float32).T)
    in_maps = []
    for c in range(NCORES):
        in_maps.append({
            "wg": np.ascontiguousarray(wg[:, c * 128:(c + 1) * 128]),
            "wu": np.ascontiguousarray(wu[:, c * 128:(c + 1) * 128]),
            "wd": np.ascontiguousarray(wd[c * 128:(c + 1) * 128, :]),
            "cenT": cenT,
            "identb": ident,
        })
    res = run_bass_kernel_spmd(nc0, in_maps, core_ids=list(range(NCORES)))
    wgqT = np.concatenate([res.results[c]["o_wgqT"] for c in range(NCORES)], axis=0)
    wuqT = np.concatenate([res.results[c]["o_wuqT"] for c in range(NCORES)], axis=0)
    wdqT = np.concatenate([res.results[c]["o_wdqT"].reshape(I_FFN, 128) for c in range(NCORES)], axis=1)
    Bmat = res.results[0]["o_B"].reshape(1024, 1024)

    Bpad = np.zeros((1152, SEQ), np.float32)
    Bpad[1:1025, 1:1025] = Bmat[:, 0:1024]

    nc_ffn0 = build_l0ffn()
    nc_attn0 = build_l0attn()
    nc_ffn1 = build_l1ffn()
    nc_attn1 = build_l1attn()

    def run_l0ffn(Xc):
        im = [{"x": np.ascontiguousarray(Xc[c * TOK:(c + 1) * TOK]),
               "wgqT": wgqT, "wuqT": wuqT, "wdqT": np.ascontiguousarray(wdqT),
               "identb": ident} for c in range(NCORES)]
        r = run_bass_kernel_spmd(nc_ffn0, im, core_ids=list(range(NCORES)))
        return np.concatenate([r.results[c]["y"] for c in range(NCORES)], axis=0)

    wqkv0T = np.ascontiguousarray(np.asarray(inp["L0_attn_W_qkv"], np.float32).T)
    wproj0T = np.ascontiguousarray(np.asarray(inp["L0_attn_W_proj"], np.float32).T)
    gb = tile_row(inp["L0_attn_ln_gamma"])
    bbb = tile_row(inp["L0_attn_ln_beta"])
    bpb = tile_row(inp["L0_attn_b_proj"])

    def run_l0attn(Xc):
        im = []
        for c in range(NCORES):
            b, hq = c // 2, c % 2
            Bq = np.zeros((4, 9, 128, 512), np.float32)
            for qc in range(4):
                for mt in range(9):
                    Bq[qc, mt] = Bpad[mt * 128:(mt + 1) * 128,
                                      hq * 2048 + qc * 512: hq * 2048 + (qc + 1) * 512]
            im.append({
                "x_own": np.ascontiguousarray(Xc[c * TOK:(c + 1) * TOK]),
                "x_full": np.ascontiguousarray(Xc[b * SEQ:(b + 1) * SEQ]),
                "wqkvT": wqkv0T, "wprojT": wproj0T,
                "gb": gb, "bbb": bbb, "bpb": bpb, "Bq": Bq, "identb": ident,
            })
        r = run_bass_kernel_spmd(nc_attn0, im, core_ids=list(range(NCORES)))
        return np.concatenate([r.results[c]["y"] for c in range(NCORES)], axis=0)

    wrT = np.ascontiguousarray(np.asarray(inp["L1_ffn_W_r"], np.float32).T)
    wkT = np.ascontiguousarray(np.asarray(inp["L1_ffn_W_k"], np.float32).T)
    wvT = np.ascontiguousarray(np.asarray(inp["L1_ffn_W_v"], np.float32).T)
    murT = np.ascontiguousarray(np.asarray(inp["L1_ffn_mu_r"], np.float32).reshape(8, 128).T)
    mukT = np.ascontiguousarray(np.asarray(inp["L1_ffn_mu_k"], np.float32).reshape(8, 128).T)

    def run_l1ffn(Xc):
        im = []
        for c in range(NCORES):
            xb = np.zeros((1, D), np.float32) if c % 2 == 0 else \
                np.ascontiguousarray(Xc[c * TOK - 1: c * TOK])
            im.append({"x": np.ascontiguousarray(Xc[c * TOK:(c + 1) * TOK]), "xb": xb,
                       "wrT": wrT, "wkT": wkT, "wvT": wvT,
                       "mur": murT, "muk": mukT, "identb": ident})
        r = run_bass_kernel_spmd(nc_ffn1, im, core_ids=list(range(NCORES)))
        return np.concatenate([r.results[c]["y"] for c in range(NCORES)], axis=0)

    wqkv1 = np.ascontiguousarray(inp["L1_attn_w_qkv"], dtype=np.float32)
    proj1 = np.asarray(inp["L1_attn_proj_matrix"], np.float32)
    projp = np.zeros((8, 128, 256), np.float32)
    for h in range(16):
        projp[h // 2, 64 * (h % 2):64 * (h % 2) + 64, :] = proj1[h]
    wout1 = np.ascontiguousarray(inp["L1_attn_w_out"], dtype=np.float32)
    bob = tile_row(inp["L1_attn_b_out"])

    def run_l1attn(Xc):
        im = []
        for c in range(NCORES):
            b = c // 2
            im.append({"x_own": np.ascontiguousarray(Xc[c * TOK:(c + 1) * TOK]),
                       "x_full": np.ascontiguousarray(Xc[b * SEQ:(b + 1) * SEQ]),
                       "wqkv": wqkv1, "projp": projp, "wout": wout1,
                       "bob": bob, "identb": ident})
        r = run_bass_kernel_spmd(nc_attn1, im, core_ids=list(range(NCORES)))
        return np.concatenate([r.results[c]["y"] for c in range(NCORES)], axis=0)

    X1 = run_l0ffn(X)
    X2 = run_l0attn(X1)
    X3 = run_l0ffn(X2)
    X4 = run_l1ffn(X3)
    X5 = run_l1attn(X4)
    X6 = run_l1ffn(X5)
    if _DBG is not None:
        _DBG.extend([X1, X2, X3, X4, X5, X6])
    return X6.reshape(4, SEQ, D)


if __name__ == "__main__":
    pass


# version 06
# speedup vs baseline: 1.1845x; 1.0397x over previous
import numpy as np
import ml_dtypes
from contextlib import ExitStack

import concourse.bass as bass
import concourse.mybir as mybir
from concourse.bass_utils import run_bass_kernel_spmd

F32 = mybir.dt.float32
BF16 = mybir.dt.bfloat16
I32 = mybir.dt.int32
U8 = mybir.dt.uint8
AF = mybir.ActivationFunctionType
ALU = mybir.AluOpType
AX = mybir.AxisListType

MAGIC = 12582912.0
NCORES = 8
D = 1024
I_FFN = 2816
TOK = 2048
NTT = TOK // 128
SEQ = 4096
E3 = 3072

ENGS = ("sync", "gpsimd", "vector", "scalar", "tensor")


class K:

    def __init__(self, nc, ctx):
        self.nc = nc
        self.ctx = ctx
        self.q = {e: [] for e in ENGS}
        self.csem = {e: ctx.enter_context(nc.semaphore(f"c_{e}")) for e in ENGS}
        self.dsem = {}
        self.ccnt = {e: 0 for e in ENGS}
        self.dcnt = {}

    def _dma_sem(self, eng, key):
        kk = (eng, key)
        if kk not in self.dsem:
            self.dsem[kk] = self.ctx.enter_context(
                self.nc.semaphore(f"d_{eng}_{key}"))
            self.dcnt[kk] = 0
        return kk

    def emit(self, eng, fn):
        sem = self.csem[eng]
        prev = self.ccnt[eng]
        self.ccnt[eng] += 1

        def wrapped(E, fn=fn, sem=sem, prev=prev):
            if prev > 0:
                E.wait_ge(sem, prev)
            fn(E).then_inc(sem, 1)
        self.q[eng].append(wrapped)
        return (sem, self.ccnt[eng])

    def dma(self, eng, fn, key="dflt"):
        kk = self._dma_sem(eng, key)
        self.dcnt[kk] += 16
        sem, val = self.dsem[kk], self.dcnt[kk]

        def wrapped(E, fn=fn, sem=sem):
            fn(E).then_inc(sem, 16)
        self.q[eng].append(wrapped)
        return (sem, val)

    def token(self, eng):
        return (self.csem[eng], self.ccnt[eng])

    def wait(self, eng, tok):
        if tok is None:
            return
        sem, val = tok
        if val <= 0:
            return
        self.q[eng].append(lambda E, s=sem, v=val: E.wait_ge(s, v))

    def barrier(self):
        toks = []
        for e in ENGS:
            toks.append(self.emit(e, lambda E: E.nop()))
        for kk, sem in self.dsem.items():
            if self.dcnt[kk] > 0:
                toks.append((sem, self.dcnt[kk]))
        for e in ENGS:
            for tok in toks:
                if tok[0] is not self.csem[e]:
                    self.wait(e, tok)

    def finalize(self):
        with self.nc.Block() as block:
            @block.sync
            def _(E):
                for f in self.q["sync"]:
                    f(E)

            @block.gpsimd
            def _(E):
                for f in self.q["gpsimd"]:
                    f(E)

            @block.vector
            def _(E):
                for f in self.q["vector"]:
                    f(E)

            @block.scalar
            def _(E):
                for f in self.q["scalar"]:
                    f(E)

            @block.tensor
            def _(E):
                for f in self.q["tensor"]:
                    f(E)


def emit_fp4_qd(k, x_ap, out_ap, scr, D_cols):
    nb = D_cols // 16
    v = "vector"
    bm, s, r = scr["bm"][:, 0:nb], scr["s"][:, 0:nb], scr["r"][:, 0:nb]
    t = scr["t"][:, 0:D_cols]
    u = scr["u"][:, 0:D_cols]
    w1 = scr["w1"][:, 0:D_cols]
    w2 = scr["w2"][:, 0:D_cols]
    dd = scr["dd"][:, 0:D_cols]
    ee = scr["ee"][:, 0:D_cols]
    x3 = x_ap.rearrange("p (b k) -> p b k", k=16)
    k.emit(v, lambda E: E.tensor_reduce(bm, x3, AX.X, ALU.max,
                                        apply_absolute_value=True))
    k.emit(v, lambda E: E.tensor_scalar(s, bm, 1.0 / 6.0, 1e-12,
                                        ALU.mult, ALU.max))
    k.emit(v, lambda E: E.reciprocal(r, s))
    rb = r.unsqueeze(2).broadcast_to([128, nb, 16])
    sb = s.unsqueeze(2).broadcast_to([128, nb, 16])
    t3 = t.rearrange("p (b k) -> p b k", k=16)
    k.emit(v, lambda E: E.tensor_tensor(t3, x3, rb, ALU.mult))
    k.emit(v, lambda E: E.tensor_mul(u, t, t))
    k.emit(v, lambda E: E.tensor_single_scalar(w1, u, 4.0, ALU.is_gt))
    k.emit(v, lambda E: E.tensor_single_scalar(w2, u, 16.0, ALU.is_gt))
    k.emit(v, lambda E: E.scalar_tensor_tensor(dd, w2, 0.5, w1, ALU.mult, ALU.add))
    k.emit(v, lambda E: E.scalar_tensor_tensor(ee, w1, 0.5, w2, ALU.mult, ALU.add))
    k.emit(v, lambda E: E.tensor_scalar(dd, dd, -1.0, 2.0, ALU.mult, ALU.add))
    k.emit(v, lambda E: E.tensor_scalar(ee, ee, 0.5, None, ALU.add))
    k.emit(v, lambda E: E.tensor_tensor(u, t, dd, ALU.mult))
    k.emit(v, lambda E: E.tensor_scalar(u, u, MAGIC, -MAGIC, ALU.add, ALU.add))
    k.emit(v, lambda E: E.tensor_tensor(u, u, ee, ALU.mult))
    u3 = u.rearrange("p (b k) -> p b k", k=16)
    o3 = out_ap.rearrange("p (b k) -> p b k", k=16)
    return k.emit(v, lambda E: E.tensor_tensor(o3, u3, sb, ALU.mult))


def emit_ln(k, x_ap, xn_ap, scr, eps=1e-5):
    DD = x_ap.shape[1]
    NP = x_ap.shape[0]
    st = scr["st"][0:NP, :]
    nch = (DD + 511) // 512
    for c in range(nch):
        cw = min(512, DD - c * 512)
        k.emit("vector", lambda E, c=c, cw=cw: E.bn_stats(
            st[:, 6 * c:6 * c + 6], x_ap[:, c * 512:c * 512 + cw]))
    k.emit("vector", lambda E: E.bn_aggr(st[:, 12:14], st[:, 0:6 * nch]))
    k.emit("vector", lambda E: E.tensor_scalar(st[:, 14:15], st[:, 13:14], eps, None, ALU.add))
    vr = k.emit("vector", lambda E: E.reciprocal(st[:, 15:16], st[:, 14:15]))
    k.wait("scalar", vr)
    srs = k.emit("scalar", lambda E: E.activation(st[:, 16:17], st[:, 15:16], AF.Sqrt))
    k.wait("vector", srs)
    vfin = k.emit("vector", lambda E: E.tensor_scalar(xn_ap, x_ap, st[:, 12:13], st[:, 16:17],
                                                      ALU.subtract, ALU.mult))
    return vfin



def build_prep(parts=("gu", "d", "b")):
    nc = bass.Bass("TRN2", target_bir_lowering=False)
    wg = nc.dram_tensor("wg", [I_FFN, 128], F32, kind="ExternalInput").ap()
    wu = nc.dram_tensor("wu", [I_FFN, 128], F32, kind="ExternalInput").ap()
    wd = nc.dram_tensor("wd", [128, I_FFN], F32, kind="ExternalInput").ap()
    cenT = nc.dram_tensor("cenT", [2, 1024], F32, kind="ExternalInput").ap()
    identb = nc.dram_tensor("identb", [128, 128], F32, kind="ExternalInput").ap()

    o_wgqT = nc.dram_tensor("o_wgqT", [128, I_FFN], BF16, kind="ExternalOutput").ap()
    o_wuqT = nc.dram_tensor("o_wuqT", [128, I_FFN], BF16, kind="ExternalOutput").ap()
    o_wdqT = nc.dram_tensor("o_wdqT", [22, 128, 128], BF16, kind="ExternalOutput").ap()
    o_B = nc.dram_tensor("o_B", [8, 128, 1024], F32, kind="ExternalOutput").ap()

    NTI = 22

    with ExitStack() as ctx:
        e = ctx.enter_context
        k = K(nc, ctx)
        sw = e(nc.sbuf_tensor("sw", [128, I_FFN], F32))
        swq = e(nc.sbuf_tensor("swq", [128, I_FFN], BF16))
        sib = e(nc.sbuf_tensor("sib", [128, 128], BF16))
        souT = e(nc.sbuf_tensor("souT", [128, I_FFN], BF16))
        scr = {nm: e(nc.sbuf_tensor(f"scr_{nm}", [128, I_FFN], F32))
               for nm in ("t", "u", "w1", "w2", "dd", "ee")}
        for nm in ("bm", "s", "r"):
            scr[nm] = e(nc.sbuf_tensor(f"scr_{nm}", [128, I_FFN // 16], F32))
        scr["sq"] = scr["t"]
        scen3 = e(nc.sbuf_tensor("scen3", [3, 1024], F32))
        scen23 = e(nc.sbuf_tensor("scen23", [3, 1024], F32))
        ssq2 = e(nc.sbuf_tensor("ssq2", [2, 1024], F32))
        sn2r = e(nc.sbuf_tensor("sn2r", [1, 1024], F32))
        sones2 = e(nc.sbuf_tensor("sones2", [2, 1], F32))
        sones1 = e(nc.sbuf_tensor("sones1", [1, 128], F32))
        sn2c = e(nc.sbuf_tensor("sn2c", [128, 8], F32))
        sdist = e(nc.sbuf_tensor("sdist", [128, 8 * 1024], F32))
        smax = e(nc.sbuf_tensor("smax", [128, 10], F32))
        sfac = e(nc.sbuf_tensor("sfac", [1, 4], F32))
        sfac128 = e(nc.sbuf_tensor("sfac128", [128, 1], F32))
        srow = e(nc.sbuf_tensor("srow", [1, 128], F32))
        sone1 = e(nc.sbuf_tensor("sone1", [1, 1], F32))
        ps_a = e(nc.psum_tensor("ps_a", [128, 512], F32))
        ps_b = e(nc.psum_tensor("ps_b", [128, 512], F32))
        ps_t0 = e(nc.psum_tensor("ps_t0", [128, 1024], BF16))
        ps_t1 = e(nc.psum_tensor("ps_t1", [128, 1024], BF16))
        dscr = nc.dram_tensor("dscr", [128], F32, kind="Internal").ap()

        c_id = k.dma("gpsimd", lambda E: E.dma_start(out=sib[:], in_=identb), key="id")

        if "gu" not in parts:
            wparts = []
        else:
            wparts = None
        wg_t = wg.rearrange("(n p) m -> n p m", p=128)
        wu_t = wu.rearrange("(n p) m -> n p m", p=128)
        for part, (src, outT) in enumerate([(wg_t, o_wgqT), (wu_t, o_wuqT)] if wparts is None else []):
            ld = [None] * NTI
            vq = [None] * NTI
            for i in range(NTI):
                if i >= 4:
                    k.wait("sync", vq[i - 4])
                ld[i] = k.dma("sync", lambda E, i=i, src=src: E.dma_start(
                    out=sw[:, i * 128:(i + 1) * 128], in_=src[i]), key=f"w{i % 4}")
                k.wait("vector", ld[i])
                sl = slice(i * 128, (i + 1) * 128)
                sub = {nm: scr[nm][:, 0:128] for nm in ("t", "u", "w1", "w2", "dd", "ee")}
                sub.update({nm: scr[nm][:, 0:8] for nm in ("bm", "s", "r")})
                vq[i] = emit_fp4_qd(k, sw[:, sl], swq[:, sl], sub, 128)
            for i in range(NTI):
                k.wait("tensor", vq[i])
                k.wait("tensor", c_id)
                pst = ps_t0 if i % 2 == 0 else ps_t1
                if i >= 2:
                    k.wait("tensor", k.token("vector"))
                tprev = k.emit("tensor", lambda E, i=i, pst=pst: E.transpose(
                    pst[:, 0:128], swq[:, i * 128:(i + 1) * 128], sib[:]))
                k.wait("vector", tprev)
                k.emit("vector", lambda E, i=i, pst=pst: E.tensor_copy(
                    souT[:, i * 128:(i + 1) * 128], pst[:, 0:128]))
            k.barrier()
            st = k.dma("sync", lambda E, outT=outT: E.dma_start(out=outT, in_=souT[:]), key="store")
            k.wait("sync", st)
            k.barrier()

        if "d" not in parts:
            k.barrier(); k.finalize(); return nc
        ldd = k.dma("sync", lambda E: E.dma_start(out=sw[:], in_=wd), key="wd")
        k.wait("vector", ldd)
        vq = emit_fp4_qd(k, sw[:], swq[:], scr, I_FFN)
        k.wait("tensor", vq)
        for i in range(NTI):
            pst = ps_t0 if i % 2 == 0 else ps_t1
            if i >= 2:
                k.wait("tensor", k.token("vector"))
            t = k.emit("tensor", lambda E, i=i, pst=pst: E.transpose(
                pst[:, 0:128], swq[:, i * 128:(i + 1) * 128], sib[:]))
            k.wait("vector", t)
            k.emit("vector", lambda E, i=i, pst=pst: E.tensor_copy(
                souT[:, i * 128:(i + 1) * 128], pst[:, 0:128]))
        k.barrier()
        souT_v = souT[:].rearrange("p (n m) -> p n m", m=128)
        st = k.dma("sync", lambda E: E.dma_start(out=o_wdqT.rearrange("n p m -> p n m"), in_=souT_v), key="store")
        k.wait("sync", st)
        k.barrier()

        if "b" not in parts:
            k.finalize(); return nc
        lc = k.dma("sync", lambda E: E.dma_start(out=scen3[0:2, :], in_=cenT), key="cen")
        k.wait("vector", lc)
        k.emit("vector", lambda E: E.memset(sones2[:], 1.0))
        k.emit("vector", lambda E: E.memset(sn2r[:], 1.0))
        vms = k.emit("vector", lambda E: E.memset(sones1[:], 1.0))
        k.wait("gpsimd", vms)
        gons = k.dma("gpsimd", lambda E: E.dma_start(out=scen23[2:3, :], in_=sn2r[:]), key="ones")
        k.emit("vector", lambda E: E.tensor_scalar(scen23[0:2, :], scen3[0:2, :], -2.0, None, ALU.mult))
        vsq = k.emit("vector", lambda E: E.tensor_mul(ssq2[:], scen3[0:2, :], scen3[0:2, :]))
        k.wait("tensor", vsq)
        k.emit("tensor", lambda E: E.matmul(ps_a[0:1, 0:512], sones2[:], ssq2[:, 0:512], start=True, stop=True))
        tn2 = k.emit("tensor", lambda E: E.matmul(ps_b[0:1, 0:512], sones2[:], ssq2[:, 512:1024], start=True, stop=True))
        k.wait("vector", tn2)
        k.wait("vector", gons)
        k.emit("vector", lambda E: E.tensor_copy(sn2r[0:1, 0:512], ps_a[0:1, 0:512]))
        vn2 = k.emit("vector", lambda E: E.tensor_copy(sn2r[0:1, 512:1024], ps_b[0:1, 0:512]))
        k.wait("gpsimd", vn2)
        gn2 = k.dma("gpsimd", lambda E: E.dma_start(out=scen3[2:3, :], in_=sn2r[:]), key="n2")
        k.wait("tensor", gn2)
        k.wait("tensor", vn2)
        for ti in range(8):
            k.emit("tensor", lambda E, ti=ti: E.matmul(
                ps_b[:, ti:ti + 1], sn2r[0:1, ti * 128:(ti + 1) * 128], sones1[0:1, 0:1],
                start=True, stop=True))
        k.wait("vector", k.token("tensor"))
        vn2c = k.emit("vector", lambda E: E.tensor_copy(sn2c[:], ps_b[:, 0:8]))
        vtile = vn2c
        for ti in range(8):
            sl = slice(ti * 128, (ti + 1) * 128)
            k.wait("tensor", vtile)
            k.emit("tensor", lambda E, sl=sl: E.matmul(ps_a[:, 0:512], scen23[:, sl], scen3[:, 0:512], start=True, stop=True))
            t2 = k.emit("tensor", lambda E, sl=sl: E.matmul(ps_b[:, 0:512], scen23[:, sl], scen3[:, 512:1024], start=True, stop=True))
            k.wait("vector", t2)
            d0 = ti * 1024
            k.emit("vector", lambda E, d0=d0, ti=ti: E.tensor_scalar(
                sdist[:, d0:d0 + 512], ps_a[:, 0:512], sn2c[:, ti:ti + 1], 0.0, ALU.add, ALU.max))
            vtile = k.emit("vector", lambda E, d0=d0, ti=ti: E.tensor_scalar(
                sdist[:, d0 + 512:d0 + 1024], ps_b[:, 0:512], sn2c[:, ti:ti + 1], 0.0, ALU.add, ALU.max))
            k.wait("scalar", vtile)
            k.emit("scalar", lambda E, d0=d0: E.activation(
                sdist[:, d0:d0 + 1024], sdist[:, d0:d0 + 1024], AF.Sqrt))
        k.barrier()
        k.emit("vector", lambda E: E.reduce_max(smax[:, 0:1], sdist[:], AX.X))
        k.barrier()
        st = k.dma("sync", lambda E: E.dma_start(out=dscr, in_=smax[:, 0:1]), key="dsw")
        k.wait("sync", st)
        lr = k.dma("sync", lambda E: E.dma_start(out=srow[:], in_=dscr.rearrange("(a b) -> a b", a=1)), key="dsr")
        k.wait("vector", lr)
        k.emit("vector", lambda E: E.reduce_max(sfac[:, 0:1], srow[:], AX.X))
        k.emit("vector", lambda E: E.reciprocal(sfac[:, 1:2], sfac[:, 0:1]))
        k.emit("vector", lambda E: E.tensor_scalar(sfac[:, 2:3], sfac[:, 0:1], 1e-6, None, ALU.is_gt))
        vf = k.emit("vector", lambda E: E.tensor_scalar(
            sfac[:, 1:2], sfac[:, 1:2], sfac[0:1, 2:3], -0.1, ALU.mult, ALU.mult))
        k.wait("tensor", vf)
        tf = k.emit("tensor", lambda E: E.matmul(ps_a[:, 0:1], sones1[0:1, :], sfac[0:1, 1:2], start=True, stop=True))
        k.wait("vector", tf)
        k.emit("vector", lambda E: E.tensor_copy(sfac128[:], ps_a[:, 0:1]))
        for ti in range(8):
            d0 = ti * 1024
            k.emit("vector", lambda E, d0=d0: E.tensor_scalar(
                sdist[:, d0:d0 + 1024], sdist[:, d0:d0 + 1024], sfac128[:, 0:1], None, ALU.mult))
        k.barrier()
        sdist_v = sdist[:].rearrange("p (n m) -> p n m", m=1024)
        st = k.dma("sync", lambda E: E.dma_start(out=o_B.rearrange("n p m -> p n m"), in_=sdist_v), key="store")
        k.wait("sync", st)
        k.barrier()
        k.finalize()
    return nc



FFN_CHUNKS = [(0, 512), (512, 512), (1024, 512), (1536, 512), (2048, 512), (2560, 256)]


def build_l0ffn(stage="full"):
    nc = bass.Bass("TRN2", target_bir_lowering=False)
    x_in = nc.dram_tensor("x", [TOK, D], F32, kind="ExternalInput").ap()
    wgqT = nc.dram_tensor("wgqT", [D, I_FFN], BF16, kind="ExternalInput").ap()
    wuqT = nc.dram_tensor("wuqT", [D, I_FFN], BF16, kind="ExternalInput").ap()
    wdqT = nc.dram_tensor("wdqT", [I_FFN, D], BF16, kind="ExternalInput").ap()
    identb = nc.dram_tensor("identb", [128, 128], F32, kind="ExternalInput").ap()
    y_out = nc.dram_tensor("y", [TOK, D], F32, kind="ExternalOutput").ap()

    x_t = x_in.rearrange("(n p) m -> n p m", p=128)
    y_t = y_out.rearrange("(n p) m -> n p m", p=128)

    with ExitStack() as ctx:
        e = ctx.enter_context
        k = K(nc, ctx)
        swg = e(nc.sbuf_tensor("swg", [128, 8 * I_FFN], BF16))
        swu = e(nc.sbuf_tensor("swu", [128, 8 * I_FFN], BF16))
        swd = e(nc.sbuf_tensor("swd", [128, 22 * D], BF16))
        sib = e(nc.sbuf_tensor("sib", [128, 128], BF16))
        sxqT = e(nc.sbuf_tensor("sxqT", [128, 8 * TOK], BF16))
        sx = [e(nc.sbuf_tensor(f"sx{i}", [128, D], F32)) for i in range(2)]
        sx2 = sx
        sxn = e(nc.sbuf_tensor("sxn", [128, D], F32))
        sq_bf = e(nc.sbuf_tensor("sq_bf", [128, D], BF16))
        scr = {nm: e(nc.sbuf_tensor(f"scr_{nm}", [128, D], F32))
               for nm in ("u", "w1", "w2", "dd", "ee")}
        scr["t"] = e(nc.sbuf_tensor("scr_t", [128, 1536], F32))
        scr["sq"] = scr["dd"]
        for nm in ("bm", "s", "r"):
            scr[nm] = e(nc.sbuf_tensor(f"scr_{nm}", [128, 64], F32))
        sst = e(nc.sbuf_tensor("sst", [128, 18], F32))
        scr["st"] = sst
        ssg = scr["w1"]
        sh = scr["ee"]
        shqT = [e(nc.sbuf_tensor(f"shqT{i}", [128, 512], BF16)) for i in range(2)]
        shq = [sq_bf[:, 0:512], sq_bf[:, 512:1024]]
        sy = [sxn, scr["t"][:, 512:1536]]
        ps_g = [e(nc.psum_tensor(f"ps_g{i}", [128, 512], F32)) for i in range(2)]
        ps_u = [e(nc.psum_tensor(f"ps_u{i}", [128, 512], F32)) for i in range(2)]
        ps_o = [e(nc.psum_tensor(f"ps_o{i}", [128, 512], F32)) for i in range(2)]
        ps_t = [e(nc.psum_tensor(f"ps_t{i}", [128, 1024], BF16)) for i in range(2)]

        lwg = k.dma("sync", lambda E: E.dma_start(out=swg[:].rearrange("p (n m) -> p n m", m=I_FFN),
                                                  in_=wgqT.rearrange("(n p) m -> p n m", p=128)), key="wg")
        lwu = k.dma("sync", lambda E: E.dma_start(out=swu[:].rearrange("p (n m) -> p n m", m=I_FFN),
                                                  in_=wuqT.rearrange("(n p) m -> p n m", p=128)), key="wu")
        lwd = k.dma("sync", lambda E: E.dma_start(out=swd[:].rearrange("p (n m) -> p n m", m=D),
                                                  in_=wdqT.rearrange("(n p) m -> p n m", p=128)), key="wd")
        lid = k.dma("gpsimd", lambda E: E.dma_start(out=sib[:], in_=identb), key="id")

        vq = [None] * NTT
        vcp = [None] * NTT
        ld = [None] * NTT
        ttr = [None] * NTT
        for t in range(NTT):
            if t >= 2:
                k.wait("sync", vq[t - 2])
            ld[t] = k.dma("sync", lambda E, t=t: E.dma_start(out=sx[t % 2][:], in_=x_t[t]),
                          key=f"x{t % 2}")
            k.wait("vector", ld[t])
            k.wait("scalar", ld[t])
            if t >= 1:
                k.wait("scalar", vq[t - 1])
            emit_ln(k, sx[t % 2][:], sxn[:], scr)
            vq[t] = emit_fp4_qd(k, sxn[:], sq_bf[:], scr, D)
            k.wait("tensor", vq[t])
            if t == 0:
                k.wait("tensor", lid)
            if t >= 2:
                k.wait("tensor", vcp[t - 2])
            for dt in range(8):
                ttr[t] = k.emit("tensor", lambda E, t=t, dt=dt: E.transpose(
                    ps_t[t % 2][:, dt * 128:(dt + 1) * 128],
                    sq_bf[:, dt * 128:(dt + 1) * 128], sib[:]))
            k.wait("vector", ttr[t])
            dst = sxqT[:].rearrange("p (n m) -> p n m", m=TOK)[:, :, t * 128:(t + 1) * 128]
            src = ps_t[t % 2][:].rearrange("p (n m) -> p n m", m=128)
            vcp[t] = k.emit("vector", lambda E, dst=dst, src=src: E.tensor_copy(dst, src))

        if stage == "A":
            k.barrier()
            sA = k.dma("sync", lambda E: E.dma_start(out=y_t[0], in_=sx[0][:]), key="ya")
            k.wait("sync", sA)
            k.barrier()
            k.finalize()
            return nc
        k.wait("tensor", lwg)
        k.wait("tensor", lwu)
        k.wait("tensor", lwd)

        tg = [None, None]
        tu = [None, None]
        tdn_tok = [None]
        vhq_tok = [None, None]
        vhq_prev = None
        vh_last = [None, None]
        ssil_last = [None, None]
        vhT_last = [None, None]
        ttr_done = [None, None]
        vy = [None, None]
        st_y = [None, None]
        last_ci = len(FFN_CHUNKS) - 1

        def emit_transpose_down(ci, c0, cw):
            b = ci % 2
            pt = ps_t[b]
            nt = cw // 128
            k.wait("tensor", vhq_tok[b])
            k.wait("tensor", vhT_last[b])
            tt = None
            for it in range(nt):
                tt = k.emit("tensor", lambda E, it=it, pt=pt, b=b: E.transpose(
                    pt[:, it * 128:(it + 1) * 128],
                    shq[b][:, it * 128:(it + 1) * 128], sib[:]))
            ttr_done[b] = tt
            k.wait("vector", tt)
            vhT = k.emit("vector", lambda E, nt=nt, pt=pt, b=b: E.tensor_copy(
                shqT[b][:, 0:nt * 128], pt[:, 0:nt * 128]))
            vhT_last[b] = vhT
            k.wait("tensor", vhT)
            if ci == 0:
                k.wait("tensor", vy[0])
                k.wait("tensor", vy[1])
            for it in range(nt):
                iti = c0 // 128 + it
                for half in range(2):
                    tdn_tok[0] = k.emit("tensor", lambda E, it=it, iti=iti, half=half, ci=ci, nt=nt, b=b: E.matmul(
                        ps_o[half][:, 0:512],
                        shqT[b][:, it * 128:(it + 1) * 128],
                        swd[:, iti * D + half * 512: iti * D + half * 512 + 512],
                        start=(ci == 0 and it == 0), stop=(ci == last_ci and it == nt - 1)))

        for t in range(NTT):
            if t < 2:
                k.wait("sync", vq[NTT - 2 + t])
            if t >= 2:
                k.wait("sync", vy[t % 2])
            ld2 = k.dma("sync", lambda E, t=t: E.dma_start(out=sx2[t % 2][:], in_=x_t[t]),
                        key=f"x2{t % 2}")
            pend = []
            for ci, (c0, cw) in enumerate(FFN_CHUNKS):
                pg, pu = ps_g[ci % 2], ps_u[ci % 2]
                if t == 0 and ci < 2:
                    k.wait("tensor", vcp[NTT - 1])
                k.wait("tensor", ssil_last[ci % 2])
                for dt in range(8):
                    lhs = sxqT[:, dt * TOK + t * 128: dt * TOK + (t + 1) * 128]
                    tg[ci % 2] = k.emit("tensor", lambda E, dt=dt, c0=c0, cw=cw, lhs=lhs, pg=pg: E.matmul(
                        pg[:, 0:cw], lhs, swg[:, dt * I_FFN + c0: dt * I_FFN + c0 + cw],
                        start=(dt == 0), stop=(dt == 7)))
                k.wait("tensor", vh_last[ci % 2])
                for dt in range(8):
                    lhs = sxqT[:, dt * TOK + t * 128: dt * TOK + (t + 1) * 128]
                    tu[ci % 2] = k.emit("tensor", lambda E, dt=dt, c0=c0, cw=cw, lhs=lhs, pu=pu: E.matmul(
                        pu[:, 0:cw], lhs, swu[:, dt * I_FFN + c0: dt * I_FFN + c0 + cw],
                        start=(dt == 0), stop=(dt == 7)))
                k.wait("scalar", tg[ci % 2])
                k.wait("scalar", vhq_prev)
                ssil = k.emit("scalar", lambda E, cw=cw, pg=pg: E.activation(
                    ssg[:, 0:cw], pg[:, 0:cw], AF.Silu))
                ssil_last[ci % 2] = ssil
                k.wait("vector", ssil)
                k.wait("vector", tu[ci % 2])
                vh = k.emit("vector", lambda E, cw=cw, pu=pu: E.tensor_tensor(
                    sh[:, 0:cw], ssg[:, 0:cw], pu[:, 0:cw], ALU.mult))
                vh_last[ci % 2] = vh
                k.wait("vector", ttr_done[ci % 2])
                sub = {nm: scr[nm][:, 0:cw] for nm in ("t", "u", "w1", "w2", "dd", "ee")}
                sub.update({nm: scr[nm][:, 0:cw // 16] for nm in ("bm", "s", "r")})
                vhq = emit_fp4_qd(k, sh[:, 0:cw], shq[ci % 2][:, 0:cw], sub, cw)
                vhq_prev = vhq
                vhq_tok[ci % 2] = vhq
                if pend:
                    emit_transpose_down(*pend.pop(0))
                pend.append((ci, c0, cw))
            while pend:
                emit_transpose_down(*pend.pop(0))
            k.wait("vector", tdn_tok[0])
            k.wait("vector", ld2)
            if t >= 2:
                k.wait("vector", st_y[t % 2])
            for half in range(2):
                k.emit("vector", lambda E, half=half, t=t: E.tensor_scalar(
                    sy[t % 2][:, half * 512:(half + 1) * 512], ps_o[half][:, 0:512],
                    0.5, None, ALU.mult))
            for half in range(2):
                vy[t % 2] = k.emit("vector", lambda E, half=half, t=t: E.tensor_add(
                    sy[t % 2][:, half * 512:(half + 1) * 512],
                    sy[t % 2][:, half * 512:(half + 1) * 512],
                    sx2[t % 2][:, half * 512:(half + 1) * 512]))
            k.wait("sync", vy[t % 2])
            st_y[t % 2] = k.dma("sync", lambda E, t=t: E.dma_start(out=y_t[t], in_=sy[t % 2][:]),
                                key=f"y{t % 2}")
        if stage in ("B1", "B2", "B3"):
            k.barrier()
            sA = k.dma("sync", lambda E: E.dma_start(out=y_t[0], in_=sx[0][:]), key="ya")
            k.wait("sync", sA)
            k.barrier()
            k.finalize()
            return nc
        k.wait("sync", st_y[0])
        k.wait("sync", st_y[1])
        k.barrier()
        k.finalize()
    return nc



def build_l0attn(tok=TOK, seq=SEQ):
    H = 16
    HD = 64
    nmt = seq // 128
    nqc = tok // 512
    nmg = seq // 512
    nbias = min(nmt, 9)
    VW = H * 65

    nc = bass.Bass("TRN2", target_bir_lowering=False, dynamic_dma_scratch_size=8192)
    x_own = nc.dram_tensor("x_own", [tok, D], F32, kind="ExternalInput").ap()
    x_full = nc.dram_tensor("x_full", [seq, D], F32, kind="ExternalInput").ap()
    wqkvT = nc.dram_tensor("wqkvT", [D, E3], F32, kind="ExternalInput").ap()
    wprojT = nc.dram_tensor("wprojT", [D, D], F32, kind="ExternalInput").ap()
    gb = nc.dram_tensor("gb", [128, D], F32, kind="ExternalInput").ap()
    bbb = nc.dram_tensor("bbb", [128, D], F32, kind="ExternalInput").ap()
    bpb = nc.dram_tensor("bpb", [128, D], F32, kind="ExternalInput").ap()
    Bq = nc.dram_tensor("Bq", [nqc, nbias, 128, 512], F32, kind="ExternalInput").ap()
    identb = nc.dram_tensor("identb", [128, 128], F32, kind="ExternalInput").ap()
    y_out = nc.dram_tensor("y", [tok, D], F32, kind="ExternalOutput").ap()

    xf_t = x_full.rearrange("(n p) m -> n p m", p=128)
    xo_t = x_own.rearrange("(n p) m -> n p m", p=128)
    y_t = y_out.rearrange("(n p) m -> n p m", p=128)
    SCALE = 1.0 / 8.0

    with ExitStack() as ctx:
        e = ctx.enter_context
        k = K(nc, ctx)
        sKT = e(nc.sbuf_tensor("sKT", [128, 8 * seq], BF16))
        sV = e(nc.sbuf_tensor("sV", [128, nmt * VW], BF16))
        sQT = e(nc.sbuf_tensor("sQT", [128, 8 * tok], BF16))
        sAT = e(nc.sbuf_tensor("sAT", [128, 8 * tok], BF16))
        sxnTg = e(nc.sbuf_tensor("sxnTg", [128, max(8 * 512, nbias * 512)], BF16))
        sx = [e(nc.sbuf_tensor(f"sx{i}", [128, D], F32)) for i in range(2)]
        sxnb = e(nc.sbuf_tensor("sxnb", [128, D], BF16))
        sptx = e(nc.sbuf_tensor("sptx", [128, 1024], BF16))
        sst = e(nc.sbuf_tensor("sst", [128, 18], F32))
        sib = e(nc.sbuf_tensor("sib", [128, 128], BF16))
        sone = e(nc.sbuf_tensor("sone", [128, 1], F32))
        sorow = e(nc.sbuf_tensor("sorow", [1, 128], F32))
        sonesb = e(nc.sbuf_tensor("sonesb", [128, 128], BF16))
        skn2 = e(nc.sbuf_tensor("skn2", [128, 9], F32))
        cscr = nc.dram_tensor("cscr", [16, tok], F32, kind="Internal").ap()
        if seq == 4096:
            sw = sAT[:, 0:16384]
            ssq_big = sAT[:, 0:8192].bitcast(F32)
            sw4 = sKT[:, 0:8 * 1024]
            sgb = sKT[:, 8192:10240].bitcast(F32)
            sbbb = sKT[:, 10240:12288].bitcast(F32)
            sy = sKT[:, 12288:14336].bitcast(F32)
            sxn = sKT[:, 14336:16384].bitcast(F32)
            sxn2 = sKT[:, 16384:18432].bitcast(F32)
        else:
            sw = e(nc.sbuf_tensor("sw", [128, 16384], BF16))[:, 0:16384]
            ssq_big = e(nc.sbuf_tensor("ssqb", [128, 4096], F32))[:, 0:4096]
            sw4 = sw[:, 0:8 * 1024]
            sgb = e(nc.sbuf_tensor("sgbt", [128, D], F32))[:, :]
            sbbb = e(nc.sbuf_tensor("sbbbt", [128, D], F32))[:, :]
            sy = e(nc.sbuf_tensor("syt", [128, D], F32))[:, :]
            sxn = e(nc.sbuf_tensor("sxnt", [128, D], F32))[:, :]
            sxn2 = e(nc.sbuf_tensor("sxn2t", [128, D], F32))[:, :]
        sbias = sxnTg[:, 0:nbias * 512]
        szrow = sx[0][0:1, 512:1024]
        snrm = sx[1][0:1, 500:1020]
        ssc = sx[0][:, 0:512]
        sPt = [sx[1][:, 0:256].bitcast(BF16), sx[1][:, 256:512].bitcast(BF16)]
        szl = sx[1][0:64, 512:1024]
        szi = sx[1][64:128, 512:1024]
        ps_s = [e(nc.psum_tensor(f"ps_s{i}", [128, 512], F32)) for i in range(2)]
        ps_a = [e(nc.psum_tensor(f"ps_a{i}", [128, 512], F32)) for i in range(2)]
        ps_z = e(nc.psum_tensor("ps_z", [128, 512], F32))
        ps_t = [e(nc.psum_tensor(f"ps_t{i}", [128, 1024], BF16)) for i in range(2)]

        lid = k.dma("gpsimd", lambda E: E.dma_start(out=sib[:], in_=identb), key="id")

        xfree = [None, None]
        xldlast = [None, None]

        def ln_transpose_group(src_t, g, ld_prev_tok):
            vtok = None
            for j in range(4):
                t = 4 * g + j
                k.wait("sync", xfree[j % 2])
                ldx = k.dma("sync", lambda E, t=t, j=j, src_t=src_t: E.dma_start(
                    out=sx[j % 2][:], in_=xf_t[t] if src_t is xf_t else xo_t[t]),
                    key=f"x{j % 2}")
                xldlast[j % 2] = ldx
                k.wait("vector", ldx)
                k.wait("scalar", ldx)
                emit_ln(k, sx[j % 2][:], sxnb[:], {"st": sst})
                xfree[j % 2] = k.token("vector")
                vln = k.token("vector")
                k.wait("tensor", vln)
                if g == 0 and j == 0:
                    k.wait("tensor", lid)
                ttr = None
                for dt in range(8):
                    ttr = k.emit("tensor", lambda E, dt=dt, j=j: E.transpose(
                        ps_t[j % 2][:, dt * 128:(dt + 1) * 128],
                        sxnb[:, dt * 128:(dt + 1) * 128], sib[:]))
                k.wait("vector", ttr)
                dst = sxnTg[:, 0:4096].rearrange("p (n m) -> p n m", m=512)[:, :, j * 128:(j + 1) * 128]
                srcp = ps_t[j % 2][:].rearrange("p (n m) -> p n m", m=128)
                vtok = k.emit("vector", lambda E, dst=dst, srcp=srcp: E.tensor_copy(dst, srcp))
                if j < 3:
                    k.wait("tensor", vtok)
            return vtok

        lwk = k.dma("gpsimd", lambda E: E.dma_start(
            out=sw[:].rearrange("p (n m) -> p n m", m=2048),
            in_=wqkvT[:, 1024:3072].rearrange("(n p) m -> p n m", p=128)), key="wk")
        vsetones = k.emit("vector", lambda E: E.memset(
            sV[:].rearrange("p (n m) -> p n m", m=65)[:, :, 64:65], 1.0))
        prev_group_done = [None]
        cp_s = [None, None]
        cp_a = [None, None]
        for g in range(nmg):
            vg = ln_transpose_group(xf_t, g, None)
            k.wait("tensor", vg)
            k.wait("tensor", lwk)
            for et in range(8):
                k.wait("tensor", cp_s[et % 2])
                tm = None
                for dt in range(8):
                    tm = k.emit("tensor", lambda E, et=et, dt=dt: E.matmul(
                        ps_s[et % 2][:, 0:512],
                        sw[:, dt * 2048 + et * 128: dt * 2048 + et * 128 + 128],
                        sxnTg[:, dt * 512: dt * 512 + 512],
                        start=(dt == 0), stop=(dt == 7)))
                k.wait("vector", tm)
                vev = k.emit("vector", lambda E, et=et, g=g: E.tensor_copy(
                    sKT[:, et * seq + g * 512: et * seq + g * 512 + 512], ps_s[et % 2][:, 0:512]))
                cp_s[et % 2] = vev
            for j in range(4):
                t = 4 * g + j
                for ch in range(2):
                    k.wait("tensor", cp_a[ch])
                    tm = None
                    for dt in range(8):
                        tm = k.emit("tensor", lambda E, j=j, ch=ch, dt=dt: E.matmul(
                            ps_a[ch][:, 0:512],
                            sxnTg[:, dt * 512 + j * 128: dt * 512 + j * 128 + 128],
                            sw[:, dt * 2048 + 1024 + ch * 512: dt * 2048 + 1024 + ch * 512 + 512],
                            start=(dt == 0), stop=(dt == 7)))
                    k.wait("vector", tm)
                    dstv = sV[:].rearrange("p (n m) -> p n m", m=65)[
                        :, t * 16 + ch * 8:t * 16 + (ch + 1) * 8, 0:64]
                    srcv = ps_a[ch][:, 0:512].rearrange("p (n m) -> p n m", m=64)
                    vev = k.emit("vector", lambda E, dstv=dstv, srcv=srcv: E.tensor_copy(dstv, srcv))
                    cp_a[ch] = vev
            prev_group_done[0] = k.token("vector")

        k.wait("gpsimd", prev_group_done[0])
        k.wait("gpsimd", lwk)
        lwq = k.dma("gpsimd", lambda E: E.dma_start(
            out=sw[:, 0:8 * 1024].rearrange("p (n m) -> p n m", m=1024),
            in_=wqkvT[:, 0:1024].rearrange("(n p) m -> p n m", p=128)), key="wk")
        for g in range(tok // 512):
            vg = ln_transpose_group(xo_t, g, None)
            k.wait("tensor", vg)
            k.wait("tensor", lwq)
            for et in range(8):
                k.wait("tensor", cp_s[et % 2])
                tm = None
                for dt in range(8):
                    tm = k.emit("tensor", lambda E, et=et, dt=dt: E.matmul(
                        ps_s[et % 2][:, 0:512],
                        sw[:, dt * 1024 + et * 128: dt * 1024 + et * 128 + 128],
                        sxnTg[:, dt * 512: dt * 512 + 512],
                        start=(dt == 0), stop=(dt == 7)))
                k.wait("vector", tm)
                vev = k.emit("vector", lambda E, et=et, g=g: E.tensor_copy(
                    sQT[:, et * tok + g * 512: et * tok + g * 512 + 512], ps_s[et % 2][:, 0:512]))
                cp_s[et % 2] = vev

        k.wait("scalar", k.token("vector"))
        k.emit("vector", lambda E: E.memset(sone[:], 1.0))
        k.emit("vector", lambda E: E.memset(sorow[:], 1.0))
        k.emit("vector", lambda E: E.memset(sonesb[:], 1.0))
        k.emit("vector", lambda E: E.memset(skn2[:], 0.0))
        k.wait("tensor", k.token("vector"))
        ssq_k = ssq_big[:, 0:seq]
        last_tm = [None]
        for et in range(8):
            k.wait("scalar", last_tm[0])
            sq_t = k.emit("scalar", lambda E, et=et: E.activation(
                ssq_k, sKT[:, et * seq:(et + 1) * seq], AF.Square))
            k.wait("tensor", sq_t)
            for ch in range(seq // 512):
                for hb, base in ((0, 0), (1, 64)):
                    tm = k.emit("tensor", lambda E, ch=ch, base=base: E.matmul(
                        ps_z[base:base + 1, 0:512], sone[base:base + 64, 0:1],
                        ssq_big[base:base + 64, ch * 512:(ch + 1) * 512],
                        start=True, stop=True))
                    last_tm[0] = tm
                    k.wait("vector", tm)
                    k.emit("vector", lambda E, base=base: E.reduce_max(
                        snrm[0:1, 0:1] if base == 0 else None, None, None)) if False else None
                    k.emit("vector", lambda E, base=base, et=et: E.reduce_max(
                        skn2[base:base + 1, 8:9], ps_z[base:base + 1, 0:512], AX.X))
                    vm = k.emit("vector", lambda E, base=base, et=et: E.tensor_max(
                        skn2[base:base + 1, et:et + 1], skn2[base:base + 1, et:et + 1],
                        skn2[base:base + 1, 8:9]))
                    k.wait("tensor", vm)
        ssq_q = ssq_big[:, 0:tok]
        crow = sAT[:, 2 * tok: 2 * tok + 2 * tok].bitcast(F32)
        st_c = [None]
        for et in range(8):
            k.wait("scalar", last_tm[0])
            k.wait("scalar", k.token("vector"))
            sq_t = k.emit("scalar", lambda E, et=et: E.activation(
                ssq_q, sQT[:, et * tok:(et + 1) * tok], AF.Square))
            k.wait("tensor", sq_t)
            k.wait("tensor", st_c[0])
            for qc in range(tok // 512):
                for hb, base in ((0, 0), (1, 64)):
                    k.wait("tensor", k.token("vector"))
                    tm = k.emit("tensor", lambda E, qc=qc, base=base: E.matmul(
                        ps_z[base:base + 1, 0:512], sone[base:base + 64, 0:1],
                        ssq_q[base:base + 64, qc * 512:(qc + 1) * 512],
                        start=True, stop=True))
                    last_tm[0] = tm
                    k.wait("vector", tm)
                    vv = k.emit("vector", lambda E, qc=qc, base=base, et=et: E.tensor_scalar(
                        crow[base:base + 1, qc * 512:(qc + 1) * 512],
                        ps_z[base:base + 1, 0:512], skn2[base:base + 1, et:et + 1],
                        None, ALU.mult))
                    k.wait("scalar", vv)
                    sv = k.emit("scalar", lambda E, qc=qc, base=base: E.activation(
                        crow[base:base + 1, qc * 512:(qc + 1) * 512],
                        crow[base:base + 1, qc * 512:(qc + 1) * 512], AF.Sqrt))
                    k.wait("vector", sv)
                    k.emit("vector", lambda E, qc=qc, base=base: E.tensor_scalar(
                        crow[base:base + 1, qc * 512:(qc + 1) * 512],
                        crow[base:base + 1, qc * 512:(qc + 1) * 512], -1.0, None, ALU.mult))
            vfin = k.token("vector")
            k.wait("sync", vfin)
            st_c[0] = k.dma("sync", lambda E, et=et: E.dma_start(
                out=cscr[2 * et:2 * et + 2, :],
                in_=crow[0:65:64, 0:tok]), key="cs")
        k.wait("gpsimd", st_c[0])
        p2v = k.token("vector")
        pv_free = [p2v, p2v]
        k.wait("gpsimd", k.token("tensor"))
        k.wait("vector", xldlast[0])
        k.wait("scalar", xldlast[1])
        k.wait("vector", xldlast[1])
        ps_sx = [ps_s[0][:, 0:512], ps_s[1][:, 0:512],
                 ps_t[0][:].bitcast(F32)[:, 0:512], ps_t[1][:].bitcast(F32)[:, 0:512]]
        sPt4 = [sPt[0], sPt[1], sptx[:, 0:512], sptx[:, 512:1024]]
        scnrbufs = [sxnb[:, 0:512], sxnb[:, 512:1024]]
        exp_ring = [p2v] * 4
        pv_ring = [p2v] * 4
        cn_tok = [None, None]
        cn_last_use = [p2v, p2v]
        NIT = 2 * nmt
        LOOK = 2

        def load_cn(qc, hp):
            b = hp % 2
            k.wait("gpsimd", cn_last_use[b])
            cn_tok[b] = k.dma("gpsimd", lambda E, qc=qc, hp=hp, b=b: E.dma_start(
                out=scnrbufs[b][0:65:64, 0:512],
                in_=cscr[2 * hp:2 * hp + 2, qc * 512:(qc + 1) * 512]), key=f"cn{b}")

        for qc in range(nqc):
            k.wait("gpsimd", k.token("vector"))
            lbq = k.dma("gpsimd", lambda E, qc=qc: E.dma_start(
                out=sbias[:].rearrange("p (n m) -> p n m", m=512),
                in_=Bq[qc]), key="bq")
            load_cn(qc, 0)
            for hp in range(8):
                b = hp % 2
                if hp < 7:
                    load_cn(qc, hp + 1)
                scnr = scnrbufs[b]
                k.wait("tensor", cn_tok[b])
                z_read = [None, None]
                sexp_list = [None] * NIT
                pv_list = [None] * NIT

                def emit_score(idx, qc=qc, hp=hp, b=b, scnr=scnr,
                               sexp_list=sexp_list, lbq=lbq):
                    mt, hi = idx // 2, idx % 2
                    st = idx % 4
                    h = 2 * hp + hi
                    base = 64 * (h % 2)
                    lhsK = sKT[base:base + 64,
                               (h // 2) * seq + mt * 128:(h // 2) * seq + (mt + 1) * 128]
                    rhsQ = sQT[base:base + 64,
                               (h // 2) * tok + qc * 512:(h // 2) * tok + (qc + 1) * 512]
                    k.wait("tensor", exp_ring[st])
                    k.emit("tensor", lambda E, lhsK=lhsK, rhsQ=rhsQ, st=st: E.matmul(
                        ps_sx[st], lhsK, rhsQ, start=True, stop=False))
                    tsc = k.emit("tensor", lambda E, st=st, base=base, scnr=scnr: E.matmul(
                        ps_sx[st], sonesb[base:base + 1, :],
                        scnr[base:base + 1, 0:512], start=False, stop=True))
                    cn_last_use[b] = tsc
                    if mt < nbias:
                        sb_ = ssc
                        k.wait("vector", tsc)
                        k.wait("vector", k.token("scalar"))
                        if idx == 0:
                            k.wait("vector", lbq)
                        k.emit("vector", lambda E, st=st, sb_=sb_: E.tensor_scalar(
                            sb_, ps_sx[st], SCALE, None, ALU.mult))
                        exp_ring[st] = k.token("vector")
                        k.emit("vector", lambda E, mt=mt, sb_=sb_: E.tensor_add(
                            sb_, sb_, sbias[:, mt * 512:(mt + 1) * 512]))
                        vv = k.token("vector")
                        k.wait("scalar", vv)
                        k.wait("scalar", pv_ring[st])
                        sexp = k.emit("scalar", lambda E, st=st, sb_=sb_: E.activation(
                            sPt4[st], sb_, AF.Exp, bias=0.0, scale=1.0))
                    else:
                        k.wait("scalar", tsc)
                        k.wait("scalar", pv_ring[st])
                        sexp = k.emit("scalar", lambda E, st=st: E.activation(
                            sPt4[st], ps_sx[st], AF.Exp, bias=0.0, scale=SCALE))
                        exp_ring[st] = sexp
                    sexp_list[idx] = sexp

                def emit_pv(idx, hp=hp, z_read=z_read,
                            sexp_list=sexp_list, pv_list=pv_list):
                    mt, hi = idx // 2, idx % 2
                    st = idx % 4
                    h = 2 * hp + hi
                    k.wait("tensor", sexp_list[idx])
                    if mt == 0:
                        k.wait("tensor", pv_free[hi])
                    lhsV = sV[:].rearrange("p (n m) -> p n m", m=65)[
                        :, mt * 16 + h: mt * 16 + h + 1, 0:65].squeeze(1)
                    tpv = k.emit("tensor", lambda E, lhsV=lhsV, st=st, hi=hi, mt=mt: E.matmul(
                        ps_a[hi][0:65, 0:512], lhsV, sPt4[st],
                        start=(mt == 0), stop=(mt == nmt - 1)))
                    pv_list[idx] = tpv
                    pv_ring[st] = tpv
                    if mt == nmt - 1:
                        z_read[hi] = tpv

                for idx in range(NIT):
                    emit_score(idx)
                    if idx >= LOOK:
                        emit_pv(idx - LOOK)
                for idx in range(NIT - LOOK, NIT):
                    emit_pv(idx)
                for hi in range(2):
                    h = 2 * hp + hi
                    k.wait("vector", z_read[hi])
                    vz = k.emit("vector", lambda E, hi=hi: E.tensor_copy(szrow[:], ps_a[hi][64:65, 0:512]))
                    k.wait("tensor", vz)
                    tz = k.emit("tensor", lambda E: E.matmul(
                        ps_z[0:64, 0:512], sorow[0:1, 0:64],
                        szrow[:], start=True, stop=True))
                    k.wait("scalar", tz)
                    sl1 = k.emit("scalar", lambda E: E.activation(szl[:], ps_z[0:64, 0:512], AF.Ln))
                    k.wait("scalar", sl1)
                    sl2 = k.emit("scalar", lambda E: E.activation(szi[:], szl[:], AF.Exp, scale=-1.0))
                    k.wait("vector", sl2)
                    base = 64 * (h % 2)
                    dsta = sAT[base:base + 64,
                               (h // 2) * tok + qc * 512:(h // 2) * tok + (qc + 1) * 512]
                    pv_free[hi] = k.emit("vector", lambda E, dsta=dsta, hi=hi: E.tensor_tensor(
                        dsta, ps_a[hi][0:64, 0:512], szi[:], ALU.mult))
                    k.wait("tensor", sl1)

        p3v = k.token("vector")
        p3s = k.token("scalar")
        p3t = k.token("tensor")
        k.wait("gpsimd", lwq)
        k.wait("gpsimd", p3v)
        k.wait("gpsimd", p3t)
        k.wait("gpsimd", p3s)
        lwp = k.dma("gpsimd", lambda E: E.dma_start(
            out=sw4.rearrange("p (n m) -> p n m", m=1024),
            in_=wprojT.rearrange("(n p) m -> p n m", p=128)), key="wk")
        k.wait("sync", p3t)
        k.wait("sync", p3v)
        lbp = k.dma("sync", lambda E: E.dma_start(out=sy[:], in_=bpb), key="bp")
        lgb = k.dma("sync", lambda E: E.dma_start(out=sgb[:], in_=gb), key="gb")
        lbb = k.dma("sync", lambda E: E.dma_start(out=sbbb[:], in_=bbb), key="bb")
        k.wait("vector", lbp)
        k.wait("vector", lbb)
        vbb = k.emit("vector", lambda E: E.tensor_add(sbbb[:], sbbb[:], sy[:]))
        st_y = [None, None]
        pa_free = [None, None]
        for t in range(tok // 128):
            if t < 2:
                k.wait("sync", p3v)
                k.wait("sync", p3s)
                k.wait("sync", p3t)
            ldx = k.dma("sync", lambda E, t=t: E.dma_start(out=sx[t % 2][:], in_=xo_t[t]), key=f"x{t % 2}")
            k.wait("vector", ldx)
            k.wait("scalar", ldx)
            if t >= 1:
                k.wait("vector", st_y[(t - 1) % 2])
                k.wait("scalar", st_y[(t - 1) % 2])
            emit_ln(k, sx[t % 2][:], sxn, {"st": sst})
            k.wait("scalar", k.token("vector"))
            emit_ln(k, sxn, sxn2, {"st": sst})
            if t == 0:
                k.wait("vector", lgb)
            k.emit("vector", lambda E: E.tensor_mul(sxn2[:], sxn2[:], sgb[:]))
            vgl = k.emit("vector", lambda E: E.tensor_add(sxn2[:], sxn2[:], sbbb[:]))
            if t == 0:
                k.wait("tensor", lwp)
                k.wait("tensor", k.token("vector"))
            for half in range(2):
                k.wait("tensor", pa_free[half])
                tm = None
                for dt in range(8):
                    tm = k.emit("tensor", lambda E, dt=dt, half=half, t=t: E.matmul(
                        ps_a[half][:, 0:512],
                        sAT[:, dt * tok + t * 128: dt * tok + (t + 1) * 128],
                        sw4[:, dt * 1024 + half * 512: dt * 1024 + half * 512 + 512],
                        start=(dt == 0), stop=(dt == 7)))
                k.wait("vector", tm)
                pa_free[half] = k.emit("vector", lambda E, half=half, t=t: E.tensor_add(
                    sxn[:, half * 512:(half + 1) * 512],
                    ps_a[half][:, 0:512], sx[t % 2][:, half * 512:(half + 1) * 512]))
            vy = k.emit("vector", lambda E: E.tensor_add(sxn[:], sxn[:], sxn2[:]))
            k.wait("sync", vy)
            st_y[t % 2] = k.dma("sync", lambda E, t=t: E.dma_start(out=y_t[t], in_=sxn[:]), key=f"y{t % 2}")
            k.wait("vector", st_y[t % 2]) if t >= 14 else None
        k.wait("sync", st_y[0])
        k.wait("sync", st_y[1])
        k.barrier()
        k.finalize()
    return nc



def build_l1ffn(tok=TOK):
    nc = bass.Bass("TRN2", target_bir_lowering=False)
    x_in = nc.dram_tensor("x", [tok, D], F32, kind="ExternalInput").ap()
    xb_in = nc.dram_tensor("xb", [1, D], F32, kind="ExternalInput").ap()
    wrT = nc.dram_tensor("wrT", [D, D], F32, kind="ExternalInput").ap()
    wkT = nc.dram_tensor("wkT", [D, D], F32, kind="ExternalInput").ap()
    wvT = nc.dram_tensor("wvT", [D, D], F32, kind="ExternalInput").ap()
    mur = nc.dram_tensor("mur", [128, 8], F32, kind="ExternalInput").ap()
    muk = nc.dram_tensor("muk", [128, 8], F32, kind="ExternalInput").ap()
    identb = nc.dram_tensor("identb", [128, 128], F32, kind="ExternalInput").ap()
    y_out = nc.dram_tensor("y", [tok, D], F32, kind="ExternalOutput").ap()
    x_t = x_in.rearrange("(n p) m -> n p m", p=128)
    y_t = y_out.rearrange("(n p) m -> n p m", p=128)
    ntt = tok // 128
    dscr = nc.dram_tensor("dscr", [D], F32, kind="Internal").ap()

    with ExitStack() as ctx:
        e = ctx.enter_context
        k = K(nc, ctx)
        sxnT = e(nc.sbuf_tensor("sxnT", [128, 8 * tok], BF16))
        sxrT = e(nc.sbuf_tensor("sxrT", [128, 8 * tok], BF16))
        sxkT = e(nc.sbuf_tensor("sxkT", [128, 8 * tok], BF16))
        skaT = e(nc.sbuf_tensor("skaT", [128, 8 * tok], BF16))
        swr = e(nc.sbuf_tensor("swr", [128, 8 * D], BF16))
        swk = e(nc.sbuf_tensor("swk", [128, 8 * D], BF16))
        swv = e(nc.sbuf_tensor("swv", [128, 8 * D], BF16))
        sx = [e(nc.sbuf_tensor(f"sx{i}", [128, D], F32)) for i in range(2)]
        sxnb = e(nc.sbuf_tensor("sxnb", [128, D], BF16))
        sst = e(nc.sbuf_tensor("sst", [128, 18], F32))
        sib = e(nc.sbuf_tensor("sib", [128, 128], BF16))
        smu = e(nc.sbuf_tensor("smu", [128, 16], F32))
        sxbc = e(nc.sbuf_tensor("sxbc", [128, 16], BF16))
        sxbrow = e(nc.sbuf_tensor("sxbrow", [1, D], F32))
        srl = e(nc.sbuf_tensor("srl", [128, 512], BF16))
        sr = e(nc.sbuf_tensor("sr", [128, D], F32))
        sv1 = e(nc.sbuf_tensor("sv1", [128, D], F32))
        sy = e(nc.sbuf_tensor("sy", [128, D], F32))
        ps_a = [e(nc.psum_tensor(f"ps_a{i}", [128, 512], F32)) for i in range(2)]
        ps_b = [e(nc.psum_tensor(f"ps_b{i}", [128, 512], F32)) for i in range(2)]
        ps_t = [e(nc.psum_tensor(f"ps_t{i}", [128, 1024], BF16)) for i in range(2)]

        lid = k.dma("gpsimd", lambda E: E.dma_start(out=sib[:], in_=identb), key="id")
        lmur = k.dma("sync", lambda E: E.dma_start(out=smu[:, 0:8], in_=mur), key="mu")
        lmuk = k.dma("sync", lambda E: E.dma_start(out=smu[:, 8:16], in_=muk), key="mu2")
        lwr = k.dma("gpsimd", lambda E: E.dma_start(
            out=swr[:].rearrange("p (n m) -> p n m", m=D),
            in_=wrT.rearrange("(n p) m -> p n m", p=128)), key="wr")
        lwk = k.dma("gpsimd", lambda E: E.dma_start(
            out=swk[:].rearrange("p (n m) -> p n m", m=D),
            in_=wkT.rearrange("(n p) m -> p n m", p=128)), key="wk2")
        lwv = k.dma("gpsimd", lambda E: E.dma_start(
            out=swv[:].rearrange("p (n m) -> p n m", m=D),
            in_=wvT.rearrange("(n p) m -> p n m", p=128)), key="wv")
        lxb = k.dma("sync", lambda E: E.dma_start(out=sxbrow[:], in_=xb_in), key="xb")

        vcp = [None] * ntt
        ld = [None] * ntt
        for t in range(ntt):
            if t >= 2:
                k.wait("sync", vcp[t - 2])
            ld[t] = k.dma("sync", lambda E, t=t: E.dma_start(out=sx[t % 2][:], in_=x_t[t]), key=f"x{t % 2}")
            k.wait("vector", ld[t])
            emit_ln(k, sx[t % 2][:], sxnb[:], {"st": sst})
            vln = k.token("vector")
            k.wait("tensor", vln)
            if t == 0:
                k.wait("tensor", lid)
            if t >= 2:
                k.wait("tensor", vcp[t - 2])
            ttr = None
            for dt in range(8):
                ttr = k.emit("tensor", lambda E, dt=dt, t=t: E.transpose(
                    ps_t[t % 2][:, dt * 128:(dt + 1) * 128],
                    sxnb[:, dt * 128:(dt + 1) * 128], sib[:]))
            k.wait("vector", ttr)
            dst = sxnT[:].rearrange("p (n m) -> p n m", m=tok)[:, :, t * 128:(t + 1) * 128]
            srcp = ps_t[t % 2][:].rearrange("p (n m) -> p n m", m=128)
            vcp[t] = k.emit("vector", lambda E, dst=dst, srcp=srcp: E.tensor_copy(dst, srcp))
        k.wait("vector", lxb)
        emit_ln(k, sxbrow[:], sxbrow[:], {"st": sst})
        vbr = k.token("vector")
        k.wait("sync", vbr)
        stb = k.dma("sync", lambda E: E.dma_start(out=dscr, in_=sxbrow[:]), key="xb2")
        k.wait("sync", stb)
        k.wait("gpsimd", stb)
        lxc = k.dma("gpsimd", lambda E: E.dma_start(
            out=sxbc[:, 0:8], in_=dscr.rearrange("(n p) -> p n", p=128),
            allow_slow_non_contiguous=True), key="xb3")

        k.wait("vector", lxc)
        k.wait("vector", lmur)
        k.wait("vector", lmuk)
        for which, dstT, mu0 in (("r", sxrT, 0), ("k", sxkT, 8)):
            for dt in range(8):
                xo = dt * tok
                k.emit("vector", lambda E, xo=xo, dstT=dstT: E.tensor_sub(
                    dstT[:, xo + 1: xo + tok], sxnT[:, xo + 1: xo + tok], sxnT[:, xo: xo + tok - 1]))
                k.emit("vector", lambda E, xo=xo, dstT=dstT, mu0=mu0, dt=dt: E.scalar_tensor_tensor(
                    dstT[:, xo + 1: xo + tok], dstT[:, xo + 1: xo + tok],
                    smu[:, mu0 + dt: mu0 + dt + 1], sxnT[:, xo: xo + tok - 1], ALU.mult, ALU.add))
                k.emit("vector", lambda E, xo=xo, dstT=dstT, dt=dt: E.tensor_sub(
                    dstT[:, xo: xo + 1], sxnT[:, xo: xo + 1], sxbc[:, dt: dt + 1]))
                k.emit("vector", lambda E, xo=xo, dstT=dstT, mu0=mu0, dt=dt: E.scalar_tensor_tensor(
                    dstT[:, xo: xo + 1], dstT[:, xo: xo + 1],
                    smu[:, mu0 + dt: mu0 + dt + 1], sxbc[:, dt: dt + 1], ALU.mult, ALU.add))
        vmix = k.token("vector")

        k.wait("tensor", vmix)
        k.wait("tensor", lwk)
        srl_free = [None, None]
        for kt in range(8):
            for ch in range(tok // 512):
                pb = ps_b[ch % 2]
                k.wait("tensor", srl_free[ch % 2])
                tm = None
                for ct in range(8):
                    tm = k.emit("tensor", lambda E, kt=kt, ch=ch, ct=ct, pb=pb: E.matmul(
                        pb[:, 0:512],
                        swk[:, ct * D + kt * 128: ct * D + kt * 128 + 128],
                        sxkT[:, ct * tok + ch * 512: ct * tok + (ch + 1) * 512],
                        start=(ct == 0), stop=(ct == 7)))
                k.wait("vector", tm)
                vrl = k.emit("vector", lambda E, pb=pb: E.tensor_scalar(
                    srl[:], pb[:, 0:512], 0.0, None, ALU.max))
                srl_free[ch % 2] = vrl
                k.wait("scalar", vrl)
                ssqk = k.emit("scalar", lambda E, kt=kt, ch=ch: E.activation(
                    skaT[:, kt * tok + ch * 512: kt * tok + (ch + 1) * 512], srl[:], AF.Square))
                k.wait("vector", ssqk)
                srl_free[ch % 2] = ssqk

        k.wait("tensor", lwr)
        k.wait("tensor", lwv)
        k.wait("tensor", k.token("scalar"))
        st_y = [None, None]
        pr_free = [None, None]
        pv_free = [None, None]
        for t in range(ntt):
            if t < 2:
                k.wait("sync", vcp[ntt - 2 + t])
            else:
                k.wait("sync", st_y[t % 2])
            ld2 = k.dma("sync", lambda E, t=t: E.dma_start(out=sx[t % 2][:], in_=x_t[t]), key=f"x{t % 2}")
            for half in range(2):
                k.wait("tensor", pr_free[half])
                tm = None
                for ct in range(8):
                    tm = k.emit("tensor", lambda E, ct=ct, half=half, t=t: E.matmul(
                        ps_a[half][:, 0:512],
                        sxrT[:, ct * tok + t * 128: ct * tok + (t + 1) * 128],
                        swr[:, ct * D + half * 512: ct * D + half * 512 + 512],
                        start=(ct == 0), stop=(ct == 7)))
                k.wait("scalar", tm)
                k.wait("scalar", pv_free[half])
                pr_free[half] = k.emit("scalar", lambda E, half=half: E.activation(
                    sr[:, half * 512:(half + 1) * 512], ps_a[half][:, 0:512], AF.Sigmoid))
            for half in range(2):
                k.wait("tensor", pv_free[half])
                tm = None
                for ct in range(8):
                    tm = k.emit("tensor", lambda E, ct=ct, half=half, t=t: E.matmul(
                        ps_b[half][:, 0:512],
                        skaT[:, ct * tok + t * 128: ct * tok + (t + 1) * 128],
                        swv[:, ct * D + half * 512: ct * D + half * 512 + 512],
                        start=(ct == 0), stop=(ct == 7)))
                k.wait("vector", tm)
                k.wait("vector", pr_free[half])
                pv_free[half] = k.emit("vector", lambda E, half=half: E.tensor_tensor(
                    sv1[:, half * 512:(half + 1) * 512], sr[:, half * 512:(half + 1) * 512],
                    ps_b[half][:, 0:512], ALU.mult))
            k.wait("vector", ld2)
            if t >= 1:
                k.wait("vector", st_y[(t - 1) % 2])
            vy = k.emit("vector", lambda E, t=t: E.scalar_tensor_tensor(
                sy[:], sv1[:], 0.5, sx[t % 2][:], ALU.mult, ALU.add))
            k.wait("sync", vy)
            st_y[t % 2] = k.dma("sync", lambda E, t=t: E.dma_start(out=y_t[t], in_=sy[:]), key=f"y{t % 2}")
            k.wait("vector", st_y[t % 2]) if False else None
        k.wait("sync", st_y[0])
        k.wait("sync", st_y[1])
        k.barrier()
        k.finalize()
    return nc



def build_l1attn(tok=TOK, seq=SEQ):
    H, HD, FEAT = 16, 64, 256
    nmt = seq // 128
    ntt = tok // 128
    VW = H * 65

    nc = bass.Bass("TRN2", target_bir_lowering=False, dynamic_dma_scratch_size=8192)
    x_own = nc.dram_tensor("x_own", [tok, D], F32, kind="ExternalInput").ap()
    x_full = nc.dram_tensor("x_full", [seq, D], F32, kind="ExternalInput").ap()
    wqkv = nc.dram_tensor("wqkv", [D, E3], F32, kind="ExternalInput").ap()
    projp = nc.dram_tensor("projp", [8, 128, FEAT], F32, kind="ExternalInput").ap()
    wout = nc.dram_tensor("wout", [D, D], F32, kind="ExternalInput").ap()
    bob = nc.dram_tensor("bob", [128, D], F32, kind="ExternalInput").ap()
    identb = nc.dram_tensor("identb", [128, 128], F32, kind="ExternalInput").ap()
    y_out = nc.dram_tensor("y", [tok, D], F32, kind="ExternalOutput").ap()

    xf_t = x_full.rearrange("(n p) m -> n p m", p=128)
    xo_t = x_own.rearrange("(n p) m -> n p m", p=128)
    y_t = y_out.rearrange("(n p) m -> n p m", p=128)

    with ExitStack() as ctx:
        e = ctx.enter_context
        k = K(nc, ctx)
        sKT = e(nc.sbuf_tensor("sKT", [128, 8 * seq], BF16))
        sV = e(nc.sbuf_tensor("sV", [128, nmt * VW], BF16))
        sQT = e(nc.sbuf_tensor("sQT", [128, 8 * tok], BF16))
        sxnTg = e(nc.sbuf_tensor("sxnTg", [128, 8 * 512], BF16))
        sx = [e(nc.sbuf_tensor(f"sx{i}", [128, D], F32)) for i in range(2)]
        sxnb = e(nc.sbuf_tensor("sxnb", [128, D], BF16))
        sst = e(nc.sbuf_tensor("sst", [128, 18], F32))
        sib = e(nc.sbuf_tensor("sib", [128, 128], BF16))
        sproj = e(nc.sbuf_tensor("sproj", [128, 8 * FEAT], BF16))
        skv = e(nc.sbuf_tensor("skv", [128, H * 130], BF16))
        skp = [e(nc.sbuf_tensor(f"skp{i}", [128, FEAT], BF16)) for i in range(2)]
        selu = sxnTg[:, 0:1024].bitcast(F32)
        selu2 = sxnTg[:, 1024:2048].bitcast(F32)
        szi = e(nc.sbuf_tensor("szi", [128, 1], F32))
        sy = e(nc.sbuf_tensor("sy", [128, D], F32))
        sbo = e(nc.sbuf_tensor("sbo", [128, D], F32))
        swl = e(nc.sbuf_tensor("swl", [128, 8 * 1024], BF16))
        sattn = None
        sqp = None
        ps_a = [e(nc.psum_tensor(f"ps_a{i}", [128, 512], F32)) for i in range(2)]
        ps_kv = [e(nc.psum_tensor(f"ps_kv{i}", [128, 512], F32)) for i in range(4)]
        ps_t = [e(nc.psum_tensor(f"ps_t{i}", [128, 1024], BF16)) for i in range(2)]

        lid = k.dma("gpsimd", lambda E: E.dma_start(out=sib[:], in_=identb), key="id")
        lpj = k.dma("gpsimd", lambda E: E.dma_start(
            out=sproj[:].rearrange("p (n m) -> p n m", m=FEAT), in_=projp), key="pj")

        xfree = [None, None]
        xldlast = [None, None]

        def ln_group(src_t, g):
            vtok = None
            for j in range(4):
                t = 4 * g + j
                k.wait("sync", xfree[j % 2])
                ldx = k.dma("sync", lambda E, t=t, j=j, src_t=src_t: E.dma_start(
                    out=sx[j % 2][:], in_=xf_t[t] if src_t is xf_t else xo_t[t]),
                    key=f"x{j % 2}")
                xldlast[j % 2] = ldx
                k.wait("vector", ldx)
                emit_ln(k, sx[j % 2][:], sxnb[:], {"st": sst})
                xfree[j % 2] = k.token("vector")
                vln = k.token("vector")
                k.wait("tensor", vln)
                if g == 0 and j == 0:
                    k.wait("tensor", lid)
                ttr = None
                for dt in range(8):
                    ttr = k.emit("tensor", lambda E, dt=dt, j=j: E.transpose(
                        ps_t[j % 2][:, dt * 128:(dt + 1) * 128],
                        sxnb[:, dt * 128:(dt + 1) * 128], sib[:]))
                k.wait("vector", ttr)
                dst = sxnTg[:, 0:4096].rearrange("p (n m) -> p n m", m=512)[:, :, j * 128:(j + 1) * 128]
                srcp = ps_t[j % 2][:].rearrange("p (n m) -> p n m", m=128)
                vtok = k.emit("vector", lambda E, dst=dst, srcp=srcp: E.tensor_copy(dst, srcp))
                if j < 3:
                    k.wait("tensor", vtok)
            return vtok

        lwk = k.dma("gpsimd", lambda E: E.dma_start(
            out=swl[:].rearrange("p (n m) -> p n m", m=1024),
            in_=wqkv[:, 1024:2048].rearrange("(n p) m -> p n m", p=128)), key="wk")
        vone = k.emit("vector", lambda E: E.memset(
            sV[:].rearrange("p (n m) -> p n m", m=65)[:, :, 64:65], 1.0))
        lwv_tok = [None]
        cp_a1 = [None, None]
        for g in range(seq // 512):
            vg = ln_group(xf_t, g)
            k.wait("tensor", vg)
            k.wait("tensor", lwk)
            for et in range(8):
                k.wait("tensor", cp_a1[et % 2])
                tm = None
                for dt in range(8):
                    tm = k.emit("tensor", lambda E, et=et, dt=dt: E.matmul(
                        ps_a[et % 2][:, 0:512],
                        swl[:, dt * 1024 + et * 128: dt * 1024 + et * 128 + 128],
                        sxnTg[:, dt * 512: dt * 512 + 512],
                        start=(dt == 0), stop=(dt == 7)))
                k.wait("vector", tm)
                vev = k.emit("vector", lambda E, et=et, g=g: E.tensor_copy(
                    sKT[:, et * seq + g * 512: et * seq + g * 512 + 512], ps_a[et % 2][:, 0:512]))
                cp_a1[et % 2] = vev
            prev = k.token("vector")
        k.wait("gpsimd", lwk)
        k.wait("gpsimd", prev)
        lwv = k.dma("gpsimd", lambda E: E.dma_start(
            out=swl[:].rearrange("p (n m) -> p n m", m=1024),
            in_=wqkv[:, 2048:3072].rearrange("(n p) m -> p n m", p=128)), key="wk")
        for g in range(seq // 512):
            vg = ln_group(xf_t, g)
            k.wait("tensor", vg)
            k.wait("tensor", lwv)
            k.wait("tensor", vone)
            for j in range(4):
                t = 4 * g + j
                for ch in range(2):
                    k.wait("tensor", cp_a1[ch])
                    tm = None
                    for dt in range(8):
                        tm = k.emit("tensor", lambda E, j=j, ch=ch, dt=dt: E.matmul(
                            ps_a[ch][:, 0:512],
                            sxnTg[:, dt * 512 + j * 128: dt * 512 + j * 128 + 128],
                            swl[:, dt * 1024 + ch * 512: dt * 1024 + ch * 512 + 512],
                            start=(dt == 0), stop=(dt == 7)))
                    k.wait("vector", tm)
                    dstv = sV[:].rearrange("p (n m) -> p n m", m=65)[
                        :, t * 16 + ch * 8:t * 16 + (ch + 1) * 8, 0:64]
                    srcv = ps_a[ch][:, 0:512].rearrange("p (n m) -> p n m", m=64)
                    vev = k.emit("vector", lambda E, dstv=dstv, srcv=srcv: E.tensor_copy(dstv, srcv))
                    cp_a1[ch] = vev
            prev = k.token("vector")
        k.wait("gpsimd", lwv)
        k.wait("gpsimd", prev)
        lwq = k.dma("gpsimd", lambda E: E.dma_start(
            out=swl[:].rearrange("p (n m) -> p n m", m=1024),
            in_=wqkv[:, 0:1024].rearrange("(n p) m -> p n m", p=128)), key="wk")
        for g in range(tok // 512):
            vg = ln_group(xo_t, g)
            k.wait("tensor", vg)
            k.wait("tensor", lwq)
            for et in range(8):
                k.wait("tensor", cp_a1[et % 2])
                tm = None
                for dt in range(8):
                    tm = k.emit("tensor", lambda E, et=et, dt=dt: E.matmul(
                        ps_a[et % 2][:, 0:512],
                        swl[:, dt * 1024 + et * 128: dt * 1024 + et * 128 + 128],
                        sxnTg[:, dt * 512: dt * 512 + 512],
                        start=(dt == 0), stop=(dt == 7)))
                k.wait("vector", tm)
                vev = k.emit("vector", lambda E, et=et, g=g: E.tensor_copy(
                    sQT[:, et * tok + g * 512: et * tok + g * 512 + 512], ps_a[et % 2][:, 0:512]))
                cp_a1[et % 2] = vev

        k.wait("tensor", lpj)
        k.wait("tensor", cp_a1[0])
        k.wait("tensor", cp_a1[1])

        def emit_elu1(pin, pout_bf, width):
            v1 = k.emit("vector", lambda E: E.tensor_scalar(selu[:, 0:width], pin[:, 0:width], 0.0, None, ALU.min))
            k.wait("scalar", v1)
            s1 = k.emit("scalar", lambda E: E.activation(selu2[:, 0:width], selu[:, 0:width], AF.Exp))
            k.wait("vector", s1)
            v2 = k.emit("vector", lambda E: E.tensor_scalar(selu[:, 0:width], pin[:, 0:width], 0.0, None, ALU.max))
            v3 = k.emit("vector", lambda E: E.tensor_add(pout_bf, selu[:, 0:width], selu2[:, 0:width]))
            return v3

        kp_free = [None, None]
        kv_done = []
        for hp in range(8):
            kvp = [ps_kv[0], ps_kv[1], ps_kv[2], ps_kv[3]]
            for nt in range(nmt):
                for hi in range(2):
                    h = 2 * hp + hi
                    base = 64 * (h % 2)
                    lhsK = sKT[base:base + 64, (h // 2) * seq + nt * 128:(h // 2) * seq + (nt + 1) * 128]
                    rhsP = sproj[base:base + 64, hp * FEAT:(hp + 1) * FEAT]
                    k.wait("tensor", kp_free[hi])
                    tkp = k.emit("tensor", lambda E, lhsK=lhsK, rhsP=rhsP, hi=hi: E.matmul(
                        ps_a[hi][:, 0:FEAT], lhsK, rhsP, start=True, stop=True))
                    k.wait("vector", tkp)
                    velu = emit_elu1(ps_a[hi], skp[hi][:, 0:FEAT], FEAT)
                    kp_free[hi] = velu
                    k.wait("tensor", velu)
                    for ft in range(2):
                        tkv = k.emit("tensor", lambda E, hi=hi, ft=ft, nt=nt, h=h: E.matmul(
                            kvp[2 * hi + ft][:, 0:65],
                            skp[hi][:, ft * 128:(ft + 1) * 128],
                            sV[:, (nt * 16 + h) * 65:(nt * 16 + h) * 65 + 65],
                            start=(nt == 0), stop=(nt == nmt - 1)))
                        if nt == nmt - 1:
                            k.wait("vector", tkv)
            for hi in range(2):
                h = 2 * hp + hi
                for ft in range(2):
                    k.emit("vector", lambda E, hi=hi, ft=ft, h=h: E.tensor_copy(
                        skv[:, h * 130 + ft * 65: h * 130 + ft * 65 + 65],
                        kvp[2 * hi + ft][:, 0:65]))
            kv_done.append(k.token("vector"))
        kvd = k.token("vector")

        sattn = sV[:, 0:8192]
        sattn = sV[:, 0:ntt * 1024]
        sqp = sV[:, ntt * 1024: ntt * 1024 + 2 * tok]
        qp_free = None
        at_free = [None, None]
        at_done = []
        for hp in range(8):
            for hi in range(2):
                h = 2 * hp + hi
                base = 64 * (h % 2)
                for fc in range(2):
                    for qc in range(tok // 512):
                        rhsQ = sQT[base:base + 64, (h // 2) * tok + qc * 512:(h // 2) * tok + (qc + 1) * 512]
                        lhsP = sproj[base:base + 64, hp * FEAT + fc * 128: hp * FEAT + (fc + 1) * 128]
                        k.wait("tensor", qp_free)
                        tqp = k.emit("tensor", lambda E, lhsP=lhsP, rhsQ=rhsQ, fc=fc: E.matmul(
                            ps_a[fc][:, 0:512], lhsP, rhsQ, start=True, stop=True))
                        k.wait("vector", tqp)
                        dq = sqp.rearrange("p (n m) -> p n m", m=tok)[:, fc:fc + 1, qc * 512:(qc + 1) * 512].squeeze(1)
                        qp_free = emit_elu1(ps_a[fc], dq, 512)
                k.wait("tensor", qp_free)
                if hp == 0 and hi == 0:
                    k.wait("tensor", kvd)
                for tt in range(ntt):
                    k.wait("tensor", at_free[tt % 2])
                    tm = None
                    for ft in range(2):
                        tm = k.emit("tensor", lambda E, ft=ft, tt=tt, h=h: E.matmul(
                            ps_kv[2 + (tt % 2)][:, 0:65],
                            sqp[:, ft * tok + tt * 128: ft * tok + (tt + 1) * 128],
                            skv[:, h * 130 + ft * 65: h * 130 + ft * 65 + 65],
                            start=(ft == 0), stop=(ft == 1)))
                    k.wait("vector", tm)
                    pk = ps_kv[2 + (tt % 2)]
                    k.emit("vector", lambda E, pk=pk: E.tensor_scalar(szi[:], pk[:, 64:65], 1e-8, None, ALU.add))
                    k.emit("vector", lambda E: E.reciprocal(szi[:], szi[:]))
                    va = k.emit("vector", lambda E, pk=pk, tt=tt, h=h: E.tensor_scalar(
                        sattn[:, tt * 1024 + h * 64: tt * 1024 + (h + 1) * 64],
                        pk[:, 0:64], szi[:, 0:1], None, ALU.mult))
                    at_free[tt % 2] = va
                    at_done.append(va)
        atd = k.token("vector")

        sAT = sKT[:, 0:8 * tok]
        if 8 * tok + 8192 <= 8 * seq:
            sw4 = sKT[:, 8 * tok: 8 * tok + 8192]
        else:
            sw4 = swl[:, 0:8192]
        p3t = k.token("tensor")
        p3v = k.token("vector")
        p3s = k.token("scalar")
        k.wait("gpsimd", p3t)
        k.wait("gpsimd", p3v)
        lw4 = k.dma("gpsimd", lambda E: E.dma_start(
            out=sw4.rearrange("p (n m) -> p n m", m=1024),
            in_=wout.rearrange("(n p) m -> p n m", p=128)), key="wk")
        k.wait("sync", p3t)
        k.wait("sync", p3v)
        lbo = k.dma("sync", lambda E: E.dma_start(out=sbo[:], in_=bob), key="bo")
        k.wait("tensor", atd)
        vT = [None] * ntt
        for tt in range(ntt):
            ttr = None
            for dt in range(8):
                ttr = k.emit("tensor", lambda E, tt=tt, dt=dt: E.transpose(
                    ps_t[tt % 2][:, dt * 128:(dt + 1) * 128],
                    sattn[:, tt * 1024 + dt * 128: tt * 1024 + (dt + 1) * 128], sib[:]))
            k.wait("vector", ttr)
            dst = sAT.rearrange("p (n m) -> p n m", m=tok)[:, :, tt * 128:(tt + 1) * 128]
            srcp = ps_t[tt % 2][:].rearrange("p (n m) -> p n m", m=128)
            vT[tt] = k.emit("vector", lambda E, dst=dst, srcp=srcp: E.tensor_copy(dst, srcp))
            if tt + 1 < ntt:
                k.wait("tensor", vT[tt])
        st_y = [None, None]
        pa_free = [None, None]
        for t in range(ntt):
            if t < 2:
                k.wait("sync", xldlast[t % 2])
                k.wait("sync", k.token("vector"))
            else:
                k.wait("sync", st_y[t % 2])
            ldx = k.dma("sync", lambda E, t=t: E.dma_start(out=sx[t % 2][:], in_=xo_t[t]), key=f"x{t % 2}")
            if t == 0:
                k.wait("tensor", lw4)
                k.wait("tensor", vT[ntt - 1])
            for half in range(2):
                k.wait("tensor", pa_free[half])
                tm = None
                for dt in range(8):
                    tm = k.emit("tensor", lambda E, dt=dt, half=half, t=t: E.matmul(
                        ps_a[half][:, 0:512],
                        sAT[:, dt * tok + t * 128: dt * tok + (t + 1) * 128],
                        sw4[:, dt * 1024 + half * 512: dt * 1024 + half * 512 + 512],
                        start=(dt == 0), stop=(dt == 7)))
                k.wait("vector", tm)
                k.wait("vector", ldx)
                if half == 0:
                    k.wait("vector", lbo)
                if t >= 1:
                    k.wait("vector", st_y[(t - 1) % 2])
                k.emit("vector", lambda E, half=half, t=t: E.tensor_add(
                    sy[:, half * 512:(half + 1) * 512], ps_a[half][:, 0:512],
                    sx[t % 2][:, half * 512:(half + 1) * 512]))
                pa_free[half] = k.emit("vector", lambda E, half=half: E.tensor_add(
                    sy[:, half * 512:(half + 1) * 512], sy[:, half * 512:(half + 1) * 512],
                    sbo[:, half * 512:(half + 1) * 512]))
            vy = k.token("vector")
            k.wait("sync", vy)
            st_y[t % 2] = k.dma("sync", lambda E, t=t: E.dma_start(out=y_t[t], in_=sy[:]), key=f"y{t % 2}")
        k.wait("sync", st_y[0])
        k.wait("sync", st_y[1])
        k.barrier()
        k.finalize()
    return nc



_DBG = None


def kernel(**inp):
    X = np.ascontiguousarray(inp["L0_attn_x"], dtype=np.float32).reshape(16384, D)
    ident = np.eye(128, dtype=np.float32)

    def tile_row(v):
        return np.ascontiguousarray(np.broadcast_to(np.asarray(v, np.float32), (128, D)))

    nc0 = build_prep()
    wg = np.ascontiguousarray(inp["L0_ffn_w_gate"], dtype=np.float32)
    wu = np.ascontiguousarray(inp["L0_ffn_w_up"], dtype=np.float32)
    wd = np.ascontiguousarray(inp["L0_ffn_w_down"], dtype=np.float32)
    cenT = np.ascontiguousarray(np.asarray(inp["L0_attn_spatial_centers"], np.float32).T)
    in_maps = []
    for c in range(NCORES):
        in_maps.append({
            "wg": np.ascontiguousarray(wg[:, c * 128:(c + 1) * 128]),
            "wu": np.ascontiguousarray(wu[:, c * 128:(c + 1) * 128]),
            "wd": np.ascontiguousarray(wd[c * 128:(c + 1) * 128, :]),
            "cenT": cenT,
            "identb": ident,
        })
    res = run_bass_kernel_spmd(nc0, in_maps, core_ids=list(range(NCORES)))
    wgqT = np.concatenate([res.results[c]["o_wgqT"] for c in range(NCORES)], axis=0)
    wuqT = np.concatenate([res.results[c]["o_wuqT"] for c in range(NCORES)], axis=0)
    wdqT = np.concatenate([res.results[c]["o_wdqT"].reshape(I_FFN, 128) for c in range(NCORES)], axis=1)
    Bmat = res.results[0]["o_B"].reshape(1024, 1024)

    Bpad = np.zeros((1152, SEQ), np.float32)
    Bpad[1:1025, 1:1025] = Bmat[:, 0:1024]

    nc_ffn0 = build_l0ffn()
    nc_attn0 = build_l0attn()
    nc_ffn1 = build_l1ffn()
    nc_attn1 = build_l1attn()

    def run_l0ffn(Xc):
        im = [{"x": np.ascontiguousarray(Xc[c * TOK:(c + 1) * TOK]),
               "wgqT": wgqT, "wuqT": wuqT, "wdqT": np.ascontiguousarray(wdqT),
               "identb": ident} for c in range(NCORES)]
        r = run_bass_kernel_spmd(nc_ffn0, im, core_ids=list(range(NCORES)))
        return np.concatenate([r.results[c]["y"] for c in range(NCORES)], axis=0)

    wqkv0T = np.ascontiguousarray(np.asarray(inp["L0_attn_W_qkv"], np.float32).T)
    wproj0T = np.ascontiguousarray(np.asarray(inp["L0_attn_W_proj"], np.float32).T)
    gb = tile_row(inp["L0_attn_ln_gamma"])
    bbb = tile_row(inp["L0_attn_ln_beta"])
    bpb = tile_row(inp["L0_attn_b_proj"])

    def run_l0attn(Xc):
        im = []
        for c in range(NCORES):
            b, hq = c // 2, c % 2
            Bq = np.zeros((4, 9, 128, 512), np.float32)
            for qc in range(4):
                for mt in range(9):
                    Bq[qc, mt] = Bpad[mt * 128:(mt + 1) * 128,
                                      hq * 2048 + qc * 512: hq * 2048 + (qc + 1) * 512]
            im.append({
                "x_own": np.ascontiguousarray(Xc[c * TOK:(c + 1) * TOK]),
                "x_full": np.ascontiguousarray(Xc[b * SEQ:(b + 1) * SEQ]),
                "wqkvT": wqkv0T, "wprojT": wproj0T,
                "gb": gb, "bbb": bbb, "bpb": bpb, "Bq": Bq, "identb": ident,
            })
        r = run_bass_kernel_spmd(nc_attn0, im, core_ids=list(range(NCORES)))
        return np.concatenate([r.results[c]["y"] for c in range(NCORES)], axis=0)

    wrT = np.ascontiguousarray(np.asarray(inp["L1_ffn_W_r"], np.float32).T)
    wkT = np.ascontiguousarray(np.asarray(inp["L1_ffn_W_k"], np.float32).T)
    wvT = np.ascontiguousarray(np.asarray(inp["L1_ffn_W_v"], np.float32).T)
    murT = np.ascontiguousarray(np.asarray(inp["L1_ffn_mu_r"], np.float32).reshape(8, 128).T)
    mukT = np.ascontiguousarray(np.asarray(inp["L1_ffn_mu_k"], np.float32).reshape(8, 128).T)

    def run_l1ffn(Xc):
        im = []
        for c in range(NCORES):
            xb = np.zeros((1, D), np.float32) if c % 2 == 0 else \
                np.ascontiguousarray(Xc[c * TOK - 1: c * TOK])
            im.append({"x": np.ascontiguousarray(Xc[c * TOK:(c + 1) * TOK]), "xb": xb,
                       "wrT": wrT, "wkT": wkT, "wvT": wvT,
                       "mur": murT, "muk": mukT, "identb": ident})
        r = run_bass_kernel_spmd(nc_ffn1, im, core_ids=list(range(NCORES)))
        return np.concatenate([r.results[c]["y"] for c in range(NCORES)], axis=0)

    wqkv1 = np.ascontiguousarray(inp["L1_attn_w_qkv"], dtype=np.float32)
    proj1 = np.asarray(inp["L1_attn_proj_matrix"], np.float32)
    projp = np.zeros((8, 128, 256), np.float32)
    for h in range(16):
        projp[h // 2, 64 * (h % 2):64 * (h % 2) + 64, :] = proj1[h]
    wout1 = np.ascontiguousarray(inp["L1_attn_w_out"], dtype=np.float32)
    bob = tile_row(inp["L1_attn_b_out"])

    def run_l1attn(Xc):
        im = []
        for c in range(NCORES):
            b = c // 2
            im.append({"x_own": np.ascontiguousarray(Xc[c * TOK:(c + 1) * TOK]),
                       "x_full": np.ascontiguousarray(Xc[b * SEQ:(b + 1) * SEQ]),
                       "wqkv": wqkv1, "projp": projp, "wout": wout1,
                       "bob": bob, "identb": ident})
        r = run_bass_kernel_spmd(nc_attn1, im, core_ids=list(range(NCORES)))
        return np.concatenate([r.results[c]["y"] for c in range(NCORES)], axis=0)

    X1 = run_l0ffn(X)
    X2 = run_l0attn(X1)
    X3 = run_l0ffn(X2)
    X4 = run_l1ffn(X3)
    X5 = run_l1attn(X4)
    X6 = run_l1ffn(X5)
    if _DBG is not None:
        _DBG.extend([X1, X2, X3, X4, X5, X6])
    return X6.reshape(4, SEQ, D)


if __name__ == "__main__":
    pass


# version 07 (final)
# speedup vs baseline: 1.2016x; 1.0144x over previous
import numpy as np
import ml_dtypes
from contextlib import ExitStack

import concourse.bass as bass
import concourse.mybir as mybir
from concourse.bass_utils import run_bass_kernel_spmd

F32 = mybir.dt.float32
BF16 = mybir.dt.bfloat16
I32 = mybir.dt.int32
U8 = mybir.dt.uint8
AF = mybir.ActivationFunctionType
ALU = mybir.AluOpType
AX = mybir.AxisListType

MAGIC = 12582912.0
NCORES = 8
D = 1024
I_FFN = 2816
TOK = 2048
NTT = TOK // 128
SEQ = 4096
E3 = 3072

ENGS = ("sync", "gpsimd", "vector", "scalar", "tensor")


class K:

    def __init__(self, nc, ctx):
        self.nc = nc
        self.ctx = ctx
        self.q = {e: [] for e in ENGS}
        self.csem = {e: ctx.enter_context(nc.semaphore(f"c_{e}")) for e in ENGS}
        self.dsem = {}
        self.ccnt = {e: 0 for e in ENGS}
        self.dcnt = {}

    def _dma_sem(self, eng, key):
        kk = (eng, key)
        if kk not in self.dsem:
            self.dsem[kk] = self.ctx.enter_context(
                self.nc.semaphore(f"d_{eng}_{key}"))
            self.dcnt[kk] = 0
        return kk

    def emit(self, eng, fn):
        sem = self.csem[eng]
        prev = self.ccnt[eng]
        self.ccnt[eng] += 1

        def wrapped(E, fn=fn, sem=sem, prev=prev):
            if prev > 0:
                E.wait_ge(sem, prev)
            fn(E).then_inc(sem, 1)
        self.q[eng].append(wrapped)
        return (sem, self.ccnt[eng])

    def dma(self, eng, fn, key="dflt"):
        kk = self._dma_sem(eng, key)
        self.dcnt[kk] += 16
        sem, val = self.dsem[kk], self.dcnt[kk]

        def wrapped(E, fn=fn, sem=sem):
            fn(E).then_inc(sem, 16)
        self.q[eng].append(wrapped)
        return (sem, val)

    def token(self, eng):
        return (self.csem[eng], self.ccnt[eng])

    def wait(self, eng, tok):
        if tok is None:
            return
        sem, val = tok
        if val <= 0:
            return
        self.q[eng].append(lambda E, s=sem, v=val: E.wait_ge(s, v))

    def barrier(self):
        toks = []
        for e in ENGS:
            toks.append(self.emit(e, lambda E: E.nop()))
        for kk, sem in self.dsem.items():
            if self.dcnt[kk] > 0:
                toks.append((sem, self.dcnt[kk]))
        for e in ENGS:
            for tok in toks:
                if tok[0] is not self.csem[e]:
                    self.wait(e, tok)

    def finalize(self):
        with self.nc.Block() as block:
            @block.sync
            def _(E):
                for f in self.q["sync"]:
                    f(E)

            @block.gpsimd
            def _(E):
                for f in self.q["gpsimd"]:
                    f(E)

            @block.vector
            def _(E):
                for f in self.q["vector"]:
                    f(E)

            @block.scalar
            def _(E):
                for f in self.q["scalar"]:
                    f(E)

            @block.tensor
            def _(E):
                for f in self.q["tensor"]:
                    f(E)


def emit_fp4_qd(k, x_ap, out_ap, scr, D_cols):
    nb = D_cols // 16
    v = "vector"
    bm, s, r = scr["bm"][:, 0:nb], scr["s"][:, 0:nb], scr["r"][:, 0:nb]
    t = scr["t"][:, 0:D_cols]
    u = scr["u"][:, 0:D_cols]
    w1 = scr["w1"][:, 0:D_cols]
    w2 = scr["w2"][:, 0:D_cols]
    dd = scr["dd"][:, 0:D_cols]
    ee = scr["ee"][:, 0:D_cols]
    x3 = x_ap.rearrange("p (b k) -> p b k", k=16)
    k.emit(v, lambda E: E.tensor_reduce(bm, x3, AX.X, ALU.max,
                                        apply_absolute_value=True))
    k.emit(v, lambda E: E.tensor_scalar(s, bm, 1.0 / 6.0, 1e-12,
                                        ALU.mult, ALU.max))
    k.emit(v, lambda E: E.reciprocal(r, s))
    rb = r.unsqueeze(2).broadcast_to([128, nb, 16])
    sb = s.unsqueeze(2).broadcast_to([128, nb, 16])
    t3 = t.rearrange("p (b k) -> p b k", k=16)
    k.emit(v, lambda E: E.tensor_tensor(t3, x3, rb, ALU.mult))
    k.emit(v, lambda E: E.tensor_mul(u, t, t))
    k.emit(v, lambda E: E.tensor_single_scalar(w1, u, 4.0, ALU.is_gt))
    k.emit(v, lambda E: E.tensor_single_scalar(w2, u, 16.0, ALU.is_gt))
    k.emit(v, lambda E: E.scalar_tensor_tensor(dd, w2, 0.5, w1, ALU.mult, ALU.add))
    k.emit(v, lambda E: E.scalar_tensor_tensor(ee, w1, 0.5, w2, ALU.mult, ALU.add))
    w1_free = k.token("vector")
    k.emit(v, lambda E: E.tensor_scalar(dd, dd, -1.0, 2.0, ALU.mult, ALU.add))
    k.emit(v, lambda E: E.tensor_scalar(ee, ee, 0.5, None, ALU.add))
    k.emit(v, lambda E: E.tensor_tensor(u, t, dd, ALU.mult))
    k.emit(v, lambda E: E.tensor_scalar(u, u, MAGIC, -MAGIC, ALU.add, ALU.add))
    k.emit(v, lambda E: E.tensor_tensor(u, u, ee, ALU.mult))
    u3 = u.rearrange("p (b k) -> p b k", k=16)
    o3 = out_ap.rearrange("p (b k) -> p b k", k=16)
    fin = k.emit(v, lambda E: E.tensor_tensor(o3, u3, sb, ALU.mult))
    return fin, w1_free


def emit_ln(k, x_ap, xn_ap, scr, eps=1e-5):
    DD = x_ap.shape[1]
    NP = x_ap.shape[0]
    st = scr["st"][0:NP, :]
    nch = (DD + 511) // 512
    for c in range(nch):
        cw = min(512, DD - c * 512)
        k.emit("vector", lambda E, c=c, cw=cw: E.bn_stats(
            st[:, 6 * c:6 * c + 6], x_ap[:, c * 512:c * 512 + cw]))
    k.emit("vector", lambda E: E.bn_aggr(st[:, 12:14], st[:, 0:6 * nch]))
    k.emit("vector", lambda E: E.tensor_scalar(st[:, 14:15], st[:, 13:14], eps, None, ALU.add))
    vr = k.emit("vector", lambda E: E.reciprocal(st[:, 15:16], st[:, 14:15]))
    k.wait("scalar", vr)
    srs = k.emit("scalar", lambda E: E.activation(st[:, 16:17], st[:, 15:16], AF.Sqrt))
    k.wait("vector", srs)
    vfin = k.emit("vector", lambda E: E.tensor_scalar(xn_ap, x_ap, st[:, 12:13], st[:, 16:17],
                                                      ALU.subtract, ALU.mult))
    return vfin



def build_prep(parts=("gu", "d", "b")):
    nc = bass.Bass("TRN2", target_bir_lowering=False)
    wg = nc.dram_tensor("wg", [I_FFN, 128], F32, kind="ExternalInput").ap()
    wu = nc.dram_tensor("wu", [I_FFN, 128], F32, kind="ExternalInput").ap()
    wd = nc.dram_tensor("wd", [128, I_FFN], F32, kind="ExternalInput").ap()
    cenT = nc.dram_tensor("cenT", [2, 1024], F32, kind="ExternalInput").ap()
    identb = nc.dram_tensor("identb", [128, 128], F32, kind="ExternalInput").ap()

    o_wgqT = nc.dram_tensor("o_wgqT", [128, I_FFN], BF16, kind="ExternalOutput").ap()
    o_wuqT = nc.dram_tensor("o_wuqT", [128, I_FFN], BF16, kind="ExternalOutput").ap()
    o_wdqT = nc.dram_tensor("o_wdqT", [22, 128, 128], BF16, kind="ExternalOutput").ap()
    o_B = nc.dram_tensor("o_B", [8, 128, 1024], F32, kind="ExternalOutput").ap()

    NTI = 22

    with ExitStack() as ctx:
        e = ctx.enter_context
        k = K(nc, ctx)
        sw = e(nc.sbuf_tensor("sw", [128, I_FFN], F32))
        swq = e(nc.sbuf_tensor("swq", [128, I_FFN], BF16))
        sib = e(nc.sbuf_tensor("sib", [128, 128], BF16))
        souT = e(nc.sbuf_tensor("souT", [128, I_FFN], BF16))
        scr = {nm: e(nc.sbuf_tensor(f"scr_{nm}", [128, I_FFN], F32))
               for nm in ("t", "u", "w1", "w2", "dd", "ee")}
        for nm in ("bm", "s", "r"):
            scr[nm] = e(nc.sbuf_tensor(f"scr_{nm}", [128, I_FFN // 16], F32))
        scr["sq"] = scr["t"]
        scen3 = e(nc.sbuf_tensor("scen3", [3, 1024], F32))
        scen23 = e(nc.sbuf_tensor("scen23", [3, 1024], F32))
        ssq2 = e(nc.sbuf_tensor("ssq2", [2, 1024], F32))
        sn2r = e(nc.sbuf_tensor("sn2r", [1, 1024], F32))
        sones2 = e(nc.sbuf_tensor("sones2", [2, 1], F32))
        sones1 = e(nc.sbuf_tensor("sones1", [1, 128], F32))
        sn2c = e(nc.sbuf_tensor("sn2c", [128, 8], F32))
        sdist = e(nc.sbuf_tensor("sdist", [128, 8 * 1024], F32))
        smax = e(nc.sbuf_tensor("smax", [128, 10], F32))
        sfac = e(nc.sbuf_tensor("sfac", [1, 4], F32))
        sfac128 = e(nc.sbuf_tensor("sfac128", [128, 1], F32))
        srow = e(nc.sbuf_tensor("srow", [1, 128], F32))
        sone1 = e(nc.sbuf_tensor("sone1", [1, 1], F32))
        ps_a = e(nc.psum_tensor("ps_a", [128, 512], F32))
        ps_b = e(nc.psum_tensor("ps_b", [128, 512], F32))
        ps_t0 = e(nc.psum_tensor("ps_t0", [128, 1024], BF16))
        ps_t1 = e(nc.psum_tensor("ps_t1", [128, 1024], BF16))
        dscr = nc.dram_tensor("dscr", [128], F32, kind="Internal").ap()

        c_id = k.dma("gpsimd", lambda E: E.dma_start(out=sib[:], in_=identb), key="id")

        if "gu" not in parts:
            wparts = []
        else:
            wparts = None
        wg_t = wg.rearrange("(n p) m -> n p m", p=128)
        wu_t = wu.rearrange("(n p) m -> n p m", p=128)
        for part, (src, outT) in enumerate([(wg_t, o_wgqT), (wu_t, o_wuqT)] if wparts is None else []):
            ld = [None] * NTI
            vq = [None] * NTI
            for i in range(NTI):
                if i >= 4:
                    k.wait("sync", vq[i - 4])
                ld[i] = k.dma("sync", lambda E, i=i, src=src: E.dma_start(
                    out=sw[:, i * 128:(i + 1) * 128], in_=src[i]), key=f"w{i % 4}")
                k.wait("vector", ld[i])
                sl = slice(i * 128, (i + 1) * 128)
                sub = {nm: scr[nm][:, 0:128] for nm in ("t", "u", "w1", "w2", "dd", "ee")}
                sub.update({nm: scr[nm][:, 0:8] for nm in ("bm", "s", "r")})
                vq[i], _ = emit_fp4_qd(k, sw[:, sl], swq[:, sl], sub, 128)
            for i in range(NTI):
                k.wait("tensor", vq[i])
                k.wait("tensor", c_id)
                pst = ps_t0 if i % 2 == 0 else ps_t1
                if i >= 2:
                    k.wait("tensor", k.token("vector"))
                tprev = k.emit("tensor", lambda E, i=i, pst=pst: E.transpose(
                    pst[:, 0:128], swq[:, i * 128:(i + 1) * 128], sib[:]))
                k.wait("vector", tprev)
                k.emit("vector", lambda E, i=i, pst=pst: E.tensor_copy(
                    souT[:, i * 128:(i + 1) * 128], pst[:, 0:128]))
            k.barrier()
            st = k.dma("sync", lambda E, outT=outT: E.dma_start(out=outT, in_=souT[:]), key="store")
            k.wait("sync", st)
            k.barrier()

        if "d" not in parts:
            k.barrier(); k.finalize(); return nc
        ldd = k.dma("sync", lambda E: E.dma_start(out=sw[:], in_=wd), key="wd")
        k.wait("vector", ldd)
        vq, _ = emit_fp4_qd(k, sw[:], swq[:], scr, I_FFN)
        k.wait("tensor", vq)
        for i in range(NTI):
            pst = ps_t0 if i % 2 == 0 else ps_t1
            if i >= 2:
                k.wait("tensor", k.token("vector"))
            t = k.emit("tensor", lambda E, i=i, pst=pst: E.transpose(
                pst[:, 0:128], swq[:, i * 128:(i + 1) * 128], sib[:]))
            k.wait("vector", t)
            k.emit("vector", lambda E, i=i, pst=pst: E.tensor_copy(
                souT[:, i * 128:(i + 1) * 128], pst[:, 0:128]))
        k.barrier()
        souT_v = souT[:].rearrange("p (n m) -> p n m", m=128)
        st = k.dma("sync", lambda E: E.dma_start(out=o_wdqT.rearrange("n p m -> p n m"), in_=souT_v), key="store")
        k.wait("sync", st)
        k.barrier()

        if "b" not in parts:
            k.finalize(); return nc
        lc = k.dma("sync", lambda E: E.dma_start(out=scen3[0:2, :], in_=cenT), key="cen")
        k.wait("vector", lc)
        k.emit("vector", lambda E: E.memset(sones2[:], 1.0))
        k.emit("vector", lambda E: E.memset(sn2r[:], 1.0))
        vms = k.emit("vector", lambda E: E.memset(sones1[:], 1.0))
        k.wait("gpsimd", vms)
        gons = k.dma("gpsimd", lambda E: E.dma_start(out=scen23[2:3, :], in_=sn2r[:]), key="ones")
        k.emit("vector", lambda E: E.tensor_scalar(scen23[0:2, :], scen3[0:2, :], -2.0, None, ALU.mult))
        vsq = k.emit("vector", lambda E: E.tensor_mul(ssq2[:], scen3[0:2, :], scen3[0:2, :]))
        k.wait("tensor", vsq)
        k.emit("tensor", lambda E: E.matmul(ps_a[0:1, 0:512], sones2[:], ssq2[:, 0:512], start=True, stop=True))
        tn2 = k.emit("tensor", lambda E: E.matmul(ps_b[0:1, 0:512], sones2[:], ssq2[:, 512:1024], start=True, stop=True))
        k.wait("vector", tn2)
        k.wait("vector", gons)
        k.emit("vector", lambda E: E.tensor_copy(sn2r[0:1, 0:512], ps_a[0:1, 0:512]))
        vn2 = k.emit("vector", lambda E: E.tensor_copy(sn2r[0:1, 512:1024], ps_b[0:1, 0:512]))
        k.wait("gpsimd", vn2)
        gn2 = k.dma("gpsimd", lambda E: E.dma_start(out=scen3[2:3, :], in_=sn2r[:]), key="n2")
        k.wait("tensor", gn2)
        k.wait("tensor", vn2)
        for ti in range(8):
            k.emit("tensor", lambda E, ti=ti: E.matmul(
                ps_b[:, ti:ti + 1], sn2r[0:1, ti * 128:(ti + 1) * 128], sones1[0:1, 0:1],
                start=True, stop=True))
        k.wait("vector", k.token("tensor"))
        vn2c = k.emit("vector", lambda E: E.tensor_copy(sn2c[:], ps_b[:, 0:8]))
        vtile = vn2c
        for ti in range(8):
            sl = slice(ti * 128, (ti + 1) * 128)
            k.wait("tensor", vtile)
            k.emit("tensor", lambda E, sl=sl: E.matmul(ps_a[:, 0:512], scen23[:, sl], scen3[:, 0:512], start=True, stop=True))
            t2 = k.emit("tensor", lambda E, sl=sl: E.matmul(ps_b[:, 0:512], scen23[:, sl], scen3[:, 512:1024], start=True, stop=True))
            k.wait("vector", t2)
            d0 = ti * 1024
            k.emit("vector", lambda E, d0=d0, ti=ti: E.tensor_scalar(
                sdist[:, d0:d0 + 512], ps_a[:, 0:512], sn2c[:, ti:ti + 1], 0.0, ALU.add, ALU.max))
            vtile = k.emit("vector", lambda E, d0=d0, ti=ti: E.tensor_scalar(
                sdist[:, d0 + 512:d0 + 1024], ps_b[:, 0:512], sn2c[:, ti:ti + 1], 0.0, ALU.add, ALU.max))
            k.wait("scalar", vtile)
            k.emit("scalar", lambda E, d0=d0: E.activation(
                sdist[:, d0:d0 + 1024], sdist[:, d0:d0 + 1024], AF.Sqrt))
        k.barrier()
        k.emit("vector", lambda E: E.reduce_max(smax[:, 0:1], sdist[:], AX.X))
        k.barrier()
        st = k.dma("sync", lambda E: E.dma_start(out=dscr, in_=smax[:, 0:1]), key="dsw")
        k.wait("sync", st)
        lr = k.dma("sync", lambda E: E.dma_start(out=srow[:], in_=dscr.rearrange("(a b) -> a b", a=1)), key="dsr")
        k.wait("vector", lr)
        k.emit("vector", lambda E: E.reduce_max(sfac[:, 0:1], srow[:], AX.X))
        k.emit("vector", lambda E: E.reciprocal(sfac[:, 1:2], sfac[:, 0:1]))
        k.emit("vector", lambda E: E.tensor_scalar(sfac[:, 2:3], sfac[:, 0:1], 1e-6, None, ALU.is_gt))
        vf = k.emit("vector", lambda E: E.tensor_scalar(
            sfac[:, 1:2], sfac[:, 1:2], sfac[0:1, 2:3], -0.1, ALU.mult, ALU.mult))
        k.wait("tensor", vf)
        tf = k.emit("tensor", lambda E: E.matmul(ps_a[:, 0:1], sones1[0:1, :], sfac[0:1, 1:2], start=True, stop=True))
        k.wait("vector", tf)
        k.emit("vector", lambda E: E.tensor_copy(sfac128[:], ps_a[:, 0:1]))
        for ti in range(8):
            d0 = ti * 1024
            k.emit("vector", lambda E, d0=d0: E.tensor_scalar(
                sdist[:, d0:d0 + 1024], sdist[:, d0:d0 + 1024], sfac128[:, 0:1], None, ALU.mult))
        k.barrier()
        sdist_v = sdist[:].rearrange("p (n m) -> p n m", m=1024)
        st = k.dma("sync", lambda E: E.dma_start(out=o_B.rearrange("n p m -> p n m"), in_=sdist_v), key="store")
        k.wait("sync", st)
        k.barrier()
        k.finalize()
    return nc



FFN_CHUNKS = [(0, 512), (512, 512), (1024, 512), (1536, 512), (2048, 512), (2560, 256)]


def build_l0ffn(stage="full"):
    nc = bass.Bass("TRN2", target_bir_lowering=False)
    x_in = nc.dram_tensor("x", [TOK, D], F32, kind="ExternalInput").ap()
    wgqT = nc.dram_tensor("wgqT", [D, I_FFN], BF16, kind="ExternalInput").ap()
    wuqT = nc.dram_tensor("wuqT", [D, I_FFN], BF16, kind="ExternalInput").ap()
    wdqT = nc.dram_tensor("wdqT", [I_FFN, D], BF16, kind="ExternalInput").ap()
    identb = nc.dram_tensor("identb", [128, 128], F32, kind="ExternalInput").ap()
    y_out = nc.dram_tensor("y", [TOK, D], F32, kind="ExternalOutput").ap()

    x_t = x_in.rearrange("(n p) m -> n p m", p=128)
    y_t = y_out.rearrange("(n p) m -> n p m", p=128)

    with ExitStack() as ctx:
        e = ctx.enter_context
        k = K(nc, ctx)
        swg = e(nc.sbuf_tensor("swg", [128, 8 * I_FFN], BF16))
        swu = e(nc.sbuf_tensor("swu", [128, 8 * I_FFN], BF16))
        swd = e(nc.sbuf_tensor("swd", [128, 22 * D], BF16))
        sib = e(nc.sbuf_tensor("sib", [128, 128], BF16))
        sxqT = e(nc.sbuf_tensor("sxqT", [128, 8 * TOK], BF16))
        sx = [e(nc.sbuf_tensor(f"sx{i}", [128, D], F32)) for i in range(2)]
        sx2 = sx
        sxn = e(nc.sbuf_tensor("sxn", [128, D], F32))
        sq_bf = e(nc.sbuf_tensor("sq_bf", [128, D], BF16))
        scr = {nm: e(nc.sbuf_tensor(f"scr_{nm}", [128, D], F32))
               for nm in ("u", "w1", "w2", "dd", "ee")}
        scr["t"] = e(nc.sbuf_tensor("scr_t", [128, 1536], F32))
        scr["sq"] = scr["dd"]
        for nm in ("bm", "s", "r"):
            scr[nm] = e(nc.sbuf_tensor(f"scr_{nm}", [128, 64], F32))
        sst = e(nc.sbuf_tensor("sst", [128, 18], F32))
        scr["st"] = sst
        ssg = scr["w1"]
        sh = scr["ee"]
        shqT = [e(nc.sbuf_tensor(f"shqT{i}", [128, 512], BF16)) for i in range(2)]
        shq = [sq_bf[:, 0:512], sq_bf[:, 512:1024]]
        sy = [sxn, scr["t"][:, 512:1536]]
        ps_g = [e(nc.psum_tensor(f"ps_g{i}", [128, 512], F32)) for i in range(2)]
        ps_u = [e(nc.psum_tensor(f"ps_u{i}", [128, 512], F32)) for i in range(2)]
        ps_o = [e(nc.psum_tensor(f"ps_o{i}", [128, 512], F32)) for i in range(2)]
        ps_t = [e(nc.psum_tensor(f"ps_t{i}", [128, 1024], BF16)) for i in range(2)]

        lwg = k.dma("sync", lambda E: E.dma_start(out=swg[:].rearrange("p (n m) -> p n m", m=I_FFN),
                                                  in_=wgqT.rearrange("(n p) m -> p n m", p=128)), key="wg")
        lwu = k.dma("sync", lambda E: E.dma_start(out=swu[:].rearrange("p (n m) -> p n m", m=I_FFN),
                                                  in_=wuqT.rearrange("(n p) m -> p n m", p=128)), key="wu")
        lwd = k.dma("sync", lambda E: E.dma_start(out=swd[:].rearrange("p (n m) -> p n m", m=D),
                                                  in_=wdqT.rearrange("(n p) m -> p n m", p=128)), key="wd")
        lid = k.dma("gpsimd", lambda E: E.dma_start(out=sib[:], in_=identb), key="id")

        vq = [None] * NTT
        vcp = [None] * NTT
        ld = [None] * NTT
        ttr = [None] * NTT
        for t in range(NTT):
            if t >= 2:
                k.wait("sync", vq[t - 2])
            ld[t] = k.dma("sync", lambda E, t=t: E.dma_start(out=sx[t % 2][:], in_=x_t[t]),
                          key=f"x{t % 2}")
            k.wait("vector", ld[t])
            k.wait("scalar", ld[t])
            if t >= 1:
                k.wait("scalar", vq[t - 1])
            emit_ln(k, sx[t % 2][:], sxn[:], scr)
            vq[t], _ = emit_fp4_qd(k, sxn[:], sq_bf[:], scr, D)
            k.wait("tensor", vq[t])
            if t == 0:
                k.wait("tensor", lid)
            if t >= 2:
                k.wait("tensor", vcp[t - 2])
            for dt in range(8):
                ttr[t] = k.emit("tensor", lambda E, t=t, dt=dt: E.transpose(
                    ps_t[t % 2][:, dt * 128:(dt + 1) * 128],
                    sq_bf[:, dt * 128:(dt + 1) * 128], sib[:]))
            k.wait("vector", ttr[t])
            dst = sxqT[:].rearrange("p (n m) -> p n m", m=TOK)[:, :, t * 128:(t + 1) * 128]
            src = ps_t[t % 2][:].rearrange("p (n m) -> p n m", m=128)
            vcp[t] = k.emit("vector", lambda E, dst=dst, src=src: E.tensor_copy(dst, src))

        if stage == "A":
            k.barrier()
            sA = k.dma("sync", lambda E: E.dma_start(out=y_t[0], in_=sx[0][:]), key="ya")
            k.wait("sync", sA)
            k.barrier()
            k.finalize()
            return nc
        k.wait("tensor", lwg)
        k.wait("tensor", lwu)
        k.wait("tensor", lwd)

        tg = [None, None]
        tu = [None, None]
        tdn_tok = [None]
        vhq_tok = [None, None]
        vhq_prev = None
        vh_last = [None, None]
        ssil_last = [None, None]
        vhT_last = [None, None]
        ttr_done = [None, None]
        vy = [None, None]
        st_y = [None, None]
        last_ci = len(FFN_CHUNKS) - 1

        def emit_transpose_down(ci, c0, cw):
            b = ci % 2
            pt = ps_t[b]
            nt = cw // 128
            k.wait("tensor", vhq_tok[b])
            k.wait("tensor", vhT_last[b])
            tt = None
            for it in range(nt):
                tt = k.emit("tensor", lambda E, it=it, pt=pt, b=b: E.transpose(
                    pt[:, it * 128:(it + 1) * 128],
                    shq[b][:, it * 128:(it + 1) * 128], sib[:]))
            ttr_done[b] = tt
            k.wait("vector", tt)
            vhT = k.emit("vector", lambda E, nt=nt, pt=pt, b=b: E.tensor_copy(
                shqT[b][:, 0:nt * 128], pt[:, 0:nt * 128]))
            vhT_last[b] = vhT
            k.wait("tensor", vhT)
            if ci == 0:
                k.wait("tensor", vy[0])
                k.wait("tensor", vy[1])
            for it in range(nt):
                iti = c0 // 128 + it
                for half in range(2):
                    tdn_tok[0] = k.emit("tensor", lambda E, it=it, iti=iti, half=half, ci=ci, nt=nt, b=b: E.matmul(
                        ps_o[half][:, 0:512],
                        shqT[b][:, it * 128:(it + 1) * 128],
                        swd[:, iti * D + half * 512: iti * D + half * 512 + 512],
                        start=(ci == 0 and it == 0), stop=(ci == last_ci and it == nt - 1)))

        for t in range(NTT):
            if t < 2:
                k.wait("sync", vq[NTT - 2 + t])
            if t >= 2:
                k.wait("sync", vy[t % 2])
            ld2 = k.dma("sync", lambda E, t=t: E.dma_start(out=sx2[t % 2][:], in_=x_t[t]),
                        key=f"x2{t % 2}")
            pend = []
            for ci, (c0, cw) in enumerate(FFN_CHUNKS):
                pg, pu = ps_g[ci % 2], ps_u[ci % 2]
                if t == 0 and ci < 2:
                    k.wait("tensor", vcp[NTT - 1])
                k.wait("tensor", ssil_last[ci % 2])
                for dt in range(8):
                    lhs = sxqT[:, dt * TOK + t * 128: dt * TOK + (t + 1) * 128]
                    tg[ci % 2] = k.emit("tensor", lambda E, dt=dt, c0=c0, cw=cw, lhs=lhs, pg=pg: E.matmul(
                        pg[:, 0:cw], lhs, swg[:, dt * I_FFN + c0: dt * I_FFN + c0 + cw],
                        start=(dt == 0), stop=(dt == 7)))
                k.wait("tensor", vh_last[ci % 2])
                for dt in range(8):
                    lhs = sxqT[:, dt * TOK + t * 128: dt * TOK + (t + 1) * 128]
                    tu[ci % 2] = k.emit("tensor", lambda E, dt=dt, c0=c0, cw=cw, lhs=lhs, pu=pu: E.matmul(
                        pu[:, 0:cw], lhs, swu[:, dt * I_FFN + c0: dt * I_FFN + c0 + cw],
                        start=(dt == 0), stop=(dt == 7)))
                k.wait("scalar", tg[ci % 2])
                k.wait("scalar", vhq_prev)
                ssil = k.emit("scalar", lambda E, cw=cw, pg=pg: E.activation(
                    ssg[:, 0:cw], pg[:, 0:cw], AF.Silu))
                ssil_last[ci % 2] = ssil
                k.wait("vector", ssil)
                k.wait("vector", tu[ci % 2])
                vh = k.emit("vector", lambda E, cw=cw, pu=pu: E.tensor_tensor(
                    sh[:, 0:cw], ssg[:, 0:cw], pu[:, 0:cw], ALU.mult))
                vh_last[ci % 2] = vh
                k.wait("vector", ttr_done[ci % 2])
                sub = {nm: scr[nm][:, 0:cw] for nm in ("t", "u", "w1", "w2", "dd", "ee")}
                sub.update({nm: scr[nm][:, 0:cw // 16] for nm in ("bm", "s", "r")})
                vhq, w1f = emit_fp4_qd(k, sh[:, 0:cw], shq[ci % 2][:, 0:cw], sub, cw)
                vhq_prev = w1f
                vhq_tok[ci % 2] = vhq
                if pend:
                    emit_transpose_down(*pend.pop(0))
                pend.append((ci, c0, cw))
            while pend:
                emit_transpose_down(*pend.pop(0))
            k.wait("vector", tdn_tok[0])
            k.wait("vector", ld2)
            if t >= 2:
                k.wait("vector", st_y[t % 2])
            for half in range(2):
                k.emit("vector", lambda E, half=half, t=t: E.tensor_scalar(
                    sy[t % 2][:, half * 512:(half + 1) * 512], ps_o[half][:, 0:512],
                    0.5, None, ALU.mult))
            for half in range(2):
                vy[t % 2] = k.emit("vector", lambda E, half=half, t=t: E.tensor_add(
                    sy[t % 2][:, half * 512:(half + 1) * 512],
                    sy[t % 2][:, half * 512:(half + 1) * 512],
                    sx2[t % 2][:, half * 512:(half + 1) * 512]))
            k.wait("sync", vy[t % 2])
            st_y[t % 2] = k.dma("sync", lambda E, t=t: E.dma_start(out=y_t[t], in_=sy[t % 2][:]),
                                key=f"y{t % 2}")
        if stage in ("B1", "B2", "B3"):
            k.barrier()
            sA = k.dma("sync", lambda E: E.dma_start(out=y_t[0], in_=sx[0][:]), key="ya")
            k.wait("sync", sA)
            k.barrier()
            k.finalize()
            return nc
        k.wait("sync", st_y[0])
        k.wait("sync", st_y[1])
        k.barrier()
        k.finalize()
    return nc



def build_l0attn(tok=TOK, seq=SEQ):
    H = 16
    HD = 64
    nmt = seq // 128
    nqc = tok // 512
    nmg = seq // 512
    nbias = min(nmt, 9)
    VW = H * 65

    nc = bass.Bass("TRN2", target_bir_lowering=False, dynamic_dma_scratch_size=8192)
    x_own = nc.dram_tensor("x_own", [tok, D], F32, kind="ExternalInput").ap()
    x_full = nc.dram_tensor("x_full", [seq, D], F32, kind="ExternalInput").ap()
    wqkvT = nc.dram_tensor("wqkvT", [D, E3], F32, kind="ExternalInput").ap()
    wprojT = nc.dram_tensor("wprojT", [D, D], F32, kind="ExternalInput").ap()
    gb = nc.dram_tensor("gb", [128, D], F32, kind="ExternalInput").ap()
    bbb = nc.dram_tensor("bbb", [128, D], F32, kind="ExternalInput").ap()
    bpb = nc.dram_tensor("bpb", [128, D], F32, kind="ExternalInput").ap()
    Bq = nc.dram_tensor("Bq", [nqc, nbias, 128, 512], F32, kind="ExternalInput").ap()
    identb = nc.dram_tensor("identb", [128, 128], F32, kind="ExternalInput").ap()
    y_out = nc.dram_tensor("y", [tok, D], F32, kind="ExternalOutput").ap()

    xf_t = x_full.rearrange("(n p) m -> n p m", p=128)
    xo_t = x_own.rearrange("(n p) m -> n p m", p=128)
    y_t = y_out.rearrange("(n p) m -> n p m", p=128)
    SCALE = 1.0 / 8.0

    with ExitStack() as ctx:
        e = ctx.enter_context
        k = K(nc, ctx)
        sKT = e(nc.sbuf_tensor("sKT", [128, 8 * seq], BF16))
        sV = e(nc.sbuf_tensor("sV", [128, nmt * VW], BF16))
        sQT = e(nc.sbuf_tensor("sQT", [128, 8 * tok], BF16))
        sAT = e(nc.sbuf_tensor("sAT", [128, 8 * tok], BF16))
        sxnTg = e(nc.sbuf_tensor("sxnTg", [128, max(8 * 512, nbias * 512)], BF16))
        sx = [e(nc.sbuf_tensor(f"sx{i}", [128, D], F32)) for i in range(2)]
        sxnb = e(nc.sbuf_tensor("sxnb", [128, D], BF16))
        sptx = e(nc.sbuf_tensor("sptx", [128, 1024], BF16))
        sst = e(nc.sbuf_tensor("sst", [128, 18], F32))
        sib = e(nc.sbuf_tensor("sib", [128, 128], BF16))
        sone = e(nc.sbuf_tensor("sone", [128, 1], F32))
        sorow = e(nc.sbuf_tensor("sorow", [1, 128], F32))
        sonesb = e(nc.sbuf_tensor("sonesb", [128, 128], BF16))
        skn2 = e(nc.sbuf_tensor("skn2", [128, 9], F32))
        cscr = nc.dram_tensor("cscr", [16, tok], F32, kind="Internal").ap()
        if seq == 4096:
            sw = sAT[:, 0:16384]
            ssq_big = sAT[:, 0:8192].bitcast(F32)
            sw4 = sKT[:, 0:8 * 1024]
            sgb = sKT[:, 8192:10240].bitcast(F32)
            sbbb = sKT[:, 10240:12288].bitcast(F32)
            sy = sKT[:, 12288:14336].bitcast(F32)
            sxn = sKT[:, 14336:16384].bitcast(F32)
            sxn2 = sKT[:, 16384:18432].bitcast(F32)
        else:
            sw = e(nc.sbuf_tensor("sw", [128, 16384], BF16))[:, 0:16384]
            ssq_big = e(nc.sbuf_tensor("ssqb", [128, 4096], F32))[:, 0:4096]
            sw4 = sw[:, 0:8 * 1024]
            sgb = e(nc.sbuf_tensor("sgbt", [128, D], F32))[:, :]
            sbbb = e(nc.sbuf_tensor("sbbbt", [128, D], F32))[:, :]
            sy = e(nc.sbuf_tensor("syt", [128, D], F32))[:, :]
            sxn = e(nc.sbuf_tensor("sxnt", [128, D], F32))[:, :]
            sxn2 = e(nc.sbuf_tensor("sxn2t", [128, D], F32))[:, :]
        sbias = sxnTg[:, 0:nbias * 512]
        szrow = sx[0][0:1, 512:1024]
        snrm = sx[1][0:1, 500:1020]
        ssc = sx[0][:, 0:512]
        sPt = [sx[1][:, 0:256].bitcast(BF16), sx[1][:, 256:512].bitcast(BF16)]
        szl = sx[1][0:64, 512:1024]
        szi = sx[1][64:128, 512:1024]
        ps_s = [e(nc.psum_tensor(f"ps_s{i}", [128, 512], F32)) for i in range(2)]
        ps_a = [e(nc.psum_tensor(f"ps_a{i}", [128, 512], F32)) for i in range(2)]
        ps_z = e(nc.psum_tensor("ps_z", [128, 512], F32))
        ps_t = [e(nc.psum_tensor(f"ps_t{i}", [128, 1024], BF16)) for i in range(2)]

        lid = k.dma("gpsimd", lambda E: E.dma_start(out=sib[:], in_=identb), key="id")

        xfree = [None, None]
        xldlast = [None, None]

        def ln_transpose_group(src_t, g, ld_prev_tok):
            vtok = None
            for j in range(4):
                t = 4 * g + j
                k.wait("sync", xfree[j % 2])
                ldx = k.dma("sync", lambda E, t=t, j=j, src_t=src_t: E.dma_start(
                    out=sx[j % 2][:], in_=xf_t[t] if src_t is xf_t else xo_t[t]),
                    key=f"x{j % 2}")
                xldlast[j % 2] = ldx
                k.wait("vector", ldx)
                k.wait("scalar", ldx)
                emit_ln(k, sx[j % 2][:], sxnb[:], {"st": sst})
                xfree[j % 2] = k.token("vector")
                vln = k.token("vector")
                k.wait("tensor", vln)
                if g == 0 and j == 0:
                    k.wait("tensor", lid)
                ttr = None
                for dt in range(8):
                    ttr = k.emit("tensor", lambda E, dt=dt, j=j: E.transpose(
                        ps_t[j % 2][:, dt * 128:(dt + 1) * 128],
                        sxnb[:, dt * 128:(dt + 1) * 128], sib[:]))
                k.wait("vector", ttr)
                dst = sxnTg[:, 0:4096].rearrange("p (n m) -> p n m", m=512)[:, :, j * 128:(j + 1) * 128]
                srcp = ps_t[j % 2][:].rearrange("p (n m) -> p n m", m=128)
                vtok = k.emit("vector", lambda E, dst=dst, srcp=srcp: E.tensor_copy(dst, srcp))
                if j < 3:
                    k.wait("tensor", vtok)
            return vtok

        lwk = k.dma("gpsimd", lambda E: E.dma_start(
            out=sw[:].rearrange("p (n m) -> p n m", m=2048),
            in_=wqkvT[:, 1024:3072].rearrange("(n p) m -> p n m", p=128)), key="wk")
        vsetones = k.emit("vector", lambda E: E.memset(
            sV[:].rearrange("p (n m) -> p n m", m=65)[:, :, 64:65], 1.0))
        prev_group_done = [None]
        cp_s = [None, None]
        cp_a = [None, None]
        for g in range(nmg):
            vg = ln_transpose_group(xf_t, g, None)
            k.wait("tensor", vg)
            k.wait("tensor", lwk)
            for et in range(8):
                k.wait("tensor", cp_s[et % 2])
                tm = None
                for dt in range(8):
                    tm = k.emit("tensor", lambda E, et=et, dt=dt: E.matmul(
                        ps_s[et % 2][:, 0:512],
                        sw[:, dt * 2048 + et * 128: dt * 2048 + et * 128 + 128],
                        sxnTg[:, dt * 512: dt * 512 + 512],
                        start=(dt == 0), stop=(dt == 7)))
                k.wait("vector", tm)
                vev = k.emit("vector", lambda E, et=et, g=g: E.tensor_copy(
                    sKT[:, et * seq + g * 512: et * seq + g * 512 + 512], ps_s[et % 2][:, 0:512]))
                cp_s[et % 2] = vev
            for j in range(4):
                t = 4 * g + j
                for ch in range(2):
                    k.wait("tensor", cp_a[ch])
                    tm = None
                    for dt in range(8):
                        tm = k.emit("tensor", lambda E, j=j, ch=ch, dt=dt: E.matmul(
                            ps_a[ch][:, 0:512],
                            sxnTg[:, dt * 512 + j * 128: dt * 512 + j * 128 + 128],
                            sw[:, dt * 2048 + 1024 + ch * 512: dt * 2048 + 1024 + ch * 512 + 512],
                            start=(dt == 0), stop=(dt == 7)))
                    k.wait("vector", tm)
                    dstv = sV[:].rearrange("p (n m) -> p n m", m=65)[
                        :, t * 16 + ch * 8:t * 16 + (ch + 1) * 8, 0:64]
                    srcv = ps_a[ch][:, 0:512].rearrange("p (n m) -> p n m", m=64)
                    vev = k.emit("vector", lambda E, dstv=dstv, srcv=srcv: E.tensor_copy(dstv, srcv))
                    cp_a[ch] = vev
            prev_group_done[0] = k.token("vector")

        k.wait("gpsimd", prev_group_done[0])
        k.wait("gpsimd", lwk)
        lwq = k.dma("gpsimd", lambda E: E.dma_start(
            out=sw[:, 0:8 * 1024].rearrange("p (n m) -> p n m", m=1024),
            in_=wqkvT[:, 0:1024].rearrange("(n p) m -> p n m", p=128)), key="wk")
        for g in range(tok // 512):
            vg = ln_transpose_group(xo_t, g, None)
            k.wait("tensor", vg)
            k.wait("tensor", lwq)
            for et in range(8):
                k.wait("tensor", cp_s[et % 2])
                tm = None
                for dt in range(8):
                    tm = k.emit("tensor", lambda E, et=et, dt=dt: E.matmul(
                        ps_s[et % 2][:, 0:512],
                        sw[:, dt * 1024 + et * 128: dt * 1024 + et * 128 + 128],
                        sxnTg[:, dt * 512: dt * 512 + 512],
                        start=(dt == 0), stop=(dt == 7)))
                k.wait("vector", tm)
                vev = k.emit("vector", lambda E, et=et, g=g: E.tensor_copy(
                    sQT[:, et * tok + g * 512: et * tok + g * 512 + 512], ps_s[et % 2][:, 0:512]))
                cp_s[et % 2] = vev

        k.wait("scalar", k.token("vector"))
        k.emit("vector", lambda E: E.memset(sone[:], 1.0))
        k.emit("vector", lambda E: E.memset(sorow[:], 1.0))
        k.emit("vector", lambda E: E.memset(sonesb[:], 1.0))
        k.emit("vector", lambda E: E.memset(skn2[:], 0.0))
        k.wait("tensor", k.token("vector"))
        ssq_k = ssq_big[:, 0:seq]
        last_tm = [None]
        for et in range(8):
            k.wait("scalar", last_tm[0])
            sq_t = k.emit("scalar", lambda E, et=et: E.activation(
                ssq_k, sKT[:, et * seq:(et + 1) * seq], AF.Square))
            k.wait("tensor", sq_t)
            for ch in range(seq // 512):
                for hb, base in ((0, 0), (1, 64)):
                    tm = k.emit("tensor", lambda E, ch=ch, base=base: E.matmul(
                        ps_z[base:base + 1, 0:512], sone[base:base + 64, 0:1],
                        ssq_big[base:base + 64, ch * 512:(ch + 1) * 512],
                        start=True, stop=True))
                    last_tm[0] = tm
                    k.wait("vector", tm)
                    k.emit("vector", lambda E, base=base: E.reduce_max(
                        snrm[0:1, 0:1] if base == 0 else None, None, None)) if False else None
                    k.emit("vector", lambda E, base=base, et=et: E.reduce_max(
                        skn2[base:base + 1, 8:9], ps_z[base:base + 1, 0:512], AX.X))
                    vm = k.emit("vector", lambda E, base=base, et=et: E.tensor_max(
                        skn2[base:base + 1, et:et + 1], skn2[base:base + 1, et:et + 1],
                        skn2[base:base + 1, 8:9]))
                    k.wait("tensor", vm)
        ssq_q = ssq_big[:, 0:tok]
        crow = sAT[:, 2 * tok: 2 * tok + 2 * tok].bitcast(F32)
        st_c = [None]
        for et in range(8):
            k.wait("scalar", last_tm[0])
            k.wait("scalar", k.token("vector"))
            sq_t = k.emit("scalar", lambda E, et=et: E.activation(
                ssq_q, sQT[:, et * tok:(et + 1) * tok], AF.Square))
            k.wait("tensor", sq_t)
            k.wait("tensor", st_c[0])
            for qc in range(tok // 512):
                for hb, base in ((0, 0), (1, 64)):
                    k.wait("tensor", k.token("vector"))
                    tm = k.emit("tensor", lambda E, qc=qc, base=base: E.matmul(
                        ps_z[base:base + 1, 0:512], sone[base:base + 64, 0:1],
                        ssq_q[base:base + 64, qc * 512:(qc + 1) * 512],
                        start=True, stop=True))
                    last_tm[0] = tm
                    k.wait("vector", tm)
                    vv = k.emit("vector", lambda E, qc=qc, base=base, et=et: E.tensor_scalar(
                        crow[base:base + 1, qc * 512:(qc + 1) * 512],
                        ps_z[base:base + 1, 0:512], skn2[base:base + 1, et:et + 1],
                        None, ALU.mult))
                    k.wait("scalar", vv)
                    sv = k.emit("scalar", lambda E, qc=qc, base=base: E.activation(
                        crow[base:base + 1, qc * 512:(qc + 1) * 512],
                        crow[base:base + 1, qc * 512:(qc + 1) * 512], AF.Sqrt))
                    k.wait("vector", sv)
                    k.emit("vector", lambda E, qc=qc, base=base: E.tensor_scalar(
                        crow[base:base + 1, qc * 512:(qc + 1) * 512],
                        crow[base:base + 1, qc * 512:(qc + 1) * 512], -1.0, None, ALU.mult))
            vfin = k.token("vector")
            k.wait("sync", vfin)
            st_c[0] = k.dma("sync", lambda E, et=et: E.dma_start(
                out=cscr[2 * et:2 * et + 2, :],
                in_=crow[0:65:64, 0:tok]), key="cs")
        k.wait("gpsimd", st_c[0])
        p2v = k.token("vector")
        pv_free = [p2v, p2v]
        k.wait("gpsimd", k.token("tensor"))
        k.wait("vector", xldlast[0])
        k.wait("scalar", xldlast[1])
        k.wait("vector", xldlast[1])
        ps_sx = [ps_s[0][:, 0:512], ps_s[1][:, 0:512],
                 ps_t[0][:].bitcast(F32)[:, 0:512], ps_t[1][:].bitcast(F32)[:, 0:512]]
        sPt4 = [sPt[0], sPt[1], sptx[:, 0:512], sptx[:, 512:1024]]
        scnrbufs = [sxnb[:, 0:512], sxnb[:, 512:1024]]
        exp_ring = [p2v] * 4
        pv_ring = [p2v] * 4
        cn_tok = [None, None]
        cn_last_use = [p2v, p2v]
        NIT = 2 * nmt
        LOOK = 2

        def load_cn(qc, hp):
            b = hp % 2
            k.wait("gpsimd", cn_last_use[b])
            cn_tok[b] = k.dma("gpsimd", lambda E, qc=qc, hp=hp, b=b: E.dma_start(
                out=scnrbufs[b][0:65:64, 0:512],
                in_=cscr[2 * hp:2 * hp + 2, qc * 512:(qc + 1) * 512]), key=f"cn{b}")

        for qc in range(nqc):
            k.wait("gpsimd", k.token("vector"))
            lbq = k.dma("gpsimd", lambda E, qc=qc: E.dma_start(
                out=sbias[:].rearrange("p (n m) -> p n m", m=512),
                in_=Bq[qc]), key="bq")
            load_cn(qc, 0)
            for hp in range(8):
                b = hp % 2
                if hp < 7:
                    load_cn(qc, hp + 1)
                scnr = scnrbufs[b]
                k.wait("tensor", cn_tok[b])
                z_read = [None, None]
                sexp_list = [None] * NIT
                pv_list = [None] * NIT

                def emit_score(idx, qc=qc, hp=hp, b=b, scnr=scnr,
                               sexp_list=sexp_list, lbq=lbq):
                    mt, hi = idx // 2, idx % 2
                    st = idx % 4
                    h = 2 * hp + hi
                    base = 64 * (h % 2)
                    lhsK = sKT[base:base + 64,
                               (h // 2) * seq + mt * 128:(h // 2) * seq + (mt + 1) * 128]
                    rhsQ = sQT[base:base + 64,
                               (h // 2) * tok + qc * 512:(h // 2) * tok + (qc + 1) * 512]
                    k.wait("tensor", exp_ring[st])
                    k.emit("tensor", lambda E, lhsK=lhsK, rhsQ=rhsQ, st=st: E.matmul(
                        ps_sx[st], lhsK, rhsQ, start=True, stop=False))
                    tsc = k.emit("tensor", lambda E, st=st, base=base, scnr=scnr: E.matmul(
                        ps_sx[st], sonesb[base:base + 1, :],
                        scnr[base:base + 1, 0:512], start=False, stop=True))
                    cn_last_use[b] = tsc
                    if mt < nbias:
                        sb_ = ssc
                        k.wait("vector", tsc)
                        k.wait("vector", k.token("scalar"))
                        if idx == 0:
                            k.wait("vector", lbq)
                        k.emit("vector", lambda E, st=st, sb_=sb_: E.tensor_scalar(
                            sb_, ps_sx[st], SCALE, None, ALU.mult))
                        exp_ring[st] = k.token("vector")
                        k.emit("vector", lambda E, mt=mt, sb_=sb_: E.tensor_add(
                            sb_, sb_, sbias[:, mt * 512:(mt + 1) * 512]))
                        vv = k.token("vector")
                        k.wait("scalar", vv)
                        k.wait("scalar", pv_ring[st])
                        sexp = k.emit("scalar", lambda E, st=st, sb_=sb_: E.activation(
                            sPt4[st], sb_, AF.Exp, bias=0.0, scale=1.0))
                    else:
                        k.wait("scalar", tsc)
                        k.wait("scalar", pv_ring[st])
                        sexp = k.emit("scalar", lambda E, st=st: E.activation(
                            sPt4[st], ps_sx[st], AF.Exp, bias=0.0, scale=SCALE))
                        exp_ring[st] = sexp
                    sexp_list[idx] = sexp

                def emit_pv(idx, hp=hp, z_read=z_read,
                            sexp_list=sexp_list, pv_list=pv_list):
                    mt, hi = idx // 2, idx % 2
                    st = idx % 4
                    h = 2 * hp + hi
                    k.wait("tensor", sexp_list[idx])
                    if mt == 0:
                        k.wait("tensor", pv_free[hi])
                    lhsV = sV[:].rearrange("p (n m) -> p n m", m=65)[
                        :, mt * 16 + h: mt * 16 + h + 1, 0:65].squeeze(1)
                    tpv = k.emit("tensor", lambda E, lhsV=lhsV, st=st, hi=hi, mt=mt: E.matmul(
                        ps_a[hi][0:65, 0:512], lhsV, sPt4[st],
                        start=(mt == 0), stop=(mt == nmt - 1)))
                    pv_list[idx] = tpv
                    pv_ring[st] = tpv
                    if mt == nmt - 1:
                        z_read[hi] = tpv

                for idx in range(NIT):
                    emit_score(idx)
                    if idx >= LOOK:
                        emit_pv(idx - LOOK)
                for idx in range(NIT - LOOK, NIT):
                    emit_pv(idx)
                for hi in range(2):
                    h = 2 * hp + hi
                    k.wait("vector", z_read[hi])
                    vz = k.emit("vector", lambda E, hi=hi: E.tensor_copy(szrow[:], ps_a[hi][64:65, 0:512]))
                    k.wait("tensor", vz)
                    tz = k.emit("tensor", lambda E: E.matmul(
                        ps_z[0:64, 0:512], sorow[0:1, 0:64],
                        szrow[:], start=True, stop=True))
                    k.wait("scalar", tz)
                    sl1 = k.emit("scalar", lambda E: E.activation(szl[:], ps_z[0:64, 0:512], AF.Ln))
                    k.wait("scalar", sl1)
                    sl2 = k.emit("scalar", lambda E: E.activation(szi[:], szl[:], AF.Exp, scale=-1.0))
                    k.wait("vector", sl2)
                    base = 64 * (h % 2)
                    dsta = sAT[base:base + 64,
                               (h // 2) * tok + qc * 512:(h // 2) * tok + (qc + 1) * 512]
                    pv_free[hi] = k.emit("vector", lambda E, dsta=dsta, hi=hi: E.tensor_tensor(
                        dsta, ps_a[hi][0:64, 0:512], szi[:], ALU.mult))
                    k.wait("tensor", sl1)

        p3v = k.token("vector")
        p3s = k.token("scalar")
        p3t = k.token("tensor")
        k.wait("gpsimd", lwq)
        k.wait("gpsimd", p3v)
        k.wait("gpsimd", p3t)
        k.wait("gpsimd", p3s)
        lwp = k.dma("gpsimd", lambda E: E.dma_start(
            out=sw4.rearrange("p (n m) -> p n m", m=1024),
            in_=wprojT.rearrange("(n p) m -> p n m", p=128)), key="wk")
        k.wait("sync", p3t)
        k.wait("sync", p3v)
        lbp = k.dma("sync", lambda E: E.dma_start(out=sy[:], in_=bpb), key="bp")
        lgb = k.dma("sync", lambda E: E.dma_start(out=sgb[:], in_=gb), key="gb")
        lbb = k.dma("sync", lambda E: E.dma_start(out=sbbb[:], in_=bbb), key="bb")
        k.wait("vector", lbp)
        k.wait("vector", lbb)
        vbb = k.emit("vector", lambda E: E.tensor_add(sbbb[:], sbbb[:], sy[:]))
        st_y = [None, None]
        pa_free = [None, None]
        for t in range(tok // 128):
            if t < 2:
                k.wait("sync", p3v)
                k.wait("sync", p3s)
                k.wait("sync", p3t)
            ldx = k.dma("sync", lambda E, t=t: E.dma_start(out=sx[t % 2][:], in_=xo_t[t]), key=f"x{t % 2}")
            k.wait("vector", ldx)
            k.wait("scalar", ldx)
            if t >= 1:
                k.wait("vector", st_y[(t - 1) % 2])
                k.wait("scalar", st_y[(t - 1) % 2])
            emit_ln(k, sx[t % 2][:], sxn, {"st": sst})
            k.wait("scalar", k.token("vector"))
            emit_ln(k, sxn, sxn2, {"st": sst})
            if t == 0:
                k.wait("vector", lgb)
            k.emit("vector", lambda E: E.tensor_mul(sxn2[:], sxn2[:], sgb[:]))
            vgl = k.emit("vector", lambda E: E.tensor_add(sxn2[:], sxn2[:], sbbb[:]))
            if t == 0:
                k.wait("tensor", lwp)
                k.wait("tensor", k.token("vector"))
            for half in range(2):
                k.wait("tensor", pa_free[half])
                tm = None
                for dt in range(8):
                    tm = k.emit("tensor", lambda E, dt=dt, half=half, t=t: E.matmul(
                        ps_a[half][:, 0:512],
                        sAT[:, dt * tok + t * 128: dt * tok + (t + 1) * 128],
                        sw4[:, dt * 1024 + half * 512: dt * 1024 + half * 512 + 512],
                        start=(dt == 0), stop=(dt == 7)))
                k.wait("vector", tm)
                pa_free[half] = k.emit("vector", lambda E, half=half, t=t: E.tensor_add(
                    sxn[:, half * 512:(half + 1) * 512],
                    ps_a[half][:, 0:512], sx[t % 2][:, half * 512:(half + 1) * 512]))
            vy = k.emit("vector", lambda E: E.tensor_add(sxn[:], sxn[:], sxn2[:]))
            k.wait("sync", vy)
            st_y[t % 2] = k.dma("sync", lambda E, t=t: E.dma_start(out=y_t[t], in_=sxn[:]), key=f"y{t % 2}")
            k.wait("vector", st_y[t % 2]) if t >= 14 else None
        k.wait("sync", st_y[0])
        k.wait("sync", st_y[1])
        k.barrier()
        k.finalize()
    return nc



def build_l1ffn(tok=TOK):
    nc = bass.Bass("TRN2", target_bir_lowering=False)
    x_in = nc.dram_tensor("x", [tok, D], F32, kind="ExternalInput").ap()
    xb_in = nc.dram_tensor("xb", [1, D], F32, kind="ExternalInput").ap()
    wrT = nc.dram_tensor("wrT", [D, D], F32, kind="ExternalInput").ap()
    wkT = nc.dram_tensor("wkT", [D, D], F32, kind="ExternalInput").ap()
    wvT = nc.dram_tensor("wvT", [D, D], F32, kind="ExternalInput").ap()
    mur = nc.dram_tensor("mur", [128, 8], F32, kind="ExternalInput").ap()
    muk = nc.dram_tensor("muk", [128, 8], F32, kind="ExternalInput").ap()
    identb = nc.dram_tensor("identb", [128, 128], F32, kind="ExternalInput").ap()
    y_out = nc.dram_tensor("y", [tok, D], F32, kind="ExternalOutput").ap()
    x_t = x_in.rearrange("(n p) m -> n p m", p=128)
    y_t = y_out.rearrange("(n p) m -> n p m", p=128)
    ntt = tok // 128
    dscr = nc.dram_tensor("dscr", [D], F32, kind="Internal").ap()

    with ExitStack() as ctx:
        e = ctx.enter_context
        k = K(nc, ctx)
        sxnT = e(nc.sbuf_tensor("sxnT", [128, 8 * tok], BF16))
        sxrT = e(nc.sbuf_tensor("sxrT", [128, 8 * tok], BF16))
        sxkT = e(nc.sbuf_tensor("sxkT", [128, 8 * tok], BF16))
        skaT = e(nc.sbuf_tensor("skaT", [128, 8 * tok], BF16))
        swr = e(nc.sbuf_tensor("swr", [128, 8 * D], BF16))
        swk = e(nc.sbuf_tensor("swk", [128, 8 * D], BF16))
        swv = e(nc.sbuf_tensor("swv", [128, 8 * D], BF16))
        sx = [e(nc.sbuf_tensor(f"sx{i}", [128, D], F32)) for i in range(2)]
        sxnb = e(nc.sbuf_tensor("sxnb", [128, D], BF16))
        sst = e(nc.sbuf_tensor("sst", [128, 18], F32))
        sib = e(nc.sbuf_tensor("sib", [128, 128], BF16))
        smu = e(nc.sbuf_tensor("smu", [128, 16], F32))
        sxbc = e(nc.sbuf_tensor("sxbc", [128, 16], BF16))
        sxbrow = e(nc.sbuf_tensor("sxbrow", [1, D], F32))
        srl = e(nc.sbuf_tensor("srl", [128, 512], BF16))
        sr = e(nc.sbuf_tensor("sr", [128, D], F32))
        sv1 = e(nc.sbuf_tensor("sv1", [128, D], F32))
        sy = e(nc.sbuf_tensor("sy", [128, D], F32))
        ps_a = [e(nc.psum_tensor(f"ps_a{i}", [128, 512], F32)) for i in range(2)]
        ps_b = [e(nc.psum_tensor(f"ps_b{i}", [128, 512], F32)) for i in range(2)]
        ps_t = [e(nc.psum_tensor(f"ps_t{i}", [128, 1024], BF16)) for i in range(2)]

        lid = k.dma("gpsimd", lambda E: E.dma_start(out=sib[:], in_=identb), key="id")
        lmur = k.dma("sync", lambda E: E.dma_start(out=smu[:, 0:8], in_=mur), key="mu")
        lmuk = k.dma("sync", lambda E: E.dma_start(out=smu[:, 8:16], in_=muk), key="mu2")
        lwr = k.dma("gpsimd", lambda E: E.dma_start(
            out=swr[:].rearrange("p (n m) -> p n m", m=D),
            in_=wrT.rearrange("(n p) m -> p n m", p=128)), key="wr")
        lwk = k.dma("gpsimd", lambda E: E.dma_start(
            out=swk[:].rearrange("p (n m) -> p n m", m=D),
            in_=wkT.rearrange("(n p) m -> p n m", p=128)), key="wk2")
        lwv = k.dma("gpsimd", lambda E: E.dma_start(
            out=swv[:].rearrange("p (n m) -> p n m", m=D),
            in_=wvT.rearrange("(n p) m -> p n m", p=128)), key="wv")
        lxb = k.dma("sync", lambda E: E.dma_start(out=sxbrow[:], in_=xb_in), key="xb")

        vcp = [None] * ntt
        ld = [None] * ntt
        for t in range(ntt):
            if t >= 2:
                k.wait("sync", vcp[t - 2])
            ld[t] = k.dma("sync", lambda E, t=t: E.dma_start(out=sx[t % 2][:], in_=x_t[t]), key=f"x{t % 2}")
            k.wait("vector", ld[t])
            emit_ln(k, sx[t % 2][:], sxnb[:], {"st": sst})
            vln = k.token("vector")
            k.wait("tensor", vln)
            if t == 0:
                k.wait("tensor", lid)
            if t >= 2:
                k.wait("tensor", vcp[t - 2])
            ttr = None
            for dt in range(8):
                ttr = k.emit("tensor", lambda E, dt=dt, t=t: E.transpose(
                    ps_t[t % 2][:, dt * 128:(dt + 1) * 128],
                    sxnb[:, dt * 128:(dt + 1) * 128], sib[:]))
            k.wait("vector", ttr)
            dst = sxnT[:].rearrange("p (n m) -> p n m", m=tok)[:, :, t * 128:(t + 1) * 128]
            srcp = ps_t[t % 2][:].rearrange("p (n m) -> p n m", m=128)
            vcp[t] = k.emit("vector", lambda E, dst=dst, srcp=srcp: E.tensor_copy(dst, srcp))
        k.wait("vector", lxb)
        emit_ln(k, sxbrow[:], sxbrow[:], {"st": sst})
        vbr = k.token("vector")
        k.wait("sync", vbr)
        stb = k.dma("sync", lambda E: E.dma_start(out=dscr, in_=sxbrow[:]), key="xb2")
        k.wait("sync", stb)
        k.wait("gpsimd", stb)
        lxc = k.dma("gpsimd", lambda E: E.dma_start(
            out=sxbc[:, 0:8], in_=dscr.rearrange("(n p) -> p n", p=128),
            allow_slow_non_contiguous=True), key="xb3")

        k.wait("vector", lxc)
        k.wait("vector", lmur)
        k.wait("vector", lmuk)
        for which, dstT, mu0 in (("r", sxrT, 0), ("k", sxkT, 8)):
            for dt in range(8):
                xo = dt * tok
                k.emit("vector", lambda E, xo=xo, dstT=dstT: E.tensor_sub(
                    dstT[:, xo + 1: xo + tok], sxnT[:, xo + 1: xo + tok], sxnT[:, xo: xo + tok - 1]))
                k.emit("vector", lambda E, xo=xo, dstT=dstT, mu0=mu0, dt=dt: E.scalar_tensor_tensor(
                    dstT[:, xo + 1: xo + tok], dstT[:, xo + 1: xo + tok],
                    smu[:, mu0 + dt: mu0 + dt + 1], sxnT[:, xo: xo + tok - 1], ALU.mult, ALU.add))
                k.emit("vector", lambda E, xo=xo, dstT=dstT, dt=dt: E.tensor_sub(
                    dstT[:, xo: xo + 1], sxnT[:, xo: xo + 1], sxbc[:, dt: dt + 1]))
                k.emit("vector", lambda E, xo=xo, dstT=dstT, mu0=mu0, dt=dt: E.scalar_tensor_tensor(
                    dstT[:, xo: xo + 1], dstT[:, xo: xo + 1],
                    smu[:, mu0 + dt: mu0 + dt + 1], sxbc[:, dt: dt + 1], ALU.mult, ALU.add))
        vmix = k.token("vector")

        k.wait("tensor", vmix)
        k.wait("tensor", lwk)
        srl_free = [None, None]
        for kt in range(8):
            for ch in range(tok // 512):
                pb = ps_b[ch % 2]
                k.wait("tensor", srl_free[ch % 2])
                tm = None
                for ct in range(8):
                    tm = k.emit("tensor", lambda E, kt=kt, ch=ch, ct=ct, pb=pb: E.matmul(
                        pb[:, 0:512],
                        swk[:, ct * D + kt * 128: ct * D + kt * 128 + 128],
                        sxkT[:, ct * tok + ch * 512: ct * tok + (ch + 1) * 512],
                        start=(ct == 0), stop=(ct == 7)))
                k.wait("vector", tm)
                vrl = k.emit("vector", lambda E, pb=pb: E.tensor_scalar(
                    srl[:], pb[:, 0:512], 0.0, None, ALU.max))
                srl_free[ch % 2] = vrl
                k.wait("scalar", vrl)
                ssqk = k.emit("scalar", lambda E, kt=kt, ch=ch: E.activation(
                    skaT[:, kt * tok + ch * 512: kt * tok + (ch + 1) * 512], srl[:], AF.Square))
                k.wait("vector", ssqk)
                srl_free[ch % 2] = ssqk

        k.wait("tensor", lwr)
        k.wait("tensor", lwv)
        k.wait("tensor", k.token("scalar"))
        st_y = [None, None]
        pr_free = [None, None]
        pv_free = [None, None]
        for t in range(ntt):
            if t < 2:
                k.wait("sync", vcp[ntt - 2 + t])
            else:
                k.wait("sync", st_y[t % 2])
            ld2 = k.dma("sync", lambda E, t=t: E.dma_start(out=sx[t % 2][:], in_=x_t[t]), key=f"x{t % 2}")
            for half in range(2):
                k.wait("tensor", pr_free[half])
                tm = None
                for ct in range(8):
                    tm = k.emit("tensor", lambda E, ct=ct, half=half, t=t: E.matmul(
                        ps_a[half][:, 0:512],
                        sxrT[:, ct * tok + t * 128: ct * tok + (t + 1) * 128],
                        swr[:, ct * D + half * 512: ct * D + half * 512 + 512],
                        start=(ct == 0), stop=(ct == 7)))
                k.wait("scalar", tm)
                k.wait("scalar", pv_free[half])
                pr_free[half] = k.emit("scalar", lambda E, half=half: E.activation(
                    sr[:, half * 512:(half + 1) * 512], ps_a[half][:, 0:512], AF.Sigmoid))
            for half in range(2):
                k.wait("tensor", pv_free[half])
                tm = None
                for ct in range(8):
                    tm = k.emit("tensor", lambda E, ct=ct, half=half, t=t: E.matmul(
                        ps_b[half][:, 0:512],
                        skaT[:, ct * tok + t * 128: ct * tok + (t + 1) * 128],
                        swv[:, ct * D + half * 512: ct * D + half * 512 + 512],
                        start=(ct == 0), stop=(ct == 7)))
                k.wait("vector", tm)
                k.wait("vector", pr_free[half])
                pv_free[half] = k.emit("vector", lambda E, half=half: E.tensor_tensor(
                    sv1[:, half * 512:(half + 1) * 512], sr[:, half * 512:(half + 1) * 512],
                    ps_b[half][:, 0:512], ALU.mult))
            k.wait("vector", ld2)
            if t >= 1:
                k.wait("vector", st_y[(t - 1) % 2])
            vy = k.emit("vector", lambda E, t=t: E.scalar_tensor_tensor(
                sy[:], sv1[:], 0.5, sx[t % 2][:], ALU.mult, ALU.add))
            k.wait("sync", vy)
            st_y[t % 2] = k.dma("sync", lambda E, t=t: E.dma_start(out=y_t[t], in_=sy[:]), key=f"y{t % 2}")
            k.wait("vector", st_y[t % 2]) if False else None
        k.wait("sync", st_y[0])
        k.wait("sync", st_y[1])
        k.barrier()
        k.finalize()
    return nc



def build_l1attn(tok=TOK, seq=SEQ):
    H, HD, FEAT = 16, 64, 256
    nmt = seq // 128
    ntt = tok // 128
    VW = H * 65

    nc = bass.Bass("TRN2", target_bir_lowering=False, dynamic_dma_scratch_size=8192)
    x_own = nc.dram_tensor("x_own", [tok, D], F32, kind="ExternalInput").ap()
    x_full = nc.dram_tensor("x_full", [seq, D], F32, kind="ExternalInput").ap()
    wqkv = nc.dram_tensor("wqkv", [D, E3], F32, kind="ExternalInput").ap()
    projp = nc.dram_tensor("projp", [8, 128, FEAT], F32, kind="ExternalInput").ap()
    wout = nc.dram_tensor("wout", [D, D], F32, kind="ExternalInput").ap()
    bob = nc.dram_tensor("bob", [128, D], F32, kind="ExternalInput").ap()
    identb = nc.dram_tensor("identb", [128, 128], F32, kind="ExternalInput").ap()
    y_out = nc.dram_tensor("y", [tok, D], F32, kind="ExternalOutput").ap()

    xf_t = x_full.rearrange("(n p) m -> n p m", p=128)
    xo_t = x_own.rearrange("(n p) m -> n p m", p=128)
    y_t = y_out.rearrange("(n p) m -> n p m", p=128)

    with ExitStack() as ctx:
        e = ctx.enter_context
        k = K(nc, ctx)
        sKT = e(nc.sbuf_tensor("sKT", [128, 8 * seq], BF16))
        sV = e(nc.sbuf_tensor("sV", [128, nmt * VW], BF16))
        sQT = e(nc.sbuf_tensor("sQT", [128, 8 * tok], BF16))
        sxnTg = e(nc.sbuf_tensor("sxnTg", [128, 8 * 512], BF16))
        sx = [e(nc.sbuf_tensor(f"sx{i}", [128, D], F32)) for i in range(2)]
        sxnb = e(nc.sbuf_tensor("sxnb", [128, D], BF16))
        sst = e(nc.sbuf_tensor("sst", [128, 18], F32))
        sib = e(nc.sbuf_tensor("sib", [128, 128], BF16))
        sproj = e(nc.sbuf_tensor("sproj", [128, 8 * FEAT], BF16))
        skv = e(nc.sbuf_tensor("skv", [128, H * 130], BF16))
        skp = [e(nc.sbuf_tensor(f"skp{i}", [128, FEAT], BF16)) for i in range(2)]
        selu = sxnTg[:, 0:1024].bitcast(F32)
        selu2 = sxnTg[:, 1024:2048].bitcast(F32)
        szi = e(nc.sbuf_tensor("szi", [128, 1], F32))
        sy = e(nc.sbuf_tensor("sy", [128, D], F32))
        sbo = e(nc.sbuf_tensor("sbo", [128, D], F32))
        swl = e(nc.sbuf_tensor("swl", [128, 8 * 1024], BF16))
        sattn = None
        sqp = None
        ps_a = [e(nc.psum_tensor(f"ps_a{i}", [128, 512], F32)) for i in range(2)]
        ps_kv = [e(nc.psum_tensor(f"ps_kv{i}", [128, 512], F32)) for i in range(4)]
        ps_t = [e(nc.psum_tensor(f"ps_t{i}", [128, 1024], BF16)) for i in range(2)]

        lid = k.dma("gpsimd", lambda E: E.dma_start(out=sib[:], in_=identb), key="id")
        lpj = k.dma("gpsimd", lambda E: E.dma_start(
            out=sproj[:].rearrange("p (n m) -> p n m", m=FEAT), in_=projp), key="pj")

        xfree = [None, None]
        xldlast = [None, None]

        def ln_group(src_t, g):
            vtok = None
            for j in range(4):
                t = 4 * g + j
                k.wait("sync", xfree[j % 2])
                ldx = k.dma("sync", lambda E, t=t, j=j, src_t=src_t: E.dma_start(
                    out=sx[j % 2][:], in_=xf_t[t] if src_t is xf_t else xo_t[t]),
                    key=f"x{j % 2}")
                xldlast[j % 2] = ldx
                k.wait("vector", ldx)
                emit_ln(k, sx[j % 2][:], sxnb[:], {"st": sst})
                xfree[j % 2] = k.token("vector")
                vln = k.token("vector")
                k.wait("tensor", vln)
                if g == 0 and j == 0:
                    k.wait("tensor", lid)
                ttr = None
                for dt in range(8):
                    ttr = k.emit("tensor", lambda E, dt=dt, j=j: E.transpose(
                        ps_t[j % 2][:, dt * 128:(dt + 1) * 128],
                        sxnb[:, dt * 128:(dt + 1) * 128], sib[:]))
                k.wait("vector", ttr)
                dst = sxnTg[:, 0:4096].rearrange("p (n m) -> p n m", m=512)[:, :, j * 128:(j + 1) * 128]
                srcp = ps_t[j % 2][:].rearrange("p (n m) -> p n m", m=128)
                vtok = k.emit("vector", lambda E, dst=dst, srcp=srcp: E.tensor_copy(dst, srcp))
                if j < 3:
                    k.wait("tensor", vtok)
            return vtok

        swv_ov = sQT[:, 0:8192]
        lwk = k.dma("gpsimd", lambda E: E.dma_start(
            out=swl[:].rearrange("p (n m) -> p n m", m=1024),
            in_=wqkv[:, 1024:2048].rearrange("(n p) m -> p n m", p=128)), key="wk")
        lwv = k.dma("gpsimd", lambda E: E.dma_start(
            out=swv_ov.rearrange("p (n m) -> p n m", m=1024),
            in_=wqkv[:, 2048:3072].rearrange("(n p) m -> p n m", p=128)), key="wv")
        vone = k.emit("vector", lambda E: E.memset(
            sV[:].rearrange("p (n m) -> p n m", m=65)[:, :, 64:65], 1.0))
        cp_a1 = [None, None]
        for g in range(seq // 512):
            vg = ln_group(xf_t, g)
            k.wait("tensor", vg)
            k.wait("tensor", lwk)
            for et in range(8):
                k.wait("tensor", cp_a1[et % 2])
                tm = None
                for dt in range(8):
                    tm = k.emit("tensor", lambda E, et=et, dt=dt: E.matmul(
                        ps_a[et % 2][:, 0:512],
                        swl[:, dt * 1024 + et * 128: dt * 1024 + et * 128 + 128],
                        sxnTg[:, dt * 512: dt * 512 + 512],
                        start=(dt == 0), stop=(dt == 7)))
                k.wait("vector", tm)
                vev = k.emit("vector", lambda E, et=et, g=g: E.tensor_copy(
                    sKT[:, et * seq + g * 512: et * seq + g * 512 + 512], ps_a[et % 2][:, 0:512]))
                cp_a1[et % 2] = vev
            if g == 0:
                k.wait("tensor", lwv)
                k.wait("tensor", vone)
            for j in range(4):
                t = 4 * g + j
                for ch in range(2):
                    k.wait("tensor", cp_a1[ch])
                    tm = None
                    for dt in range(8):
                        tm = k.emit("tensor", lambda E, j=j, ch=ch, dt=dt: E.matmul(
                            ps_a[ch][:, 0:512],
                            sxnTg[:, dt * 512 + j * 128: dt * 512 + j * 128 + 128],
                            swv_ov[:, dt * 1024 + ch * 512: dt * 1024 + ch * 512 + 512],
                            start=(dt == 0), stop=(dt == 7)))
                    k.wait("vector", tm)
                    dstv = sV[:].rearrange("p (n m) -> p n m", m=65)[
                        :, t * 16 + ch * 8:t * 16 + (ch + 1) * 8, 0:64]
                    srcv = ps_a[ch][:, 0:512].rearrange("p (n m) -> p n m", m=64)
                    vev = k.emit("vector", lambda E, dstv=dstv, srcv=srcv: E.tensor_copy(dstv, srcv))
                    cp_a1[ch] = vev
            prev = k.token("vector")
        k.wait("gpsimd", lwk)
        k.wait("gpsimd", prev)
        lwq = k.dma("gpsimd", lambda E: E.dma_start(
            out=swl[:].rearrange("p (n m) -> p n m", m=1024),
            in_=wqkv[:, 0:1024].rearrange("(n p) m -> p n m", p=128)), key="wk")
        for g in range(tok // 512):
            vg = ln_group(xo_t, g)
            k.wait("tensor", vg)
            k.wait("tensor", lwq)
            for et in range(8):
                k.wait("tensor", cp_a1[et % 2])
                tm = None
                for dt in range(8):
                    tm = k.emit("tensor", lambda E, et=et, dt=dt: E.matmul(
                        ps_a[et % 2][:, 0:512],
                        swl[:, dt * 1024 + et * 128: dt * 1024 + et * 128 + 128],
                        sxnTg[:, dt * 512: dt * 512 + 512],
                        start=(dt == 0), stop=(dt == 7)))
                k.wait("vector", tm)
                vev = k.emit("vector", lambda E, et=et, g=g: E.tensor_copy(
                    sQT[:, et * tok + g * 512: et * tok + g * 512 + 512], ps_a[et % 2][:, 0:512]))
                cp_a1[et % 2] = vev

        k.wait("tensor", lpj)
        k.wait("tensor", cp_a1[0])
        k.wait("tensor", cp_a1[1])

        def emit_elu1(pin, pout_bf, width):
            v1 = k.emit("vector", lambda E: E.tensor_scalar(selu[:, 0:width], pin[:, 0:width], 0.0, None, ALU.min))
            k.wait("scalar", v1)
            s1 = k.emit("scalar", lambda E: E.activation(selu2[:, 0:width], selu[:, 0:width], AF.Exp))
            k.wait("vector", s1)
            v2 = k.emit("vector", lambda E: E.tensor_scalar(selu[:, 0:width], pin[:, 0:width], 0.0, None, ALU.max))
            v3 = k.emit("vector", lambda E: E.tensor_add(pout_bf, selu[:, 0:width], selu2[:, 0:width]))
            return v3

        kp_free = [None, None]
        kv_done = []
        for hp in range(8):
            kvp = [ps_kv[0], ps_kv[1], ps_kv[2], ps_kv[3]]
            for nt in range(nmt):
                for hi in range(2):
                    h = 2 * hp + hi
                    base = 64 * (h % 2)
                    lhsK = sKT[base:base + 64, (h // 2) * seq + nt * 128:(h // 2) * seq + (nt + 1) * 128]
                    rhsP = sproj[base:base + 64, hp * FEAT:(hp + 1) * FEAT]
                    k.wait("tensor", kp_free[hi])
                    tkp = k.emit("tensor", lambda E, lhsK=lhsK, rhsP=rhsP, hi=hi: E.matmul(
                        ps_a[hi][:, 0:FEAT], lhsK, rhsP, start=True, stop=True))
                    k.wait("vector", tkp)
                    velu = emit_elu1(ps_a[hi], skp[hi][:, 0:FEAT], FEAT)
                    kp_free[hi] = velu
                    k.wait("tensor", velu)
                    for ft in range(2):
                        tkv = k.emit("tensor", lambda E, hi=hi, ft=ft, nt=nt, h=h: E.matmul(
                            kvp[2 * hi + ft][:, 0:65],
                            skp[hi][:, ft * 128:(ft + 1) * 128],
                            sV[:, (nt * 16 + h) * 65:(nt * 16 + h) * 65 + 65],
                            start=(nt == 0), stop=(nt == nmt - 1)))
                        if nt == nmt - 1:
                            k.wait("vector", tkv)
            for hi in range(2):
                h = 2 * hp + hi
                for ft in range(2):
                    k.emit("vector", lambda E, hi=hi, ft=ft, h=h: E.tensor_copy(
                        skv[:, h * 130 + ft * 65: h * 130 + ft * 65 + 65],
                        kvp[2 * hi + ft][:, 0:65]))
            kv_done.append(k.token("vector"))
        kvd = k.token("vector")

        sattn = sV[:, 0:8192]
        sattn = sV[:, 0:ntt * 1024]
        sqp = sV[:, ntt * 1024: ntt * 1024 + 2 * tok]
        qp_free = None
        at_free = [None, None]
        at_done = []
        for hp in range(8):
            for hi in range(2):
                h = 2 * hp + hi
                base = 64 * (h % 2)
                for fc in range(2):
                    for qc in range(tok // 512):
                        rhsQ = sQT[base:base + 64, (h // 2) * tok + qc * 512:(h // 2) * tok + (qc + 1) * 512]
                        lhsP = sproj[base:base + 64, hp * FEAT + fc * 128: hp * FEAT + (fc + 1) * 128]
                        k.wait("tensor", qp_free)
                        tqp = k.emit("tensor", lambda E, lhsP=lhsP, rhsQ=rhsQ, fc=fc: E.matmul(
                            ps_a[fc][:, 0:512], lhsP, rhsQ, start=True, stop=True))
                        k.wait("vector", tqp)
                        dq = sqp.rearrange("p (n m) -> p n m", m=tok)[:, fc:fc + 1, qc * 512:(qc + 1) * 512].squeeze(1)
                        qp_free = emit_elu1(ps_a[fc], dq, 512)
                k.wait("tensor", qp_free)
                if hp == 0 and hi == 0:
                    k.wait("tensor", kvd)
                for tt in range(ntt):
                    k.wait("tensor", at_free[tt % 2])
                    tm = None
                    for ft in range(2):
                        tm = k.emit("tensor", lambda E, ft=ft, tt=tt, h=h: E.matmul(
                            ps_kv[2 + (tt % 2)][:, 0:65],
                            sqp[:, ft * tok + tt * 128: ft * tok + (tt + 1) * 128],
                            skv[:, h * 130 + ft * 65: h * 130 + ft * 65 + 65],
                            start=(ft == 0), stop=(ft == 1)))
                    k.wait("vector", tm)
                    pk = ps_kv[2 + (tt % 2)]
                    k.emit("vector", lambda E, pk=pk: E.tensor_scalar(szi[:], pk[:, 64:65], 1e-8, None, ALU.add))
                    k.emit("vector", lambda E: E.reciprocal(szi[:], szi[:]))
                    va = k.emit("vector", lambda E, pk=pk, tt=tt, h=h: E.tensor_scalar(
                        sattn[:, tt * 1024 + h * 64: tt * 1024 + (h + 1) * 64],
                        pk[:, 0:64], szi[:, 0:1], None, ALU.mult))
                    at_free[tt % 2] = va
                    at_done.append(va)
        atd = k.token("vector")

        sAT = sKT[:, 0:8 * tok]
        if 8 * tok + 8192 <= 8 * seq:
            sw4 = sKT[:, 8 * tok: 8 * tok + 8192]
        else:
            sw4 = swl[:, 0:8192]
        p3t = k.token("tensor")
        p3v = k.token("vector")
        p3s = k.token("scalar")
        k.wait("gpsimd", p3t)
        k.wait("gpsimd", p3v)
        lw4 = k.dma("gpsimd", lambda E: E.dma_start(
            out=sw4.rearrange("p (n m) -> p n m", m=1024),
            in_=wout.rearrange("(n p) m -> p n m", p=128)), key="wk")
        k.wait("sync", p3t)
        k.wait("sync", p3v)
        lbo = k.dma("sync", lambda E: E.dma_start(out=sbo[:], in_=bob), key="bo")
        k.wait("tensor", atd)
        vT = [None] * ntt
        for tt in range(ntt):
            ttr = None
            for dt in range(8):
                ttr = k.emit("tensor", lambda E, tt=tt, dt=dt: E.transpose(
                    ps_t[tt % 2][:, dt * 128:(dt + 1) * 128],
                    sattn[:, tt * 1024 + dt * 128: tt * 1024 + (dt + 1) * 128], sib[:]))
            k.wait("vector", ttr)
            dst = sAT.rearrange("p (n m) -> p n m", m=tok)[:, :, tt * 128:(tt + 1) * 128]
            srcp = ps_t[tt % 2][:].rearrange("p (n m) -> p n m", m=128)
            vT[tt] = k.emit("vector", lambda E, dst=dst, srcp=srcp: E.tensor_copy(dst, srcp))
            if tt + 1 < ntt:
                k.wait("tensor", vT[tt])
        st_y = [None, None]
        pa_free = [None, None]
        for t in range(ntt):
            if t < 2:
                k.wait("sync", xldlast[t % 2])
                k.wait("sync", k.token("vector"))
            else:
                k.wait("sync", st_y[t % 2])
            ldx = k.dma("sync", lambda E, t=t: E.dma_start(out=sx[t % 2][:], in_=xo_t[t]), key=f"x{t % 2}")
            if t == 0:
                k.wait("tensor", lw4)
                k.wait("tensor", vT[ntt - 1])
            for half in range(2):
                k.wait("tensor", pa_free[half])
                tm = None
                for dt in range(8):
                    tm = k.emit("tensor", lambda E, dt=dt, half=half, t=t: E.matmul(
                        ps_a[half][:, 0:512],
                        sAT[:, dt * tok + t * 128: dt * tok + (t + 1) * 128],
                        sw4[:, dt * 1024 + half * 512: dt * 1024 + half * 512 + 512],
                        start=(dt == 0), stop=(dt == 7)))
                k.wait("vector", tm)
                k.wait("vector", ldx)
                if half == 0:
                    k.wait("vector", lbo)
                if t >= 1:
                    k.wait("vector", st_y[(t - 1) % 2])
                k.emit("vector", lambda E, half=half, t=t: E.tensor_add(
                    sy[:, half * 512:(half + 1) * 512], ps_a[half][:, 0:512],
                    sx[t % 2][:, half * 512:(half + 1) * 512]))
                pa_free[half] = k.emit("vector", lambda E, half=half: E.tensor_add(
                    sy[:, half * 512:(half + 1) * 512], sy[:, half * 512:(half + 1) * 512],
                    sbo[:, half * 512:(half + 1) * 512]))
            vy = k.token("vector")
            k.wait("sync", vy)
            st_y[t % 2] = k.dma("sync", lambda E, t=t: E.dma_start(out=y_t[t], in_=sy[:]), key=f"y{t % 2}")
        k.wait("sync", st_y[0])
        k.wait("sync", st_y[1])
        k.barrier()
        k.finalize()
    return nc



_DBG = None


def kernel(**inp):
    X = np.ascontiguousarray(inp["L0_attn_x"], dtype=np.float32).reshape(16384, D)
    ident = np.eye(128, dtype=np.float32)

    def tile_row(v):
        return np.ascontiguousarray(np.broadcast_to(np.asarray(v, np.float32), (128, D)))

    nc0 = build_prep()
    wg = np.ascontiguousarray(inp["L0_ffn_w_gate"], dtype=np.float32)
    wu = np.ascontiguousarray(inp["L0_ffn_w_up"], dtype=np.float32)
    wd = np.ascontiguousarray(inp["L0_ffn_w_down"], dtype=np.float32)
    cenT = np.ascontiguousarray(np.asarray(inp["L0_attn_spatial_centers"], np.float32).T)
    in_maps = []
    for c in range(NCORES):
        in_maps.append({
            "wg": np.ascontiguousarray(wg[:, c * 128:(c + 1) * 128]),
            "wu": np.ascontiguousarray(wu[:, c * 128:(c + 1) * 128]),
            "wd": np.ascontiguousarray(wd[c * 128:(c + 1) * 128, :]),
            "cenT": cenT,
            "identb": ident,
        })
    res = run_bass_kernel_spmd(nc0, in_maps, core_ids=list(range(NCORES)))
    wgqT = np.concatenate([res.results[c]["o_wgqT"] for c in range(NCORES)], axis=0)
    wuqT = np.concatenate([res.results[c]["o_wuqT"] for c in range(NCORES)], axis=0)
    wdqT = np.concatenate([res.results[c]["o_wdqT"].reshape(I_FFN, 128) for c in range(NCORES)], axis=1)
    Bmat = res.results[0]["o_B"].reshape(1024, 1024)

    Bpad = np.zeros((1152, SEQ), np.float32)
    Bpad[1:1025, 1:1025] = Bmat[:, 0:1024]

    nc_ffn0 = build_l0ffn()
    nc_attn0 = build_l0attn()
    nc_ffn1 = build_l1ffn()
    nc_attn1 = build_l1attn()

    def run_l0ffn(Xc):
        im = [{"x": np.ascontiguousarray(Xc[c * TOK:(c + 1) * TOK]),
               "wgqT": wgqT, "wuqT": wuqT, "wdqT": np.ascontiguousarray(wdqT),
               "identb": ident} for c in range(NCORES)]
        r = run_bass_kernel_spmd(nc_ffn0, im, core_ids=list(range(NCORES)))
        return np.concatenate([r.results[c]["y"] for c in range(NCORES)], axis=0)

    wqkv0T = np.ascontiguousarray(np.asarray(inp["L0_attn_W_qkv"], np.float32).T)
    wproj0T = np.ascontiguousarray(np.asarray(inp["L0_attn_W_proj"], np.float32).T)
    gb = tile_row(inp["L0_attn_ln_gamma"])
    bbb = tile_row(inp["L0_attn_ln_beta"])
    bpb = tile_row(inp["L0_attn_b_proj"])

    def run_l0attn(Xc):
        im = []
        for c in range(NCORES):
            b, hq = c // 2, c % 2
            Bq = np.zeros((4, 9, 128, 512), np.float32)
            for qc in range(4):
                for mt in range(9):
                    Bq[qc, mt] = Bpad[mt * 128:(mt + 1) * 128,
                                      hq * 2048 + qc * 512: hq * 2048 + (qc + 1) * 512]
            im.append({
                "x_own": np.ascontiguousarray(Xc[c * TOK:(c + 1) * TOK]),
                "x_full": np.ascontiguousarray(Xc[b * SEQ:(b + 1) * SEQ]),
                "wqkvT": wqkv0T, "wprojT": wproj0T,
                "gb": gb, "bbb": bbb, "bpb": bpb, "Bq": Bq, "identb": ident,
            })
        r = run_bass_kernel_spmd(nc_attn0, im, core_ids=list(range(NCORES)))
        return np.concatenate([r.results[c]["y"] for c in range(NCORES)], axis=0)

    wrT = np.ascontiguousarray(np.asarray(inp["L1_ffn_W_r"], np.float32).T)
    wkT = np.ascontiguousarray(np.asarray(inp["L1_ffn_W_k"], np.float32).T)
    wvT = np.ascontiguousarray(np.asarray(inp["L1_ffn_W_v"], np.float32).T)
    murT = np.ascontiguousarray(np.asarray(inp["L1_ffn_mu_r"], np.float32).reshape(8, 128).T)
    mukT = np.ascontiguousarray(np.asarray(inp["L1_ffn_mu_k"], np.float32).reshape(8, 128).T)

    def run_l1ffn(Xc):
        im = []
        for c in range(NCORES):
            xb = np.zeros((1, D), np.float32) if c % 2 == 0 else \
                np.ascontiguousarray(Xc[c * TOK - 1: c * TOK])
            im.append({"x": np.ascontiguousarray(Xc[c * TOK:(c + 1) * TOK]), "xb": xb,
                       "wrT": wrT, "wkT": wkT, "wvT": wvT,
                       "mur": murT, "muk": mukT, "identb": ident})
        r = run_bass_kernel_spmd(nc_ffn1, im, core_ids=list(range(NCORES)))
        return np.concatenate([r.results[c]["y"] for c in range(NCORES)], axis=0)

    wqkv1 = np.ascontiguousarray(inp["L1_attn_w_qkv"], dtype=np.float32)
    proj1 = np.asarray(inp["L1_attn_proj_matrix"], np.float32)
    projp = np.zeros((8, 128, 256), np.float32)
    for h in range(16):
        projp[h // 2, 64 * (h % 2):64 * (h % 2) + 64, :] = proj1[h]
    wout1 = np.ascontiguousarray(inp["L1_attn_w_out"], dtype=np.float32)
    bob = tile_row(inp["L1_attn_b_out"])

    def run_l1attn(Xc):
        im = []
        for c in range(NCORES):
            b = c // 2
            im.append({"x_own": np.ascontiguousarray(Xc[c * TOK:(c + 1) * TOK]),
                       "x_full": np.ascontiguousarray(Xc[b * SEQ:(b + 1) * SEQ]),
                       "wqkv": wqkv1, "projp": projp, "wout": wout1,
                       "bob": bob, "identb": ident})
        r = run_bass_kernel_spmd(nc_attn1, im, core_ids=list(range(NCORES)))
        return np.concatenate([r.results[c]["y"] for c in range(NCORES)], axis=0)

    X1 = run_l0ffn(X)
    X2 = run_l0attn(X1)
    X3 = run_l0ffn(X2)
    X4 = run_l1ffn(X3)
    X5 = run_l1attn(X4)
    X6 = run_l1ffn(X5)
    if _DBG is not None:
        _DBG.extend([X1, X2, X3, X4, X5, X6])
    return X6.reshape(4, SEQ, D)


if __name__ == "__main__":
    pass
